# version 12
# baseline (speedup 1.0000x reference)
import numpy as np
import ml_dtypes
import concourse.bass as bass
import concourse.mybir as mybir
from concourse.bass_utils import run_bass_kernel_spmd
from contextlib import ExitStack

F32 = mybir.dt.float32
BF16 = mybir.dt.bfloat16
AF = mybir.ActivationFunctionType
ALU = mybir.AluOpType
AX = mybir.AxisListType
BF = ml_dtypes.bfloat16

B, T, D = 2, 8192, 1024
NH, HD, RD = 16, 64, 32
KVR, QR = 256, 512
FFI, NE = 2048, 8
NJ = 16
TPC = NJ * 128
SCALE = float(96.0 ** -0.5)
MARGIN = 6.0
EPS_LN, EPS_RMS = 1e-5, 1e-6
NKC = 32
NQC = 8
RING = 7

ENGINES = ("sync", "gpsimd", "vector", "scalar", "tensor")


def _Q(r, j):
    return 63 - 4 * j - r


def _E(j):
    return 64 - 4 * j


def _rope_tables():
    inv = 1.0 / 10000.0 ** (np.arange(0, RD, 2, dtype=np.float32) / RD)
    t = np.arange(T, dtype=np.float32)
    fr = np.outer(t, inv)
    emb = np.concatenate([fr, fr], -1)
    cos = np.cos(emb).astype(np.float32)
    sin = np.sin(emb).astype(np.float32)
    dd = np.arange(NH * RD) % RD
    sign = np.where(dd < RD // 2, -1.0, 1.0).astype(np.float32)
    cosT = np.ascontiguousarray(cos[:, dd].T)
    ssinT = np.ascontiguousarray((sin[:, dd] * sign[None, :]).T)
    return cosT, ssinT


def _swap_perm(ncols):
    dd = np.arange(ncols)
    j = dd % RD
    return (dd // RD) * RD + (j + RD // 2) % RD


def _host_prep(inp):
    cosT, ssinT = _rope_tables()
    f32 = np.float32
    asc = np.ascontiguousarray
    W_kr = asc(inp["L0_attn_W_krope"], f32)
    W_qr = asc(inp["L0_attn_W_qrope"], f32)
    shared = dict(
        w_kv=asc(inp["L0_attn_W_kv"], f32),
        w_kdec=asc(inp["L0_attn_W_kdec"], f32),
        w_vdec=asc(inp["L0_attn_W_vdec"], f32),
        w_q=asc(inp["L0_attn_W_q"], f32),
        w_qdec=asc(inp["L0_attn_W_qdec"], f32),
        w_kr=W_kr, w_kr_sw=asc(W_kr[:, _swap_perm(NH * RD)]),
        w_qr=W_qr, w_qr_sw=asc(W_qr[:, _swap_perm(NH * RD)]),
        w_o=asc(inp["L0_attn_W_o"], f32),
        g_kv=asc(inp["L0_attn_g_kv"], f32).reshape(2, 128).T.copy(),
        kcos=cosT, kssin=ssinT,
        wg_s=asc(inp["L0_ffn_Wg_s"].T, f32),
        wu_s=asc(inp["L0_ffn_Wu_s"].T, f32),
        wd_s=asc(inp["L0_ffn_Wd_s"].T, f32),
        wg=asc(np.transpose(inp["L0_ffn_Wg"], (0, 2, 1)), f32),
        wu=asc(np.transpose(inp["L0_ffn_Wu"], (0, 2, 1)), f32),
        wd=asc(np.transpose(inp["L0_ffn_Wd"], (0, 2, 1)), f32),
        wr=asc(inp["L0_ffn_Wr"].T, f32),
        rb=asc(inp["L0_ffn_rb"], f32).reshape(1, NE),
        eye_bf=np.eye(128, dtype=BF),
        eye_f32=np.eye(128, dtype=f32),
        ones_row_f=np.ones((1, 128), f32),
        ones_col_f=np.ones((128, 1), f32),
        ones_8k=np.ones((1, T), BF),
    )
    x = np.asarray(inp["L0_attn_x"], f32)
    in_maps = []
    for core in range(8):
        b, r = core // 4, core % 4
        qt = [_Q(r, j) for j in range(NJ)]
        x_q = asc(np.concatenate([x[b, q * 128:(q + 1) * 128] for q in qt], 0))
        qpos = np.concatenate([np.arange(q * 128, (q + 1) * 128) for q in qt])
        mask = np.zeros((NJ // 2, 8, 128, 256), np.float32)
        for pj in range(NJ // 2):
            E0 = _E(2 * pj)
            for pp in range(8):
                k0 = (E0 - 8 + pp) * 128
                ki = k0 + np.arange(128)[:, None]
                for jj in range(2):
                    q0 = qt[2 * pj + jj] * 128
                    qi = q0 + np.arange(128)[None, :]
                    mask[pj, pp, :, jj * 128:(jj + 1) * 128] = np.where(
                        ki <= qi, 0.0, -30000.0)
        m = dict(shared)
        m.update(
            x_b=x[b], x_q=x_q,
            qcos=asc(cosT[:, qpos] * SCALE),
            qssin=asc(ssinT[:, qpos] * SCALE),
            maskT=mask.astype(BF),
        )
        in_maps.append(m)
    return in_maps


class P:
    def __init__(self, nc, ctx):
        self.nc, self.ctx = nc, ctx
        self.ev = {e: [] for e in ENGINES}
        self.cnt = {}
        self.sems = []

    def sem(self, name):
        s = self.ctx.enter_context(self.nc.semaphore(name))
        self.cnt[s] = 0
        self.sems.append(s)
        return s

    def add(self, eng, fn, waits=(), inc=None, n=1):
        w = [t for t in waits if t is not None]
        tok = None
        if inc is not None:
            self.cnt[inc] += n
            tok = (inc, self.cnt[inc])
        self.ev[eng].append((fn, w, inc, n))
        return tok

    def op(self, eng, meth, *args, waits=(), inc=None, n=1, **kw):
        def fn(e, m=meth, a=args, k=kw):
            return getattr(e, m)(*a, **k)
        return self.add(eng, fn, waits, inc, n)

    def dma(self, eng, out, in_, waits=(), inc=None):
        def fn(e, o=out, i=in_):
            return e.dma_start(out=o, in_=i)
        return self.add(eng, fn, waits, inc, 16)

    def now(self, sem):
        return (sem, self.cnt[sem])

    def barrier(self):
        toks = [self.now(s) for s in self.sems if s is not self._bar]
        for e in ENGINES:
            self.op(e, "nop", waits=toks, inc=self._bar, n=1)
        full = self.now(self._bar)
        for e in ENGINES:
            self.op(e, "nop", waits=[full])

    def emit(self):
        with self.nc.Block() as block:
            for eng in ENGINES:
                def body(e, name=eng):
                    seen = {}
                    for fn, waits, inc, n in self.ev[name]:
                        for (s, v) in waits:
                            if seen.get(s, -1) < v:
                                e.wait_ge(s, v)
                                seen[s] = v
                        inst = fn(e)
                        if inc is not None:
                            inst.then_inc(inc, n)
                getattr(block, eng)(body)


def _build(dev=99):
    nc = bass.Bass("TRN2", target_bir_lowering=False)
    d = {}

    def din(name, shape, dt=F32):
        d[name] = nc.dram_tensor(name, list(shape), dt, kind="ExternalInput")
        return d[name]

    def dint(name, shape, dt):
        kind = "ExternalOutput" if dev < 99 else "Internal"
        d[name] = nc.dram_tensor(name, list(shape), dt, kind=kind)
        return d[name]

    din("x_b", (T, D)); din("x_q", (TPC, D))
    din("w_kv", (D, KVR)); din("w_kdec", (KVR, D)); din("w_vdec", (KVR, D))
    din("w_q", (D, QR)); din("w_qdec", (QR, D))
    din("w_kr", (D, NH * RD)); din("w_kr_sw", (D, NH * RD))
    din("w_qr", (QR, NH * RD)); din("w_qr_sw", (QR, NH * RD))
    din("w_o", (D, D)); din("g_kv", (128, 2))
    din("kcos", (NH * RD, T)); din("kssin", (NH * RD, T))
    din("qcos", (NH * RD, TPC)); din("qssin", (NH * RD, TPC))
    din("maskT", (NJ // 2, 8, 128, 256), BF16)
    din("wg_s", (D, FFI)); din("wu_s", (D, FFI)); din("wd_s", (FFI, D))
    din("wg", (NE, D, FFI)); din("wu", (NE, D, FFI)); din("wd", (NE, FFI, D))
    din("wr", (D, NE)); din("rb", (1, NE))
    din("eye_bf", (128, 128), BF16); din("eye_f32", (128, 128))
    din("ones_row_f", (1, 128)); din("ones_col_f", (128, 1))
    din("ones_8k", (1, T), BF16)

    out_d = nc.dram_tensor("out", [NJ, 128, D], F32, kind="ExternalOutput")

    kro_d = dint("kro_s", (NH * RD, T), BF16)
    kall_d = dint("k_all", (NH, 128, T), BF16)
    vall_d = dint("v_all", (NH // 2, 128, 64 * 136), BF16)
    qall_d = dint("qt_all", (NH, 128, TPC), BF16)
    ys_d = dint("y_s", (TPC, D), F32)
    cwT_d = dint("cwT_s", (128, 128), F32)
    mrow_d = dint("mrow_s", (1, 2048), BF16)

    S = {}

    SBASE = 16512

    def sb(name, shape, dt, off):
        S[name] = nc.alloc_sbuf_tensor_at(name, list(shape), dt,
                                          offset=SBASE + off)
        return S[name]

    sb("c_eye_bf", (128, 128), BF16, 0)
    sb("c_eye_f", (128, 128), F32, 256)
    sb("c_ones_row", (1, 128), F32, 768)
    sb("c_ones_col", (128, 1), F32, 1280)
    sb("c_g", (128, 2), F32, 1312)
    sb("wrt", (128, 64), BF16, 1344)
    sb("cwbuf", (128, 128), F32, 2048)
    sb("cw", (128, 128), F32, 2560)
    sb("cwT", (128, 128), F32, 3072)
    sb("r_m1", (128, 16), F32, 3584)
    sb("r_m2", (128, 16), F32, 3648)
    sb("r_ge", (128, 16), F32, 3712)
    sb("rbB", (128, 8), F32, 3776)
    sb("kv_normT", (128, 2 * T), BF16, 4096)
    sb("q_cT", (128, 4 * TPC), BF16, 36864)
    sb("q_ropeT", (128, 4 * TPC), BF16, 53248)
    sb("attn_o64", (128, NH * TPC), BF16, 4096)
    sb("out_acc", (128, 8 * 1024), F32, 4096)
    sb("cwB", (128, 1024), F32, 36864)
    sb("ytile0", (128, 1024), F32, 40960)
    sb("ytile1", (128, 1024), F32, 45056)
    sb("sil", (128, 1024), BF16, 49152)
    sb("an", (128, 1024), F32, 51200)
    sb("cwrow", (1, 1024), F32, 55296)
    M0 = 69632
    o = M0
    for nm, cols, dt, npart in [
        ("wkv", 8 * KVR, BF16, 128), ("wkr", 8 * 512, BF16, 128),
        ("wkr_sw", 8 * 512, BF16, 128), ("wq", 8 * 512, BF16, 128),
        ("wqr", 4 * 512, BF16, 128), ("wqr_sw", 4 * 512, BF16, 128),
        ("xbuf0", 2048, F32, 128), ("xbuf1", 2048, F32, 128),
        ("xsq", 1024, F32, 128), ("xln", 2048, BF16, 128),
        ("xlnT0", 2048, BF16, 128), ("xlnT1", 2048, BF16, 128),
        ("tabc0", 1024, F32, 128), ("tabs0", 1024, F32, 128),
        ("tabc1", 1024, F32, 128), ("tabs1", 1024, F32, 128),
        ("kro0", 1024, BF16, 128), ("kro1", 1024, BF16, 128),
        ("ropet1", 1024, F32, 128), ("sqv", 512, F32, 128),
        ("rkv", 512, F32, 1), ("rkvB", 512, F32, 128), ("lnst", 32, F32, 128),
    ]:
        sz = cols * (2 if dt == BF16 else 4)
        sb(nm, (npart, cols), dt, o)
        o += (sz + 31) & ~31
    assert o <= 212800, ("A", o)
    sb("wkdec", (128, 2 * D), BF16, 196352)
    sb("wvdec", (128, 2 * D), BF16, 200448)
    sb("wqdec", (128, 4 * D), BF16, 204544)
    assert 204544 + 4 * D * 2 <= 212800
    o = M0
    for nm, cols in [
        ("KTb0", T), ("KTb1", T),
        ("Vb0", 64 * 136 + 32), ("Vb1", 64 * 136 + 32),
        ("QTb0", TPC), ("QTb1", TPC),
    ]:
        sb(nm, (128, cols), BF16, o)
        o += (cols * 2 + 31) & ~31
    assert o <= 212800, ("A2", o)
    o = M0
    for nm, cols, dt in [
        ("ktB0", T, BF16), ("ktB1", T, BF16),
        ("vB0", 64 * 136 + 32, BF16), ("vB1", 64 * 136 + 32, BF16),
        ("qtB0", TPC, BF16), ("qtB1", TPC, BF16),
        ("probsT0", 1024, BF16), ("probsT1", 1024, BF16),
        ("maskb0", 2048, BF16), ("maskb1", 2048, BF16),
        ("ones64", 64, F32), ("rl64", 256, F32), ("rlB", 256, F32),
        ("p1tmp", 8, F32), ("mcols", 16, F32), ("mneg", 16, BF16),
    ]:
        sz = cols * (2 if dt == BF16 else 4)
        sb(nm, (128, cols), dt, o)
        o += (sz + 31) & ~31
    sb("mrow", (16, 128), BF16, o); o += 256
    assert o <= 163584, ("B", o)
    sb("masktmp", (128, 1024), F32, 163584)
    o = M0
    for nm, cols, dt in [
        ("xq0", 1024, F32), ("xq1", 1024, F32),
        ("yb0", 1024, F32), ("yb1", 1024, F32),
        ("ylnb", 1024, F32), ("ylnb2", 1024, BF16),
        ("lnst2", 32, F32), ("lgt", 8, F32),
    ]:
        sz = cols * (2 if dt == BF16 else 4)
        sb(nm, (128, cols), dt, o)
        o += (sz + 31) & ~31
    assert o <= 163584, ("C", o)
    sb("y_lnT", (128, 8 * TPC), BF16, 163584)
    sb("wo", (64, 16 * 1024), BF16, 106496)
    sb("actT", (128, 16 * 1024), BF16, M0)
    for i in range(RING):
        sb(f"ring{i}", (128, 4096), BF16, M0 + 32768 + i * 8192)
    assert M0 + 32768 + RING * 8192 <= 163584

    ctx = ExitStack()
    p = P(nc, ctx)
    sv = p.sem("sv"); ss = p.sem("ss"); st = p.sem("st")
    sp = p.sem("sp")
    dC = p.sem("dC")
    dW = p.sem("dW")
    dX = [p.sem(f"dX{i}") for i in range(2)]
    dT = [p.sem(f"dT{i}") for i in range(2)]
    dKr = [p.sem(f"dKr{i}") for i in range(2)]
    dRp = [p.sem(f"dRp{i}") for i in range(2)]
    dQr = [p.sem(f"dQr{i}") for i in range(2)]
    dSt = [p.sem(f"dSt{i}") for i in range(2)]
    dK = [p.sem(f"dK{i}") for i in range(2)]
    dQ = [p.sem(f"dQ{i}") for i in range(2)]
    dV = [p.sem(f"dV{i}") for i in range(2)]
    dM = [p.sem(f"dM{i}") for i in range(2)]
    dMr = p.sem("dMr")
    dXQ = [p.sem(f"dXQ{i}") for i in range(2)]
    dYs = [p.sem(f"dYs{i}") for i in range(2)]
    dR = [p.sem(f"dR{i}") for i in range(RING)]
    dCw = p.sem("dCw")
    dYt = [p.sem(f"dYt{i}") for i in range(2)]
    dOut = [p.sem(f"dOut{i}") for i in range(2)]
    p._bar = p.sem("bar")
    A = S

    def finish():
        for e in ENGINES:
            p.op(e, "nop", waits=[p.now(s) for s in p.sems
                                  if s is not p._bar])
        p.emit()
        ctx.close()
        return nc

    p.dma("sync", A["c_eye_bf"][:, :], d["eye_bf"][:, :], inc=dC)
    p.dma("sync", A["c_eye_f"][:, :], d["eye_f32"][:, :], inc=dC)
    p.dma("sync", A["c_ones_row"][:, :], d["ones_row_f"][:, :], inc=dC)
    p.dma("sync", A["c_ones_col"][:, :], d["ones_col_f"][:, :], inc=dC)
    p.dma("sync", A["c_g"][:, :], d["g_kv"][:, :], inc=dC)
    consts_tok = p.now(dC)

    def wload(dst, src, inc=dW):
        kt = src.shape[0] // 128
        sr = src.rearrange("(k p) m -> p k m", p=128)
        ds = dst[:, :].rearrange("p (k m) -> p k m", k=kt)
        return p.dma("gpsimd", ds, sr, inc=inc)

    wload(A["wkv"], d["w_kv"])
    wload(A["wkr"], d["w_kr"])
    wload(A["wkr_sw"], d["w_kr_sw"])
    wload(A["wq"], d["w_q"])
    wload(A["wqr"], d["w_qr"])
    wload(A["wqr_sw"], d["w_qr_sw"])
    wA_tok = p.now(dW)
    wload(A["wkdec"], d["w_kdec"])
    wload(A["wvdec"], d["w_vdec"])
    wload(A["wqdec"], d["w_qdec"])
    wA2_tok = p.now(dW)

    with ExitStack() as pctx:
        psT = [pctx.enter_context(nc.psum_tensor(f"psT{i}", [128, 1024], BF16))
               for i in range(2)]
        psK = [pctx.enter_context(nc.psum_tensor(f"psK{i}", [128, 512], F32))
               for i in range(2)]
        psR = [pctx.enter_context(nc.psum_tensor(f"psR{i}", [128, 512], F32))
               for i in range(4)]

        xfree = [None, None]
        xTfree = [None, None]
        tabfree = [None, None]
        krofree = [None, None]
        ropet1_free = [None, None]
        psT_rd = [None, None]
        psK_rd = [None, None]
        psR_rd = [None, None, None, None]
        xln_free = None
        xsq_free = None
        setfree = [None] * 4

        def sweep(nchunks, x_src, cos_src, sin_src, qside):
            nonlocal xln_free, xsq_free
            for c in range(nchunks):
                par = c % 2
                xb = A[f"xbuf{par}"]
                xs = x_src[c * 256:(c + 1) * 256].rearrange(
                    "(t p) d -> p t d", p=128)
                tx = p.dma("sync",
                           xb[:, :].rearrange("p (t d) -> p t d", t=2),
                           xs, waits=[xfree[par]], inc=dX[par])
                tc = p.dma("sync",
                           A[f"tabc{par}"][:, :].rearrange(
                               "p (k t) -> p k t", k=4),
                           cos_src[:, c * 256:(c + 1) * 256].rearrange(
                               "(k p) t -> p k t", p=128),
                           waits=[tabfree[par]], inc=dT[par])
                tsn = p.dma("sync",
                            A[f"tabs{par}"][:, :].rearrange(
                                "p (k t) -> p k t", k=4),
                            sin_src[:, c * 256:(c + 1) * 256].rearrange(
                                "(k p) t -> p k t", p=128),
                            waits=[tabfree[par]], inc=dT[par])
                ln_toks = []
                for t in range(2):
                    lb = ((c * 2 + t) % 4) * 8
                    xt = xb[:, t * 1024:(t + 1) * 1024]
                    t_r0 = p.op("vector", "reduce_sum", A["lnst"][:, lb + 0:lb + 1],
                                xt, AX.X, waits=[tx], inc=sv)
                    tq = p.op("scalar", "activation", A["xsq"][:, :], xt,
                              AF.Square, waits=[tx, xsq_free], inc=ss)
                    xsq_free = p.op("vector", "reduce_sum", A["lnst"][:, lb + 1:lb + 2],
                                    A["xsq"][:, :], AX.X, waits=[tq], inc=sv)
                    t_ms = p.op("gpsimd", "tensor_scalar", A["lnst"][:, lb + 2:lb + 4],
                                A["lnst"][:, lb + 0:lb + 2], 1.0 / D, None, ALU.mult,
                                waits=[xsq_free, setfree[(c * 2 + t) % 4]],
                                inc=sp)
                    p.op("gpsimd", "tensor_tensor", A["lnst"][:, lb + 4:lb + 5],
                         A["lnst"][:, lb + 2:lb + 3], A["lnst"][:, lb + 2:lb + 3], ALU.mult)
                    t_ve = p.op("gpsimd", "tensor_scalar", A["lnst"][:, lb + 5:lb + 6],
                                A["lnst"][:, lb + 3:lb + 4], A["lnst"][:, lb + 4:lb + 5], EPS_LN,
                                ALU.subtract, ALU.add, inc=sp)
                    t_rs = p.op("scalar", "activation", A["lnst"][:, lb + 6:lb + 7],
                                A["lnst"][:, lb + 5:lb + 6], AF.Sqrt,
                                waits=[t_ve], inc=ss)
                    t_rc = p.op("vector", "reciprocal", A["lnst"][:, lb + 7:lb + 8],
                                A["lnst"][:, lb + 6:lb + 7], waits=[t_rs], inc=sv)
                    t_cp = p.op("gpsimd", "tensor_copy", A["lnst"][:, lb + 5:lb + 6],
                                A["lnst"][:, lb + 7:lb + 8], waits=[t_rc], inc=sp)
                    tok = p.op("vector", "tensor_scalar",
                               A["xln"][:, t * 1024:(t + 1) * 1024], xt,
                               A["lnst"][:, lb + 2:lb + 3], A["lnst"][:, lb + 5:lb + 6],
                               ALU.subtract, ALU.mult,
                               waits=[xln_free, t_cp, t_ms], inc=sv)
                    setfree[(c * 2 + t) % 4] = tok
                    ln_toks.append(tok)
                xfree[par] = ln_toks[-1]
                xT = A[f"xlnT{par}"]
                last_tp = None
                for t in range(2):
                    for dkk in range(8):
                        tt = p.op("tensor", "transpose",
                                  psT[t][:, dkk * 128:dkk * 128 + 128],
                                  A["xln"][:, t * 1024 + dkk * 128:
                                           t * 1024 + dkk * 128 + 128],
                                  A["c_eye_bf"][:, :],
                                  waits=[ln_toks[t], consts_tok, psT_rd[t],
                                         xTfree[par]],
                                  inc=(st if dkk == 7 else None))
                    ev = p.op("vector", "tensor_copy",
                              bass.AP(A[f"xlnT{par}"], t * 128,
                                      [[2048, 128], [256, 8], [1, 128]]),
                              psT[t][:, :].rearrange(
                                  "p (k q) -> p k q", k=8),
                              waits=[tt], inc=sv)
                    psT_rd[t] = ev
                    last_tp = tt
                xln_free = last_tp
                xT_toks = [psT_rd[0], psT_rd[1]]
                kv_toks = []
                for m in range(2):
                    for dk in range(8):
                        tok = p.op("tensor", "matmul", psK[m][:, 0:256],
                                   A["wkv" if not qside else "wq"][
                                       :, dk * (256 if not qside else 512)
                                       + m * 128:
                                       dk * (256 if not qside else 512)
                                       + m * 128 + 128],
                                   xT[:, dk * 256:dk * 256 + 256],
                                   start=(dk == 0), stop=(dk == 7),
                                   waits=xT_toks + [wA_tok, psK_rd[m]],
                                   inc=(st if dk == 7 else None))
                    kv_toks.append(tok)
                if not qside:
                    rko = (c % 2) * 256
                    sq_toks = [
                        p.op("scalar", "activation",
                             A["sqv"][:, m * 256:m * 256 + 256],
                             psK[m][:, 0:256], AF.Square,
                             waits=[kv_toks[m]], inc=ss)
                        for m in range(2)]
                    for m in range(2):
                        t_om = p.op("tensor", "matmul", psK[0][0:1, 256:512],
                                    A["c_ones_col"][:, :],
                                    A["sqv"][:, m * 256:m * 256 + 256],
                                    start=(m == 0), stop=(m == 1),
                                    waits=[sq_toks[m], consts_tok], inc=st)
                    t_pre = p.op("vector", "tensor_scalar", A["rkv"][0:1, rko:rko + 256],
                                 psK[0][0:1, 256:512], 1.0 / KVR, EPS_RMS,
                                 ALU.mult, ALU.add, waits=[t_om], inc=sv)
                    t_sq = p.op("scalar", "activation", A["rkv"][0:1, rko:rko + 256],
                                A["rkv"][0:1, rko:rko + 256], AF.Sqrt,
                                waits=[t_pre], inc=ss)
                    t_rc = p.op("vector", "reciprocal", A["rkv"][0:1, rko:rko + 256],
                                A["rkv"][0:1, rko:rko + 256], waits=[t_sq], inc=sv)
                    t_bc = p.op("tensor", "matmul", psK[1][:, 256:512],
                                A["c_ones_row"][:, :], A["rkv"][0:1, rko:rko + 256],
                                start=True, stop=True, waits=[t_rc], inc=st)
                    t_ev = p.op("scalar", "copy", A["rkvB"][:, rko:rko + 256],
                                psK[1][:, 256:512], waits=[t_bc], inc=ss)
                    for m in range(2):
                        tok = p.op("vector", "scalar_tensor_tensor",
                                   A["kv_normT"][:, m * T + c * 256:
                                                 m * T + c * 256 + 256],
                                   psK[m][:, 0:256], A["c_g"][:, m:m + 1],
                                   A["rkvB"][:, rko:rko + 256], ALU.mult, ALU.mult,
                                   waits=[t_ev], inc=sv)
                        psK_rd[m] = tok
                    qc_tok = None
                else:
                    for m in range(2):
                        tok = p.op("vector", "tensor_copy",
                                   A["q_cT"][:, m * TPC + c * 256:
                                             m * TPC + c * 256 + 256],
                                   psK[m][:, 0:256],
                                   waits=[kv_toks[m]], inc=sv)
                        psK_rd[m] = tok
                    for m in range(2, 4):
                        mm = m - 2
                        for dk in range(8):
                            tok = p.op("tensor", "matmul", psK[mm][:, 0:256],
                                       A["wq"][:, dk * 512 + m * 128:
                                               dk * 512 + m * 128 + 128],
                                       xT[:, dk * 256:dk * 256 + 256],
                                       start=(dk == 0), stop=(dk == 7),
                                       waits=[psK_rd[mm]],
                                       inc=(st if dk == 7 else None))
                        tok = p.op("vector", "tensor_copy",
                                   A["q_cT"][:, m * TPC + c * 256:
                                             m * TPC + c * 256 + 256],
                                   psK[mm][:, 0:256], waits=[tok], inc=sv)
                        psK_rd[mm] = tok
                    qc_tok = tok
                kro = A[f"kro{par}"]
                rope_evs = []
                for m in range(4):
                    pr = m % 2
                    ro = pr * 512
                    bs_, bw_ = psR[pr * 2], psR[pr * 2 + 1]
                    if not qside:
                        wst, wsw, nk = A["wkr"], A["wkr_sw"], 8

                        def rr(dk):
                            return xT[:, dk * 256:dk * 256 + 256]
                        rw = xT_toks
                    else:
                        wst, wsw, nk = A["wqr"], A["wqr_sw"], 4

                        def rr(dk):
                            return A["q_cT"][:, dk * TPC + c * 256:
                                             dk * TPC + c * 256 + 256]
                        rw = [qc_tok]
                    for dk in range(nk):
                        t_s = p.op("tensor", "matmul", bs_[:, 0:256],
                                   wst[:, dk * 512 + m * 128:
                                       dk * 512 + m * 128 + 128], rr(dk),
                                   start=(dk == 0), stop=(dk == nk - 1),
                                   waits=rw + [psR_rd[pr * 2]],
                                   inc=(st if dk == nk - 1 else None))
                    for dk in range(nk):
                        t_w = p.op("tensor", "matmul", bw_[:, 0:256],
                                   wsw[:, dk * 512 + m * 128:
                                       dk * 512 + m * 128 + 128], rr(dk),
                                   start=(dk == 0), stop=(dk == nk - 1),
                                   waits=[psR_rd[pr * 2 + 1]],
                                   inc=(st if dk == nk - 1 else None))
                    ct = A[f"tabc{par}"][:, m * 256:m * 256 + 256]
                    st_ = A[f"tabs{par}"][:, m * 256:m * 256 + 256]
                    t1 = p.op("vector", "tensor_tensor",
                              A["ropet1"][:, ro:ro + 256], bs_[:, 0:256], ct,
                              ALU.mult,
                              waits=[t_s, tc, ropet1_free[pr]], inc=sv)
                    psR_rd[pr * 2] = t1
                    t2 = p.op("vector", "tensor_tensor",
                              A["ropet1"][:, ro + 256:ro + 512],
                              bw_[:, 0:256], st_, ALU.mult,
                              waits=[t_w, tsn, ropet1_free[pr]], inc=sv)
                    psR_rd[pr * 2 + 1] = t2
                    dst = (kro[:, m * 256:m * 256 + 256] if not qside else
                           A["q_ropeT"][:, m * TPC + c * 256:
                                        m * TPC + c * 256 + 256])
                    t3 = p.op("gpsimd", "tensor_add", dst,
                              A["ropet1"][:, ro:ro + 256],
                              A["ropet1"][:, ro + 256:ro + 512],
                              waits=[t1, t2,
                                     krofree[par] if not qside else None],
                              inc=sp)
                    ropet1_free[pr] = t3
                    rope_evs.append(t3)
                tabfree[par] = rope_evs[-1]
                xTfree[par] = rope_evs[-1]
                if not qside:
                    kd = kro_d[:, c * 256:(c + 1) * 256].rearrange(
                        "(m p) t -> p m t", p=128)
                    krofree[par] = p.dma(
                        "scalar", kd,
                        kro[:, :].rearrange("p (m t) -> p m t", m=4),
                        waits=[rope_evs[-1]], inc=dKr[par])

        sweep(NKC, d["x_b"], d["kcos"], d["kssin"], qside=False)
        sweep(NQC, d["x_q"], d["qcos"], d["qssin"], qside=True)

    if dev == 1:
        dd = nc.dram_tensor("dbg_kvnorm", [128, 2 * T], BF16,
                            kind="ExternalOutput")
        p.dma("sync", dd[:, :], A["kv_normT"][:, :],
              waits=[p.now(sv), p.now(st), p.now(ss), p.now(dKr[0]),
                     p.now(dKr[1])], inc=dC)
        dd2 = nc.dram_tensor("dbg_qc", [128, 4 * TPC], BF16,
                             kind="ExternalOutput")
        p.dma("sync", dd2[:, :], A["q_cT"][:, :], waits=[p.now(sv)], inc=dC)
        dd3 = nc.dram_tensor("dbg_qrope", [128, 4 * TPC], BF16,
                             kind="ExternalOutput")
        p.dma("sync", dd3[:, :], A["q_ropeT"][:, :], waits=[p.now(sv)],
              inc=dC)
        dd4 = nc.dram_tensor("dbg_lnst", [128, 32], F32, kind="ExternalOutput")
        p.dma("sync", dd4[:, :], A["lnst"][:, :],
              waits=[p.now(sv), p.now(sp), p.now(ss)], inc=dC)
        dd5 = nc.dram_tensor("dbg_xsq", [128, 1024], F32,
                             kind="ExternalOutput")
        p.dma("sync", dd5[:, :], A["xsq"][:, :], waits=[p.now(ss)], inc=dC)
        dd6 = nc.dram_tensor("dbg_xbuf", [128, 2048], F32,
                             kind="ExternalOutput")
        p.dma("sync", dd6[:, :], A["xbuf1"][:, :], waits=[p.now(sv)], inc=dC)
        return finish()

    p.barrier()

    for i in range(2):
        p.op("vector", "memset", A[f"KTb{i}"][64:128, :], 0.0, inc=sv)
        p.op("vector", "memset", A[f"QTb{i}"][64:128, :], 0.0, inc=sv)
        for base in (64, 132):
            p.op("vector", "memset",
                 bass.AP(A[f"Vb{i}"], base, [[64 * 136 + 32, 128], [136, 64],
                                             [1, 4]]),
                 1.0, inc=sv)
    init_tok = p.now(sv)
    ones_toks = [p.dma("sync", A[f"KTb{i}"][96:97, :], d["ones_8k"][:, :],
                       waits=[init_tok], inc=dC) for i in range(2)]
    ones_tok = ones_toks[-1]

    with ExitStack() as pctx:
        psKC = [pctx.enter_context(nc.psum_tensor(f"psKC{i}", [128, 512], F32))
                for i in range(2)]
        psVD = [pctx.enter_context(nc.psum_tensor(f"psVD{i}", [128, 512], F32))
                for i in range(2)]
        psQC = [pctx.enter_context(nc.psum_tensor(f"psQC{i}", [128, 512], F32))
                for i in range(2)]
        kc_rd = [None, None]; vd_rd = [None, None]; qc_rd = [None, None]
        buf_free = [None, None]
        vbuf_free = [None, None]

        for h in range(NH):
            par = h % 2
            KTb, QTb = A[f"KTb{par}"], A[f"QTb{par}"]
            ev = None
            for n in range(16):
                pp = n % 2
                for kk in range(2):
                    t_mm = p.op("tensor", "matmul", psKC[pp][0:64, 0:512],
                                A["wkdec"][:, kk * D + h * 64:
                                           kk * D + h * 64 + 64],
                                A["kv_normT"][:, kk * T + n * 512:
                                              kk * T + n * 512 + 512],
                                start=(kk == 0), stop=(kk == 1),
                                waits=[wA2_tok, kc_rd[pp], buf_free[par],
                                       ones_tok],
                                inc=(st if kk else None))
                ev = p.op("vector", "tensor_copy",
                          KTb[0:64, n * 512:n * 512 + 512],
                          psKC[pp][0:64, 0:512], waits=[t_mm], inc=sv)
                kc_rd[pp] = ev
            t_rope = p.dma("sync", KTb[64:96, :],
                           kro_d[h * 32:h * 32 + 32, :],
                           waits=[buf_free[par]], inc=dRp[par])
            t_store = p.dma("scalar", kall_d[h], KTb[:, :],
                            waits=[ev, t_rope], inc=dSt[par])
            if h % 2 == 0:
                vpar = (h // 2) % 2
                pr = h // 2
                Vb = A[f"Vb{vpar}"]
                vev = None
                for ng in range(32):
                    pp = ng % 2
                    for sub in range(2):
                        n = ng * 2 + sub
                        for kk in range(2):
                            t_mm = p.op("tensor", "matmul",
                                        psVD[pp][:, sub * 128:sub * 128 + 128],
                                        A["kv_normT"][:, kk * T + n * 128:
                                                      kk * T + n * 128 + 128],
                                        A["wvdec"][:, kk * D + pr * 128:
                                                   kk * D + pr * 128 + 128],
                                        start=(kk == 0), stop=(kk == 1),
                                        waits=[vd_rd[pp], vbuf_free[vpar],
                                               init_tok],
                                        inc=(st if (kk and sub) else None))
                    vev = p.op("vector", "tensor_copy",
                               bass.AP(Vb, ng * 272,
                                       [[64 * 136 + 32, 128], [136, 2],
                                        [68, 2], [1, 64]]),
                               bass.AP(psVD[pp], 0,
                                       [[512, 128], [128, 2], [64, 2],
                                        [1, 64]]),
                               waits=[t_mm], inc=sv)
                    vd_rd[pp] = vev
                vbuf_free[vpar] = p.dma("scalar", vall_d[pr],
                                        Vb[:, 0:64 * 136],
                                        waits=[vev], inc=dSt[par])
            qev = None
            for n in range(4):
                pp = n % 2
                for kq in range(4):
                    t_mm = p.op("tensor", "matmul", psQC[pp][0:64, 0:512],
                                A["wqdec"][:, kq * D + h * 64:
                                           kq * D + h * 64 + 64],
                                A["q_cT"][:, kq * TPC + n * 512:
                                          kq * TPC + n * 512 + 512],
                                start=(kq == 0), stop=(kq == 3),
                                waits=[qc_rd[pp], buf_free[par]],
                                inc=(st if kq == 3 else None))
                qev = p.op("vector", "tensor_scalar",
                           QTb[0:64, n * 512:n * 512 + 512],
                           psQC[pp][0:64, 0:512], SCALE, None, ALU.mult,
                           waits=[t_mm], inc=sv)
                qc_rd[pp] = qev
            t_qrope = p.dma("gpsimd", QTb[64:96, :],
                            A["q_ropeT"][32 * (h % 4):32 * (h % 4) + 32,
                                         (h // 4) * TPC:(h // 4 + 1) * TPC],
                            waits=[buf_free[par], init_tok], inc=dQr[par])
            buf_free[par] = p.dma("scalar", qall_d[h], QTb[:, :],
                                  waits=[qev, t_qrope], inc=dSt[par])

    if dev == 2:
        return finish()

    p.barrier()


    with ExitStack() as pctx:
        psW = [pctx.enter_context(nc.psum_tensor(f"psW{i}", [128, 1024], F32))
               for i in range(2)]
        psP = [pctx.enter_context(nc.psum_tensor(f"psP{i}", [128, 512], F32))
               for i in range(2)]
        psV = pctx.enter_context(nc.psum_tensor("psV", [128, 512], F32))
        psTb = pctx.enter_context(nc.psum_tensor("psTb", [128, 1024], BF16))

        kt_load = [None, None]; v_load = [None, None]; qt_load = [None, None]
        kt_free = [None, None]; v_free = [None, None]; qt_free = [None, None]
        psW_rd = [None, None]; psP_rd = [None, None]
        psV_rd = [None, None]
        probs_free = [None, None]
        mask_free = [None, None]
        masktmp_free = None
        psTb_rd = None
        ones_init = None
        rlB_free = None
        mrow_busy = [None]

        vpad_init = [p.op("vector", "memset", A[f"vB{i}"][:, 8704:8736],
                          0.0, inc=sv) for i in range(2)]

        def load_head(h):
            par = h % 2
            kt_load[par] = p.dma("sync", A[f"ktB{par}"][:, :], kall_d[h],
                                 waits=[kt_free[par]], inc=dK[par])
            qt_load[par] = p.dma("sync", A[f"qtB{par}"][:, :], qall_d[h],
                                 waits=[qt_free[par]], inc=dQ[par])
            if h % 2 == 0:
                vp = (h // 2) % 2
                v_load[vp] = p.dma("sync", A[f"vB{vp}"][:, 0:8704],
                                   vall_d[h // 2],
                                   waits=[v_free[vp], vpad_init[vp]],
                                   inc=dV[vp])

        def pass1(h):
            nonlocal psTb_rd
            par = h % 2
            ktB, qtB = A[f"ktB{par}"], A[f"qtB{par}"]
            red_toks = []
            pp = 0
            for j in range(NJ):
                E = _E(j)
                n = E * 4
                ps = psP[pp]
                t_mm = p.op("tensor", "matmul", ps[:, 0:n],
                            qtB[:, j * 128:j * 128 + 128],
                            bass.AP(ktB, 0, [[T, 128], [32, n]]),
                            start=True, stop=True,
                            waits=[kt_load[par], qt_load[par], psP_rd[pp],
                                   psV_rd[1] if pp == 0 else None], inc=st)
                t_r = p.op("vector", "tensor_reduce",
                           A["mcols"][:, j:j + 1], ps[:, 0:n],
                           AX.X, ALU.max, waits=[t_mm], inc=sv)
                psP_rd[pp] = t_r
                pp ^= 1
                red_toks.append(t_r)
            t_n = p.op("gpsimd", "tensor_scalar", A["mneg"][:, :],
                       A["mcols"][:, :], -1.0, -MARGIN, ALU.mult, ALU.add,
                       waits=red_toks, inc=sp)
            t_t = p.op("tensor", "transpose", psTb[0:16, 0:128],
                       A["mneg"][:, :], A["c_eye_bf"][:, :],
                       waits=[t_n, psTb_rd], inc=st)
            t_e = p.op("vector", "tensor_copy", A["mrow"][:, :],
                       psTb[0:16, 0:128], waits=[t_t], inc=sv)
            psTb_rd = t_e
            t_d1 = p.dma("gpsimd",
                         mrow_d[0:1, :].rearrange("p (j q) -> p j q", j=16),
                         A["mrow"][:, :], waits=[t_e, mrow_busy[0]],
                         inc=dMr)
            t_d2 = p.dma("gpsimd", qtB[96:97, :], mrow_d[0:1, :],
                         waits=[t_d1], inc=dMr)
            mrow_busy[0] = t_d2
            return t_d2

        def pass2(h, row_tok):
            nonlocal psV_rd, masktmp_free, ones_init
            par = h % 2
            vp = (h // 2) % 2
            voff = (h % 2) * 68
            ktB, qtB = A[f"ktB{par}"], A[f"qtB{par}"]
            vB = A[f"vB{vp}"]
            if ones_init is None:
                ones_init = p.op("vector", "memset", A["ones64"][:, :], 1.0,
                                 inc=sv)
            bpar = 0
            pend = []
            rl_busy = [None]
            pend_fin = [None]

            def do_finalize(pend_fin):
                if pend_fin[0] is None:
                    return
                fpj, flast = pend_fin[0]
                pend_fin[0] = None
                psv = psV if fpj % 2 == 0 else psP[0]
                t_rl = p.op("vector", "reciprocal", A["rl64"][64:65, 0:256],
                            psv[64:65, 0:256],
                            waits=[flast, ones_init, rl_busy[0]], inc=sv)
                t_bc = p.op("tensor", "matmul", psP[1][0:64, 0:256],
                            A["ones64"][64:65, 0:64],
                            A["rl64"][64:65, 0:256],
                            start=True, stop=True,
                            waits=[t_rl, psP_rd[1]], inc=st)
                rl_busy[0] = t_bc
                t_rb2 = p.op("vector", "tensor_copy", A["rlB"][0:64, 0:256],
                             psP[1][0:64, 0:256], waits=[t_bc], inc=sv)
                psP_rd[1] = t_rb2
                t_o = p.op("vector", "tensor_tensor",
                           A["attn_o64"][0:64, h * TPC + fpj * 256:
                                         h * TPC + fpj * 256 + 256],
                           psv[0:64, 0:256], A["rlB"][0:64, 0:256],
                           ALU.mult, waits=[t_rb2], inc=sv)
                psV_rd[fpj % 2] = t_o
                if fpj % 2 == 1:
                    psP_rd[0] = t_o

            def flush_pv(pend):
                if not pend:
                    return None
                bk_, prb_, t_e_, E0_, vp_, voff_, pvb_, bp_, w_ = pend.pop(0)
                psv = psV if pvb_ == 0 else psP[0]
                lp = None
                for i, kt in enumerate(bk_):
                    lp = p.op(
                        "tensor", "matmul", psv[0:96, 0:w_],
                        A[f"vB{vp_}"][:, kt * 136 + voff_:
                                      kt * 136 + voff_ + 96],
                        prb_[:, i * w_:i * w_ + w_],
                        start=(kt == 0), stop=(kt == E0_ - 1),
                        waits=[t_e_, psV_rd[pvb_] if kt == 0 else None,
                               psP_rd[0] if (pvb_ == 1 and kt == 0) else None,
                               v_load[vp_]],
                        inc=(st if kt == E0_ - 1 else None))
                probs_free[bp_] = lp
                return lp

            for pj in range(NJ // 2):
                E0 = _E(2 * pj)
                mpar = pj % 2
                t_ml = p.dma("sync",
                             A[f"maskb{mpar}"][:, :].rearrange(
                                 "p (k q) -> p k q", k=8),
                             d["maskT"][pj].rearrange("k p q -> p k q"),
                             waits=[mask_free[mpar]], inc=dM[mpar])
                nb = E0 // 4
                last_pv = None
                for bi in range(nb):
                    bk = list(range(bi * 4, bi * 4 + 4))
                    lastb = (bi == nb - 1)
                    w = 128 if lastb else 256
                    ps = psW[bpar]
                    for i, kt in enumerate(bk):
                        tok = p.op("tensor", "matmul",
                                   ps[:, i * w:i * w + w],
                                   ktB[:, kt * 128:kt * 128 + 128],
                                   qtB[:, pj * 256:pj * 256 + w],
                                   start=True, stop=True,
                                   waits=[psW_rd[bpar], row_tok,
                                          kt_load[par]],
                                   inc=(st if i == 3 else None))
                    mm_tok = tok
                    if len(pend) >= 2:
                        fl = flush_pv(pend)
                        if fl is not None:
                            last_pv = fl
                    if bi == 1:
                        do_finalize(pend_fin)
                    prb = A[f"probsT{bpar}"]
                    if bi < nb - 2:
                        t_e = p.op("scalar", "activation", prb[:, :],
                                   ps[:, :], AF.Exp,
                                   waits=[mm_tok, probs_free[bpar]], inc=ss)
                    elif not lastb:
                        t_a = p.op("vector", "tensor_add",
                                   A["masktmp"][:, :], ps[:, :],
                                   A[f"maskb{mpar}"][:, 0:1024],
                                   waits=[mm_tok, t_ml, masktmp_free],
                                   inc=sv)
                        t_e = p.op("scalar", "activation", prb[:, :],
                                   A["masktmp"][:, :], AF.Exp,
                                   waits=[t_a, probs_free[bpar]], inc=ss)
                        masktmp_free = t_e
                    else:
                        t_a = p.op("vector", "tensor_add",
                                   bass.AP(A["masktmp"], 0,
                                           [[1024, 128], [128, 4], [1, 128]]),
                                   bass.AP(ps, 0,
                                           [[1024, 128], [128, 4], [1, 128]]),
                                   bass.AP(A[f"maskb{mpar}"], 1024,
                                           [[2048, 128], [256, 4], [1, 128]]),
                                   waits=[mm_tok, t_ml, masktmp_free],
                                   inc=sv)
                        t_e = p.op("scalar", "activation", prb[:, 0:512],
                                   A["masktmp"][:, 0:512], AF.Exp,
                                   waits=[t_a, probs_free[bpar]], inc=ss)
                        masktmp_free = t_e
                        mask_free[mpar] = t_e
                    psW_rd[bpar] = t_e
                    pend.append((bk, prb, t_e, E0, vp, voff, pj % 2, bpar, w))
                    bpar ^= 1
                while pend:
                    fl = flush_pv(pend)
                    if fl is not None:
                        last_pv = fl
                pend_fin[0] = (pj, last_pv)
                if pj == NJ // 2 - 1:
                    do_finalize(pend_fin)
            kt_free[par] = p.now(st)
            qt_free[par] = p.now(st)
            if h % 2 == 1:
                v_free[vp] = p.now(st)

        horder = list(range(NH))
        if dev == 31:
            horder = [2, 3, 0, 1] + list(range(4, NH))
        load_head(horder[0])
        load_head(horder[1])
        for hi in range(NH):
            h = horder[hi]
            rt = pass1(h)
            pass2(h, rt)
            if hi + 2 < NH:
                load_head(horder[hi + 2])

    if dev in (3, 31):
        dd = nc.dram_tensor("dbg_ao", [64, NH * TPC], BF16,
                            kind="ExternalOutput")
        p.dma("sync", dd[:, :], A["attn_o64"][0:64, :],
              waits=[p.now(sv), p.now(st), p.now(ss)], inc=dC)
        return finish()

    p.barrier()

    p.dma("gpsimd", A["wo"][:, :].rearrange("p (h n) -> p h n", h=16),
          d["w_o"].rearrange("(h p) n -> p h n", p=64), inc=dW)
    wo_tok = p.now(dW)
    with ExitStack() as pctx:
        psY = [pctx.enter_context(nc.psum_tensor(f"psY{i}", [128, 1024], F32))
               for i in range(2)]
        psT2 = [pctx.enter_context(nc.psum_tensor(f"psT2{i}", [128, 1024],
                                                  BF16)) for i in range(2)]
        psRt = pctx.enter_context(nc.psum_tensor("psRt", [128, 512], F32))
        psCw = pctx.enter_context(nc.psum_tensor("psCw", [128, 512], F32))

        xq_free = [None, None]; psY_rd = [None, None]
        psT2_rd = [None, None]; ys_free = [None, None]
        setfree2 = [None] * 4
        for t in range(NJ):
            par = t % 2
            t_x = p.dma("sync", A[f"xq{par}"][:, :],
                        d["x_q"][t * 128:(t + 1) * 128, :],
                        waits=[xq_free[par]], inc=dXQ[par])
            for half in range(2):
                for hh in range(NH):
                    t_mm = p.op("tensor", "matmul",
                                psY[par][:, half * 512:half * 512 + 512],
                                A["attn_o64"][0:64, hh * TPC + t * 128:
                                              hh * TPC + t * 128 + 128],
                                A["wo"][0:64, hh * 1024 + half * 512:
                                        hh * 1024 + half * 512 + 512],
                                start=(hh == 0), stop=(hh == NH - 1),
                                waits=[wo_tok, psY_rd[par]],
                                inc=(st if hh == NH - 1 else None))
            yb = A[f"yb{par}"]
            t_y = p.op("vector", "tensor_add", yb[:, :], psY[par][:, :],
                       A[f"xq{par}"][:, :],
                       waits=[t_mm, t_x, ys_free[par]], inc=sv)
            psY_rd[par] = t_y
            lb2 = (t % 4) * 8
            t_r0 = p.op("scalar", "activation", A["ylnb"][:, :], yb[:, :],
                        AF.Copy, accum_out=A["lnst2"][:, lb2 + 0:lb2 + 1],
                        waits=[t_y, setfree2[t % 4]], inc=ss)
            t_sq = p.op("scalar", "activation", A[f"xq{par}"][:, :], yb[:, :],
                        AF.Square, accum_out=A["lnst2"][:, lb2 + 1:lb2 + 2],
                        waits=[t_y, t_r0], inc=ss)
            xq_free[par] = t_sq
            t_r1 = t_sq
            t_ms = p.op("gpsimd", "tensor_scalar",
                        A["lnst2"][:, lb2 + 2:lb2 + 4],
                        A["lnst2"][:, lb2 + 0:lb2 + 2], 1.0 / D, None,
                        ALU.mult, waits=[t_r0, t_r1, setfree2[t % 4]],
                        inc=sp)
            p.op("gpsimd", "tensor_tensor", A["lnst2"][:, lb2 + 4:lb2 + 5],
                 A["lnst2"][:, lb2 + 2:lb2 + 3], A["lnst2"][:, lb2 + 2:lb2 + 3], ALU.mult)
            t_ve = p.op("gpsimd", "tensor_scalar", A["lnst2"][:, lb2 + 5:lb2 + 6],
                        A["lnst2"][:, lb2 + 3:lb2 + 4], A["lnst2"][:, lb2 + 4:lb2 + 5], EPS_LN,
                        ALU.subtract, ALU.add, inc=sp)
            t_s2 = p.op("scalar", "activation", A["lnst2"][:, lb2 + 6:lb2 + 7],
                        A["lnst2"][:, lb2 + 5:lb2 + 6], AF.Sqrt,
                        waits=[t_ve], inc=ss)
            t_rc2 = p.op("vector", "reciprocal", A["lnst2"][:, lb2 + 7:lb2 + 8],
                         A["lnst2"][:, lb2 + 6:lb2 + 7], waits=[t_s2], inc=sv)
            t_cp2 = p.op("gpsimd", "tensor_copy", A["lnst2"][:, lb2 + 5:lb2 + 6],
                         A["lnst2"][:, lb2 + 7:lb2 + 8], waits=[t_rc2], inc=sp)
            t_ln = p.op("vector", "tensor_scalar", A["ylnb2"][:, :],
                        A["ylnb"][:, :],
                        A["lnst2"][:, lb2 + 2:lb2 + 3], A["lnst2"][:, lb2 + 5:lb2 + 6],
                        ALU.subtract, ALU.mult,
                        waits=[t_cp2, t_ms, t_r0], inc=sv)
            setfree2[t % 4] = t_ln
            t_ys = p.dma("scalar", ys_d[t * 128:(t + 1) * 128, :], yb[:, :],
                         waits=[t_y, t_ln], inc=dYs[par])
            ys_free[par] = t_ys
            tp = t % 2
            for dk in range(8):
                t_t = p.op("tensor", "transpose",
                           psT2[tp][:, dk * 128:dk * 128 + 128],
                           A["ylnb2"][:, dk * 128:dk * 128 + 128],
                           A["c_eye_bf"][:, :],
                           waits=[t_ln, psT2_rd[tp]],
                           inc=(st if dk == 7 else None))
            t_e = p.op("vector", "tensor_copy",
                       bass.AP(A["y_lnT"], t * 128,
                               [[8 * TPC, 128], [TPC, 8], [1, 128]]),
                       psT2[tp][:, :].rearrange("p (k q) -> p k q", k=8),
                       waits=[t_t], inc=sv)
            psT2_rd[tp] = t_e
        ylnT_tok = p.now(sv)
        t_rb = p.dma("sync", A["lgt"][0:1, 0:8], d["rb"][:, :], inc=dC)
        t_rbb = p.op("tensor", "matmul", psRt[:, 128:136],
                     A["c_ones_row"][:, :], A["lgt"][0:1, 0:8],
                     start=True, stop=True, waits=[t_rb, consts_tok], inc=st)
        t_rbe = p.op("scalar", "copy", A["rbB"][:, :],
                     psRt[:, 128:136], waits=[t_rbb], inc=ss)
        t_wr = p.dma("gpsimd",
                     A["wrt"][:, :].rearrange("p (k e) -> p k e", k=8),
                     d["wr"].rearrange("(k p) e -> p k e", p=128), inc=dW)
        prev_l = None
        sig_toks = []
        for t in range(NJ):
            for dk in range(8):
                t_mm = p.op("tensor", "matmul", psRt[:, 0:8],
                            A["y_lnT"][:, dk * TPC + t * 128:
                                       dk * TPC + t * 128 + 128],
                            A["wrt"][:, dk * 8:dk * 8 + 8],
                            start=(dk == 0), stop=(dk == 7),
                            waits=[ylnT_tok, t_wr, prev_l, t_rbe],
                            inc=(st if dk == 7 else None))
            t_l = p.op("vector", "tensor_add", A["lgt"][:, 0:8],
                       psRt[:, 0:8], A["rbB"][:, :],
                       waits=[t_mm, t_rbe,
                              sig_toks[-1] if sig_toks else None], inc=sv)
            prev_l = t_l
            t_sg = p.op("scalar", "activation",
                        A["cwbuf"][:, t * 8:t * 8 + 8], A["lgt"][:, 0:8],
                        AF.Sigmoid, waits=[t_l], inc=ss)
            sig_toks.append(t_sg)

        def sl_e(e):
            return bass.AP(A["cwbuf"], e, [[128, 128], [8, 16]])

        sig_tok = p.now(ss)
        probs3 = bass.AP(A["cwbuf"], 0, [[128, 128], [8, 16], [1, 8]])
        t_m1 = p.op("vector", "tensor_reduce", A["r_m1"][:, :], probs3,
                    AX.X, ALU.max, waits=[sig_tok], inc=sv)
        t_m1c = p.op("scalar", "copy", A["r_m2"][:, :], A["r_m1"][:, :],
                     waits=[t_m1], inc=ss)
        t_ge = None
        for t in range(NJ):
            t_ge = p.op("vector", "tensor_scalar",
                        A["r_ge"][:, t * 1:t + 1] if False else
                        A["cw"][:, t * 8:t * 8 + 8],
                        A["cwbuf"][:, t * 8:t * 8 + 8],
                        A["r_m2"][:, t:t + 1], None, ALU.is_ge,
                        waits=[t_m1c], inc=sv)
        t_gec = p.op("scalar", "copy", A["cwT"][:, :], A["cw"][:, :],
                     waits=[t_ge], inc=ss)
        t_msk = p.op("vector", "scalar_tensor_tensor", A["cw"][:, :],
                     A["cwT"][:, :], -1e30, A["cwbuf"][:, :],
                     ALU.mult, ALU.add, waits=[t_gec], inc=sv)
        t_mskc = p.op("scalar", "copy", A["cwT"][:, :], A["cw"][:, :],
                      waits=[t_msk], inc=ss)
        masked3 = bass.AP(A["cwT"], 0, [[128, 128], [8, 16], [1, 8]])
        t_m2 = p.op("vector", "tensor_reduce", A["r_m2"][:, :], masked3,
                    AX.X, ALU.max, waits=[t_mskc], inc=sv)
        t_m2c = p.op("scalar", "copy", A["r_ge"][:, :], A["r_m2"][:, :],
                     waits=[t_m2], inc=ss)
        t_ge2 = None
        for t in range(NJ):
            t_ge2 = p.op("vector", "tensor_scalar",
                         A["cw"][:, t * 8:t * 8 + 8],
                         A["cwbuf"][:, t * 8:t * 8 + 8],
                         A["r_ge"][:, t:t + 1], None, ALU.is_ge,
                         waits=[t_m2c], inc=sv)
        t_gc2 = p.op("scalar", "copy", A["cwT"][:, :], A["cw"][:, :],
                     waits=[t_ge2], inc=ss)
        t_cw = p.op("vector", "tensor_mul", A["cw"][:, :], A["cwT"][:, :],
                    A["cwbuf"][:, :], waits=[t_gc2], inc=sv)
        cw_toks = [t_cw]
        t_ct = p.op("tensor", "transpose", psCw[:, 0:128], A["cw"][:, :],
                    A["c_eye_f"][:, :], waits=cw_toks + [consts_tok], inc=st)
        t_ce = p.op("vector", "tensor_copy", A["cwT"][:, :], psCw[:, 0:128],
                    waits=[t_ct], inc=sv)
        cwT_dma = p.dma("scalar", cwT_d[:, :], A["cwT"][:, :],
                        waits=[t_ce], inc=dYs[0])

    if dev == 4:
        dd = nc.dram_tensor("dbg_ylnT", [128, 8 * TPC], BF16,
                            kind="ExternalOutput")
        p.dma("sync", dd[:, :], A["y_lnT"][:, :],
              waits=[p.now(sv), p.now(ss), p.now(st), p.now(dYs[0]),
                     p.now(dYs[1])], inc=dC)
        dd2 = nc.dram_tensor("dbg_cw", [128, 128], F32, kind="ExternalOutput")
        p.dma("sync", dd2[:, :], A["cw"][:, :], waits=[p.now(sv)], inc=dC)
        return finish()

    p.barrier()

    with ExitStack() as pctx:
        psG = pctx.enter_context(nc.psum_tensor("psG", [128, 1024], F32))
        psU = pctx.enter_context(nc.psum_tensor("psU", [128, 1024], F32))
        psO = [pctx.enter_context(nc.psum_tensor(f"psO{i}", [128, 512], F32))
               for i in range(2)]

        ring_free = [None] * RING
        gtile = [0]

        def ring_load(src_ap, kt):
            slot = gtile[0] % RING
            gtile[0] += 1
            ncol = src_ap.shape[1] * src_ap.shape[2]
            ds = S[f"ring{slot}"][:, 0:ncol].rearrange(
                "p (k m) -> p k m", k=kt)
            tok = p.dma("gpsimd", ds, src_ap,
                        waits=[ring_free[slot]], inc=dR[slot])
            return slot, tok

        psG_rd = [None, None]; psU_rd = [None, None]
        psO_rd = [None, None]
        cw_tok = cw_toks[-1]
        yt_free = [None, None]
        tf_prev = {}
        act_guard = [None]

        for chunk in range(2):
            coff = chunk * 1024
            acc_toks = {}
            for e in range(NE + 1):
                routed = e > 0
                if routed:
                    wg_src, wu_src, wd_src = (d["wg"][e - 1], d["wu"][e - 1],
                                              d["wd"][e - 1])
                else:
                    wg_src, wu_src, wd_src = d["wg_s"], d["wu_s"], d["wd_s"]
                act_toks = []
                for q in range(4):
                    sg, tg = ring_load(
                        wg_src[:, q * 512:(q + 1) * 512].rearrange(
                            "(k p) m -> p k m", p=128), 8)
                    su, tu = ring_load(
                        wu_src[:, q * 512:(q + 1) * 512].rearrange(
                            "(k p) m -> p k m", p=128), 8)
                    t_g = t_u = None
                    for hi in range(4):
                        ht = q * 4 + hi
                        for sub in range(2):
                            i = (ht * 2 + sub) % 2
                            for dk in range(8):
                                t_g = p.op(
                                    "tensor", "matmul",
                                    psG[:, i * 512:i * 512 + 512],
                                    S[f"ring{sg}"][:, dk * 512 + hi * 128:
                                                   dk * 512 + hi * 128 + 128],
                                    A["y_lnT"][:, dk * TPC + coff + sub * 512:
                                               dk * TPC + coff + sub * 512
                                               + 512],
                                    start=(dk == 0), stop=(dk == 7),
                                    waits=[tg, psG_rd[i]],
                                    inc=(st if dk == 7 else None))
                            for dk in range(8):
                                t_u = p.op(
                                    "tensor", "matmul",
                                    psU[:, i * 512:i * 512 + 512],
                                    S[f"ring{su}"][:, dk * 512 + hi * 128:
                                                   dk * 512 + hi * 128 + 128],
                                    A["y_lnT"][:, dk * TPC + coff + sub * 512:
                                               dk * TPC + coff + sub * 512
                                               + 512],
                                    start=(dk == 0), stop=(dk == 7),
                                    waits=[tu, psU_rd[i]],
                                    inc=(st if dk == 7 else None))
                            t_si = p.op("scalar", "activation",
                                        A["sil"][:, i * 512:i * 512 + 512],
                                        psG[:, i * 512:i * 512 + 512],
                                        AF.Silu, waits=[t_g, psU_rd[i]],
                                        inc=ss)
                            psG_rd[i] = t_si
                            dst = A["actT"][:, ht * 1024 + sub * 512:
                                            ht * 1024 + sub * 512 + 512]
                            t_ac = p.op("vector", "scalar_tensor_tensor",
                                        dst,
                                        psU[:, i * 512:i * 512 + 512],
                                        1.0,
                                        A["sil"][:, i * 512:i * 512 + 512],
                                        ALU.mult, ALU.mult,
                                        waits=[t_u, t_si, act_guard[0]],
                                        inc=sv)
                            psU_rd[i] = t_ac
                            act_toks.append(t_ac)
                    ring_free[sg] = t_g
                    ring_free[su] = t_u
                wd_slots = []
                for q in range(4):
                    wd_slots.append(ring_load(
                        wd_src[q * 512:(q + 1) * 512, :].rearrange(
                            "(k p) m -> p k m", p=128), 4))
                act_tok = act_toks[-1]
                for tt in range(8):
                    for half in range(2):
                        i = (tt * 2 + half) % 2
                        t_o = None
                        for ht in range(16):
                            sd, td = wd_slots[ht // 4]
                            t_o = p.op(
                                "tensor", "matmul", psO[i][:, 0:512],
                                A["actT"][:, ht * 1024 + tt * 128:
                                          ht * 1024 + tt * 128 + 128],
                                S[f"ring{sd}"][:, (ht % 4) * 1024
                                               + half * 512:
                                               (ht % 4) * 1024 + half * 512
                                               + 512],
                                start=(ht == 0), stop=(ht == 15),
                                waits=[td, act_tok, psO_rd[i]],
                                inc=(st if ht == 15 else None))
                        sl = A["out_acc"][:, tt * 1024 + half * 512:
                                          tt * 1024 + half * 512 + 512]
                        if e == 0:
                            t_a = p.op("vector", "tensor_copy", sl,
                                       psO[i][:, 0:512],
                                       waits=[t_o, tf_prev.get(tt)], inc=sv)
                        else:
                            gcol = (chunk * 8 + tt) * 8 + (e - 1)
                            t_a = p.op("vector", "scalar_tensor_tensor", sl,
                                       psO[i][:, 0:512],
                                       A["cw"][:, gcol:gcol + 1], sl,
                                       ALU.mult, ALU.add,
                                       waits=[t_o, acc_toks[(tt, half)],
                                              cw_tok], inc=sv)
                        psO_rd[i] = t_a
                        acc_toks[(tt, half)] = t_a
                for sd, td in wd_slots:
                    ring_free[sd] = p.now(st)
            for tt in range(8):
                par = tt % 2
                gt = chunk * 8 + tt
                t_y = p.dma("sync", A[f"ytile{par}"][:, :],
                            ys_d[gt * 128:(gt + 1) * 128, :],
                            waits=[yt_free[par], p.now(dYs[0]),
                                   p.now(dYs[1])], inc=dYt[par])
                t_f = p.op("vector", "tensor_add", A[f"ytile{par}"][:, :],
                           A[f"ytile{par}"][:, :],
                           bass.AP(A["out_acc"], tt * 1024,
                                   [[8192, 128], [1, 1024]]),
                           waits=[t_y, acc_toks[(tt, 0)],
                                  acc_toks[(tt, 1)]], inc=sv)
                t_od = p.dma("sync", out_d[gt], A[f"ytile{par}"][:, :],
                             waits=[t_f], inc=dOut[par])
                yt_free[par] = t_od
                tf_prev[tt] = t_f
            act_guard[0] = p.now(st)

    return finish()


def kernel(**inputs):
    in_maps = _host_prep(inputs)
    nc = _build()
    res = run_bass_kernel_spmd(nc, in_maps, core_ids=list(range(8)))
    y = np.zeros((B, T, D), np.float32)
    for core in range(8):
        b, r = core // 4, core % 4
        o = np.asarray(res.results[core]["out"])
        for j in range(NJ):
            q = _Q(r, j)
            y[b, q * 128:(q + 1) * 128] = o[j]
    return y


# version 13
# speedup vs baseline: 1.0763x; 1.0763x over previous
import numpy as np
import ml_dtypes
import concourse.bass as bass
import concourse.mybir as mybir
from concourse.bass_utils import run_bass_kernel_spmd
from contextlib import ExitStack

F32 = mybir.dt.float32
BF16 = mybir.dt.bfloat16
AF = mybir.ActivationFunctionType
ALU = mybir.AluOpType
AX = mybir.AxisListType
BF = ml_dtypes.bfloat16

B, T, D = 2, 8192, 1024
NH, HD, RD = 16, 64, 32
KVR, QR = 256, 512
FFI, NE = 2048, 8
NJ = 16
TPC = NJ * 128
SCALE = float(96.0 ** -0.5)
MARGIN = 6.0
EPS_LN, EPS_RMS = 1e-5, 1e-6
NKC = 32
NQC = 8
RING = 7

ENGINES = ("sync", "gpsimd", "vector", "scalar", "tensor")


def _Q(r, j):
    return 63 - 4 * j - r


def _E(j):
    return 64 - 4 * j


def _rope_tables():
    inv = 1.0 / 10000.0 ** (np.arange(0, RD, 2, dtype=np.float32) / RD)
    t = np.arange(T, dtype=np.float32)
    fr = np.outer(t, inv)
    emb = np.concatenate([fr, fr], -1)
    cos = np.cos(emb).astype(np.float32)
    sin = np.sin(emb).astype(np.float32)
    dd = np.arange(NH * RD) % RD
    sign = np.where(dd < RD // 2, -1.0, 1.0).astype(np.float32)
    cosT = np.ascontiguousarray(cos[:, dd].T)
    ssinT = np.ascontiguousarray((sin[:, dd] * sign[None, :]).T)
    return cosT, ssinT


def _swap_perm(ncols):
    dd = np.arange(ncols)
    j = dd % RD
    return (dd // RD) * RD + (j + RD // 2) % RD


def _host_prep(inp):
    cosT, ssinT = _rope_tables()
    f32 = np.float32
    asc = np.ascontiguousarray
    W_kr = asc(inp["L0_attn_W_krope"], f32)
    W_qr = asc(inp["L0_attn_W_qrope"], f32)
    shared = dict(
        w_kv=asc(inp["L0_attn_W_kv"], f32),
        w_kdec=asc(inp["L0_attn_W_kdec"], f32),
        w_vdec=asc(inp["L0_attn_W_vdec"], f32),
        w_q=asc(inp["L0_attn_W_q"], f32),
        w_qdec=asc(inp["L0_attn_W_qdec"], f32),
        w_kr=W_kr, w_kr_sw=asc(W_kr[:, _swap_perm(NH * RD)]),
        w_qr=W_qr, w_qr_sw=asc(W_qr[:, _swap_perm(NH * RD)]),
        w_o=asc(inp["L0_attn_W_o"], f32),
        g_kv=asc(inp["L0_attn_g_kv"], f32).reshape(2, 128).T.copy(),
        kcos=cosT, kssin=ssinT,
        wg_s=asc(inp["L0_ffn_Wg_s"].T, f32),
        wu_s=asc(inp["L0_ffn_Wu_s"].T, f32),
        wd_s=asc(inp["L0_ffn_Wd_s"].T, f32),
        wg=asc(np.transpose(inp["L0_ffn_Wg"], (0, 2, 1)), f32),
        wu=asc(np.transpose(inp["L0_ffn_Wu"], (0, 2, 1)), f32),
        wd=asc(np.transpose(inp["L0_ffn_Wd"], (0, 2, 1)), f32),
        wr=asc(inp["L0_ffn_Wr"].T, f32),
        rb=asc(inp["L0_ffn_rb"], f32).reshape(1, NE),
        eye_bf=np.eye(128, dtype=BF),
        eye_f32=np.eye(128, dtype=f32),
        ones_row_f=np.ones((1, 128), f32),
        ones_col_f=np.ones((128, 1), f32),
        ones_8k=np.ones((1, T), BF),
    )
    x = np.asarray(inp["L0_attn_x"], f32)
    in_maps = []
    for core in range(8):
        b, r = core // 4, core % 4
        qt = [_Q(r, j) for j in range(NJ)]
        x_q = asc(np.concatenate([x[b, q * 128:(q + 1) * 128] for q in qt], 0))
        qpos = np.concatenate([np.arange(q * 128, (q + 1) * 128) for q in qt])
        mask = np.zeros((NJ // 2, 8, 128, 256), np.float32)
        for pj in range(NJ // 2):
            E0 = _E(2 * pj)
            for pp in range(8):
                k0 = (E0 - 8 + pp) * 128
                ki = k0 + np.arange(128)[:, None]
                for jj in range(2):
                    q0 = qt[2 * pj + jj] * 128
                    qi = q0 + np.arange(128)[None, :]
                    mask[pj, pp, :, jj * 128:(jj + 1) * 128] = np.where(
                        ki <= qi, 0.0, -30000.0)
        m = dict(shared)
        m.update(
            x_b=x[b], x_q=x_q,
            qcos=asc(cosT[:, qpos] * SCALE),
            qssin=asc(ssinT[:, qpos] * SCALE),
            maskT=mask.astype(BF),
        )
        in_maps.append(m)
    return in_maps


class P:
    def __init__(self, nc, ctx):
        self.nc, self.ctx = nc, ctx
        self.ev = {e: [] for e in ENGINES}
        self.cnt = {}
        self.sems = []

    def sem(self, name):
        s = self.ctx.enter_context(self.nc.semaphore(name))
        self.cnt[s] = 0
        self.sems.append(s)
        return s

    def add(self, eng, fn, waits=(), inc=None, n=1):
        w = [t for t in waits if t is not None]
        tok = None
        if inc is not None:
            self.cnt[inc] += n
            tok = (inc, self.cnt[inc])
        self.ev[eng].append((fn, w, inc, n))
        return tok

    def op(self, eng, meth, *args, waits=(), inc=None, n=1, **kw):
        def fn(e, m=meth, a=args, k=kw):
            return getattr(e, m)(*a, **k)
        return self.add(eng, fn, waits, inc, n)

    def dma(self, eng, out, in_, waits=(), inc=None):
        def fn(e, o=out, i=in_):
            return e.dma_start(out=o, in_=i)
        return self.add(eng, fn, waits, inc, 16)

    def now(self, sem):
        return (sem, self.cnt[sem])

    def barrier(self):
        toks = [self.now(s) for s in self.sems if s is not self._bar]
        for e in ENGINES:
            self.op(e, "nop", waits=toks, inc=self._bar, n=1)
        full = self.now(self._bar)
        for e in ENGINES:
            self.op(e, "nop", waits=[full])

    def emit(self):
        with self.nc.Block() as block:
            for eng in ENGINES:
                def body(e, name=eng):
                    seen = {}
                    for fn, waits, inc, n in self.ev[name]:
                        for (s, v) in waits:
                            if seen.get(s, -1) < v:
                                e.wait_ge(s, v)
                                seen[s] = v
                        inst = fn(e)
                        if inc is not None:
                            inst.then_inc(inc, n)
                getattr(block, eng)(body)


def _build(dev=99):
    nc = bass.Bass("TRN2", target_bir_lowering=False)
    d = {}

    def din(name, shape, dt=F32):
        d[name] = nc.dram_tensor(name, list(shape), dt, kind="ExternalInput")
        return d[name]

    def dint(name, shape, dt):
        kind = "ExternalOutput" if dev < 99 else "Internal"
        d[name] = nc.dram_tensor(name, list(shape), dt, kind=kind)
        return d[name]

    din("x_b", (T, D)); din("x_q", (TPC, D))
    din("w_kv", (D, KVR)); din("w_kdec", (KVR, D)); din("w_vdec", (KVR, D))
    din("w_q", (D, QR)); din("w_qdec", (QR, D))
    din("w_kr", (D, NH * RD)); din("w_kr_sw", (D, NH * RD))
    din("w_qr", (QR, NH * RD)); din("w_qr_sw", (QR, NH * RD))
    din("w_o", (D, D)); din("g_kv", (128, 2))
    din("kcos", (NH * RD, T)); din("kssin", (NH * RD, T))
    din("qcos", (NH * RD, TPC)); din("qssin", (NH * RD, TPC))
    din("maskT", (NJ // 2, 8, 128, 256), BF16)
    din("wg_s", (D, FFI)); din("wu_s", (D, FFI)); din("wd_s", (FFI, D))
    din("wg", (NE, D, FFI)); din("wu", (NE, D, FFI)); din("wd", (NE, FFI, D))
    din("wr", (D, NE)); din("rb", (1, NE))
    din("eye_bf", (128, 128), BF16); din("eye_f32", (128, 128))
    din("ones_row_f", (1, 128)); din("ones_col_f", (128, 1))
    din("ones_8k", (1, T), BF16)

    out_d = nc.dram_tensor("out", [NJ, 128, D], F32, kind="ExternalOutput")

    kro_d = dint("kro_s", (NH * RD, T), BF16)
    kall_d = dint("k_all", (NH, 128, T), BF16)
    vall_d = dint("v_all", (NH // 2, 128, 64 * 136), BF16)
    qall_d = dint("qt_all", (NH, 128, TPC), BF16)
    ys_d = dint("y_s", (TPC, D), F32)
    cwT_d = dint("cwT_s", (128, 128), F32)
    mrow_d = dint("mrow_s", (1, 2048), BF16)

    S = {}

    SBASE = 16512

    def sb(name, shape, dt, off):
        S[name] = nc.alloc_sbuf_tensor_at(name, list(shape), dt,
                                          offset=SBASE + off)
        return S[name]

    sb("c_eye_bf", (128, 128), BF16, 0)
    sb("c_eye_f", (128, 128), F32, 256)
    sb("c_ones_row", (1, 128), F32, 768)
    sb("c_ones_col", (128, 1), F32, 1280)
    sb("c_g", (128, 2), F32, 1312)
    sb("wrt", (128, 64), BF16, 1344)
    sb("cwbuf", (128, 128), F32, 2048)
    sb("cw", (128, 128), F32, 2560)
    sb("cwT", (128, 128), F32, 3072)
    sb("r_m1", (128, 16), F32, 3584)
    sb("r_m2", (128, 16), F32, 3648)
    sb("r_ge", (128, 16), F32, 3712)
    sb("rbB", (128, 8), F32, 3776)
    sb("kv_normT", (128, 2 * T), BF16, 4096)
    sb("q_cT", (128, 4 * TPC), BF16, 36864)
    sb("q_ropeT", (128, 4 * TPC), BF16, 53248)
    sb("attn_o64", (128, NH * TPC), BF16, 4096)
    sb("out_acc", (128, 8 * 1024), F32, 4096)
    sb("cwB", (128, 1024), F32, 36864)
    sb("ytile0", (128, 1024), F32, 40960)
    sb("ytile1", (128, 1024), F32, 45056)
    sb("sil", (128, 1024), BF16, 49152)
    sb("an", (128, 1024), F32, 51200)
    sb("cwrow", (1, 1024), F32, 55296)
    M0 = 69632
    o = M0
    for nm, cols, dt, npart in [
        ("wkv", 8 * KVR, BF16, 128), ("wkr", 8 * 512, BF16, 128),
        ("wkr_sw", 8 * 512, BF16, 128), ("wq", 8 * 512, BF16, 128),
        ("wqr", 4 * 512, BF16, 128), ("wqr_sw", 4 * 512, BF16, 128),
        ("xbuf0", 2048, F32, 128), ("xbuf1", 2048, F32, 128),
        ("xsq", 1024, F32, 128), ("xln", 2048, BF16, 128),
        ("xlnT0", 2048, BF16, 128), ("xlnT1", 2048, BF16, 128),
        ("tabc0", 1024, F32, 128), ("tabs0", 1024, F32, 128),
        ("tabc1", 1024, F32, 128), ("tabs1", 1024, F32, 128),
        ("kro0", 4096, BF16, 128), ("kro1", 4096, BF16, 128),
        ("ropet1", 1024, F32, 128), ("sqv", 512, F32, 128),
        ("rkv", 512, F32, 1), ("rkvB", 512, F32, 128), ("lnst", 32, F32, 128),
    ]:
        sz = cols * (2 if dt == BF16 else 4)
        sb(nm, (npart, cols), dt, o)
        o += (sz + 31) & ~31
    assert o <= 196352, ("A", o)
    sb("wkdec", (128, 2 * D), BF16, 196352)
    sb("wvdec", (128, 2 * D), BF16, 200448)
    sb("wqdec", (128, 4 * D), BF16, 204544)
    assert 204544 + 4 * D * 2 <= 212800
    o = M0
    for nm, cols in [
        ("KTb0", T), ("KTb1", T),
        ("Vb0", 64 * 136 + 32), ("Vb1", 64 * 136 + 32),
        ("QTb0", TPC), ("QTb1", TPC),
    ]:
        sb(nm, (128, cols), BF16, o)
        o += (cols * 2 + 31) & ~31
    assert o <= 212800, ("A2", o)
    o = M0
    for nm, cols, dt in [
        ("ktB0", T, BF16), ("ktB1", T, BF16),
        ("vB0", 64 * 136 + 32, BF16), ("vB1", 64 * 136 + 32, BF16),
        ("qtB0", TPC, BF16), ("qtB1", TPC, BF16),
        ("probsT0", 1024, BF16), ("probsT1", 1024, BF16),
        ("maskb0", 2048, BF16), ("maskb1", 2048, BF16),
        ("ones64", 64, F32), ("rl64", 256, F32), ("rlB", 256, F32),
        ("p1tmp", 8, F32), ("mcols", 16, F32), ("mneg", 16, BF16),
    ]:
        sz = cols * (2 if dt == BF16 else 4)
        sb(nm, (128, cols), dt, o)
        o += (sz + 31) & ~31
    sb("mrow", (16, 128), BF16, o); o += 256
    assert o <= 163584, ("B", o)
    sb("masktmp", (128, 1024), F32, 163584)
    o = M0
    for nm, cols, dt in [
        ("xq0", 1024, F32), ("xq1", 1024, F32),
        ("yb0", 1024, F32), ("yb1", 1024, F32),
        ("ylnb", 1024, F32), ("ylnb2", 1024, BF16),
        ("lnst2", 32, F32), ("lgt", 8, F32),
    ]:
        sz = cols * (2 if dt == BF16 else 4)
        sb(nm, (128, cols), dt, o)
        o += (sz + 31) & ~31
    assert o <= 163584, ("C", o)
    sb("y_lnT", (128, 8 * TPC), BF16, 163584)
    sb("wo", (64, 16 * 1024), BF16, 106496)
    sb("actT", (128, 16 * 1024), BF16, M0)
    for i in range(RING):
        sb(f"ring{i}", (128, 4096), BF16, M0 + 32768 + i * 8192)
    assert M0 + 32768 + RING * 8192 <= 163584

    ctx = ExitStack()
    p = P(nc, ctx)
    sv = p.sem("sv"); ss = p.sem("ss"); st = p.sem("st")
    sp = p.sem("sp")
    dC = p.sem("dC")
    dW = p.sem("dW")
    dX = [p.sem(f"dX{i}") for i in range(2)]
    dT = [p.sem(f"dT{i}") for i in range(2)]
    dKr = [p.sem(f"dKr{i}") for i in range(2)]
    dRp = [p.sem(f"dRp{i}") for i in range(2)]
    dQr = [p.sem(f"dQr{i}") for i in range(2)]
    dSt = [p.sem(f"dSt{i}") for i in range(2)]
    dK = [p.sem(f"dK{i}") for i in range(2)]
    dQ = [p.sem(f"dQ{i}") for i in range(2)]
    dV = [p.sem(f"dV{i}") for i in range(2)]
    dM = [p.sem(f"dM{i}") for i in range(2)]
    dMr = p.sem("dMr")
    dXQ = [p.sem(f"dXQ{i}") for i in range(2)]
    dYs = [p.sem(f"dYs{i}") for i in range(2)]
    dR = [p.sem(f"dR{i}") for i in range(RING)]
    dCw = p.sem("dCw")
    dYt = [p.sem(f"dYt{i}") for i in range(2)]
    dOut = [p.sem(f"dOut{i}") for i in range(2)]
    p._bar = p.sem("bar")
    A = S

    def finish():
        for e in ENGINES:
            p.op(e, "nop", waits=[p.now(s) for s in p.sems
                                  if s is not p._bar])
        p.emit()
        ctx.close()
        return nc

    p.dma("sync", A["c_eye_bf"][:, :], d["eye_bf"][:, :], inc=dC)
    p.dma("sync", A["c_eye_f"][:, :], d["eye_f32"][:, :], inc=dC)
    p.dma("sync", A["c_ones_row"][:, :], d["ones_row_f"][:, :], inc=dC)
    p.dma("sync", A["c_ones_col"][:, :], d["ones_col_f"][:, :], inc=dC)
    p.dma("sync", A["c_g"][:, :], d["g_kv"][:, :], inc=dC)
    consts_tok = p.now(dC)

    def wload(dst, src, inc=dW):
        kt = src.shape[0] // 128
        sr = src.rearrange("(k p) m -> p k m", p=128)
        ds = dst[:, :].rearrange("p (k m) -> p k m", k=kt)
        return p.dma("gpsimd", ds, sr, inc=inc)

    wload(A["wkv"], d["w_kv"])
    wload(A["wkr"], d["w_kr"])
    wload(A["wkr_sw"], d["w_kr_sw"])
    wload(A["wq"], d["w_q"])
    wload(A["wqr"], d["w_qr"])
    wload(A["wqr_sw"], d["w_qr_sw"])
    wA_tok = p.now(dW)
    wload(A["wkdec"], d["w_kdec"])
    wload(A["wvdec"], d["w_vdec"])
    wload(A["wqdec"], d["w_qdec"])
    wA2_tok = p.now(dW)

    with ExitStack() as pctx:
        psT = [pctx.enter_context(nc.psum_tensor(f"psT{i}", [128, 1024], BF16))
               for i in range(2)]
        psK = [pctx.enter_context(nc.psum_tensor(f"psK{i}", [128, 512], F32))
               for i in range(2)]
        psR = [pctx.enter_context(nc.psum_tensor(f"psR{i}", [128, 512], F32))
               for i in range(4)]

        xfree = [None, None]
        xTfree = [None, None]
        tabfree = [None, None]
        krofree = [None, None]
        ropet1_free = [None, None]
        psT_rd = [None, None]
        psK_rd = [None, None]
        psR_rd = [None, None, None, None]
        xln_free = None
        xsq_free = None
        setfree = [None] * 4

        def sweep(nchunks, x_src, cos_src, sin_src, qside):
            nonlocal xln_free, xsq_free
            for c in range(nchunks):
                par = c % 2
                xb = A[f"xbuf{par}"]
                xs = x_src[c * 256:(c + 1) * 256].rearrange(
                    "(t p) d -> p t d", p=128)
                tx = p.dma("sync",
                           xb[:, :].rearrange("p (t d) -> p t d", t=2),
                           xs, waits=[xfree[par]], inc=dX[par])
                tc = p.dma("sync",
                           A[f"tabc{par}"][:, :].rearrange(
                               "p (k t) -> p k t", k=4),
                           cos_src[:, c * 256:(c + 1) * 256].rearrange(
                               "(k p) t -> p k t", p=128),
                           waits=[tabfree[par]], inc=dT[par])
                tsn = p.dma("sync",
                            A[f"tabs{par}"][:, :].rearrange(
                                "p (k t) -> p k t", k=4),
                            sin_src[:, c * 256:(c + 1) * 256].rearrange(
                                "(k p) t -> p k t", p=128),
                            waits=[tabfree[par]], inc=dT[par])
                ln_toks = []
                for t in range(2):
                    lb = ((c * 2 + t) % 4) * 8
                    xt = xb[:, t * 1024:(t + 1) * 1024]
                    t_r0 = p.op("vector", "reduce_sum", A["lnst"][:, lb + 0:lb + 1],
                                xt, AX.X, waits=[tx], inc=sv)
                    tq = p.op("scalar", "activation", A["xsq"][:, :], xt,
                              AF.Square, waits=[tx, xsq_free], inc=ss)
                    xsq_free = p.op("vector", "reduce_sum", A["lnst"][:, lb + 1:lb + 2],
                                    A["xsq"][:, :], AX.X, waits=[tq], inc=sv)
                    t_ms = p.op("gpsimd", "tensor_scalar", A["lnst"][:, lb + 2:lb + 4],
                                A["lnst"][:, lb + 0:lb + 2], 1.0 / D, None, ALU.mult,
                                waits=[xsq_free, setfree[(c * 2 + t) % 4]],
                                inc=sp)
                    p.op("gpsimd", "tensor_tensor", A["lnst"][:, lb + 4:lb + 5],
                         A["lnst"][:, lb + 2:lb + 3], A["lnst"][:, lb + 2:lb + 3], ALU.mult)
                    t_ve = p.op("gpsimd", "tensor_scalar", A["lnst"][:, lb + 5:lb + 6],
                                A["lnst"][:, lb + 3:lb + 4], A["lnst"][:, lb + 4:lb + 5], EPS_LN,
                                ALU.subtract, ALU.add, inc=sp)
                    t_rs = p.op("scalar", "activation", A["lnst"][:, lb + 6:lb + 7],
                                A["lnst"][:, lb + 5:lb + 6], AF.Sqrt,
                                waits=[t_ve], inc=ss)
                    t_rc = p.op("vector", "reciprocal", A["lnst"][:, lb + 7:lb + 8],
                                A["lnst"][:, lb + 6:lb + 7], waits=[t_rs], inc=sv)
                    t_cp = p.op("gpsimd", "tensor_copy", A["lnst"][:, lb + 5:lb + 6],
                                A["lnst"][:, lb + 7:lb + 8], waits=[t_rc], inc=sp)
                    tok = p.op("vector", "tensor_scalar",
                               A["xln"][:, t * 1024:(t + 1) * 1024], xt,
                               A["lnst"][:, lb + 2:lb + 3], A["lnst"][:, lb + 5:lb + 6],
                               ALU.subtract, ALU.mult,
                               waits=[xln_free, t_cp, t_ms], inc=sv)
                    setfree[(c * 2 + t) % 4] = tok
                    ln_toks.append(tok)
                xfree[par] = ln_toks[-1]
                xT = A[f"xlnT{par}"]
                last_tp = None
                for t in range(2):
                    for dkk in range(8):
                        tt = p.op("tensor", "transpose",
                                  psT[t][:, dkk * 128:dkk * 128 + 128],
                                  A["xln"][:, t * 1024 + dkk * 128:
                                           t * 1024 + dkk * 128 + 128],
                                  A["c_eye_bf"][:, :],
                                  waits=[ln_toks[t], consts_tok, psT_rd[t],
                                         xTfree[par]],
                                  inc=(st if dkk == 7 else None))
                    ev = p.op("vector", "tensor_copy",
                              bass.AP(A[f"xlnT{par}"], t * 128,
                                      [[2048, 128], [256, 8], [1, 128]]),
                              psT[t][:, :].rearrange(
                                  "p (k q) -> p k q", k=8),
                              waits=[tt], inc=sv)
                    psT_rd[t] = ev
                    last_tp = tt
                xln_free = last_tp
                xT_toks = [psT_rd[0], psT_rd[1]]
                kv_toks = []
                for m in range(2):
                    for dk in range(8):
                        tok = p.op("tensor", "matmul", psK[m][:, 0:256],
                                   A["wkv" if not qside else "wq"][
                                       :, dk * (256 if not qside else 512)
                                       + m * 128:
                                       dk * (256 if not qside else 512)
                                       + m * 128 + 128],
                                   xT[:, dk * 256:dk * 256 + 256],
                                   start=(dk == 0), stop=(dk == 7),
                                   waits=xT_toks + [wA_tok, psK_rd[m]],
                                   inc=(st if dk == 7 else None))
                    kv_toks.append(tok)
                if not qside:
                    rko = (c % 2) * 256
                    sq_toks = [
                        p.op("scalar", "activation",
                             A["sqv"][:, m * 256:m * 256 + 256],
                             psK[m][:, 0:256], AF.Square,
                             waits=[kv_toks[m]], inc=ss)
                        for m in range(2)]
                    for m in range(2):
                        t_om = p.op("tensor", "matmul", psK[0][0:1, 256:512],
                                    A["c_ones_col"][:, :],
                                    A["sqv"][:, m * 256:m * 256 + 256],
                                    start=(m == 0), stop=(m == 1),
                                    waits=[sq_toks[m], consts_tok], inc=st)
                    t_pre = p.op("vector", "tensor_scalar", A["rkv"][0:1, rko:rko + 256],
                                 psK[0][0:1, 256:512], 1.0 / KVR, EPS_RMS,
                                 ALU.mult, ALU.add, waits=[t_om], inc=sv)
                    t_sq = p.op("scalar", "activation", A["rkv"][0:1, rko:rko + 256],
                                A["rkv"][0:1, rko:rko + 256], AF.Sqrt,
                                waits=[t_pre], inc=ss)
                    t_rc = p.op("vector", "reciprocal", A["rkv"][0:1, rko:rko + 256],
                                A["rkv"][0:1, rko:rko + 256], waits=[t_sq], inc=sv)
                    t_bc = p.op("tensor", "matmul", psK[1][:, 256:512],
                                A["c_ones_row"][:, :], A["rkv"][0:1, rko:rko + 256],
                                start=True, stop=True, waits=[t_rc], inc=st)
                    t_ev = p.op("scalar", "copy", A["rkvB"][:, rko:rko + 256],
                                psK[1][:, 256:512], waits=[t_bc], inc=ss)
                    for m in range(2):
                        tok = p.op("vector", "scalar_tensor_tensor",
                                   A["kv_normT"][:, m * T + c * 256:
                                                 m * T + c * 256 + 256],
                                   psK[m][:, 0:256], A["c_g"][:, m:m + 1],
                                   A["rkvB"][:, rko:rko + 256], ALU.mult, ALU.mult,
                                   waits=[t_ev], inc=sv)
                        psK_rd[m] = tok
                    qc_tok = None
                else:
                    for m in range(2):
                        tok = p.op("vector", "tensor_copy",
                                   A["q_cT"][:, m * TPC + c * 256:
                                             m * TPC + c * 256 + 256],
                                   psK[m][:, 0:256],
                                   waits=[kv_toks[m]], inc=sv)
                        psK_rd[m] = tok
                    for m in range(2, 4):
                        mm = m - 2
                        for dk in range(8):
                            tok = p.op("tensor", "matmul", psK[mm][:, 0:256],
                                       A["wq"][:, dk * 512 + m * 128:
                                               dk * 512 + m * 128 + 128],
                                       xT[:, dk * 256:dk * 256 + 256],
                                       start=(dk == 0), stop=(dk == 7),
                                       waits=[psK_rd[mm]],
                                       inc=(st if dk == 7 else None))
                        tok = p.op("vector", "tensor_copy",
                                   A["q_cT"][:, m * TPC + c * 256:
                                             m * TPC + c * 256 + 256],
                                   psK[mm][:, 0:256], waits=[tok], inc=sv)
                        psK_rd[mm] = tok
                    qc_tok = tok
                par4, slot = (c // 4) % 2, c % 4
                kro = A[f"kro{par4}"]
                rope_evs = []
                for m in range(4):
                    pr = m % 2
                    ro = pr * 512
                    bs_, bw_ = psR[pr * 2], psR[pr * 2 + 1]
                    if not qside:
                        wst, wsw, nk = A["wkr"], A["wkr_sw"], 8

                        def rr(dk):
                            return xT[:, dk * 256:dk * 256 + 256]
                        rw = xT_toks
                    else:
                        wst, wsw, nk = A["wqr"], A["wqr_sw"], 4

                        def rr(dk):
                            return A["q_cT"][:, dk * TPC + c * 256:
                                             dk * TPC + c * 256 + 256]
                        rw = [qc_tok]
                    for dk in range(nk):
                        t_s = p.op("tensor", "matmul", bs_[:, 0:256],
                                   wst[:, dk * 512 + m * 128:
                                       dk * 512 + m * 128 + 128], rr(dk),
                                   start=(dk == 0), stop=(dk == nk - 1),
                                   waits=rw + [psR_rd[pr * 2]],
                                   inc=(st if dk == nk - 1 else None))
                    for dk in range(nk):
                        t_w = p.op("tensor", "matmul", bw_[:, 0:256],
                                   wsw[:, dk * 512 + m * 128:
                                       dk * 512 + m * 128 + 128], rr(dk),
                                   start=(dk == 0), stop=(dk == nk - 1),
                                   waits=[psR_rd[pr * 2 + 1]],
                                   inc=(st if dk == nk - 1 else None))
                    ct = A[f"tabc{par}"][:, m * 256:m * 256 + 256]
                    st_ = A[f"tabs{par}"][:, m * 256:m * 256 + 256]
                    t1 = p.op("vector", "tensor_tensor",
                              A["ropet1"][:, ro:ro + 256], bs_[:, 0:256], ct,
                              ALU.mult,
                              waits=[t_s, tc, ropet1_free[pr]], inc=sv)
                    psR_rd[pr * 2] = t1
                    t2 = p.op("vector", "tensor_tensor",
                              A["ropet1"][:, ro + 256:ro + 512],
                              bw_[:, 0:256], st_, ALU.mult,
                              waits=[t_w, tsn, ropet1_free[pr]], inc=sv)
                    psR_rd[pr * 2 + 1] = t2
                    dst = (kro[:, m * 1024 + slot * 256:
                               m * 1024 + slot * 256 + 256] if not qside else
                           A["q_ropeT"][:, m * TPC + c * 256:
                                        m * TPC + c * 256 + 256])
                    t3 = p.op("gpsimd", "tensor_add", dst,
                              A["ropet1"][:, ro:ro + 256],
                              A["ropet1"][:, ro + 256:ro + 512],
                              waits=[t1, t2,
                                     krofree[par4] if (not qside and slot == 0
                                                       and m == 0) else None],
                              inc=sp)
                    ropet1_free[pr] = t3
                    rope_evs.append(t3)
                tabfree[par] = rope_evs[-1]
                xTfree[par] = rope_evs[-1]
                if not qside and slot == 3:
                    kd = kro_d[:, (c - 3) * 256:(c + 1) * 256].rearrange(
                        "(m p) t -> p m t", p=128)
                    krofree[par4] = p.dma(
                        "scalar" if (c // 4) % 2 == 0 else "gpsimd", kd,
                        kro[:, :].rearrange("p (m t) -> p m t", m=4),
                        waits=[rope_evs[-1]], inc=dKr[par4])

        sweep(NKC, d["x_b"], d["kcos"], d["kssin"], qside=False)
        sweep(NQC, d["x_q"], d["qcos"], d["qssin"], qside=True)

    if dev == 1:
        dd = nc.dram_tensor("dbg_kvnorm", [128, 2 * T], BF16,
                            kind="ExternalOutput")
        p.dma("sync", dd[:, :], A["kv_normT"][:, :],
              waits=[p.now(sv), p.now(st), p.now(ss), p.now(dKr[0]),
                     p.now(dKr[1])], inc=dC)
        dd2 = nc.dram_tensor("dbg_qc", [128, 4 * TPC], BF16,
                             kind="ExternalOutput")
        p.dma("sync", dd2[:, :], A["q_cT"][:, :], waits=[p.now(sv)], inc=dC)
        dd3 = nc.dram_tensor("dbg_qrope", [128, 4 * TPC], BF16,
                             kind="ExternalOutput")
        p.dma("sync", dd3[:, :], A["q_ropeT"][:, :], waits=[p.now(sv)],
              inc=dC)
        dd4 = nc.dram_tensor("dbg_lnst", [128, 32], F32, kind="ExternalOutput")
        p.dma("sync", dd4[:, :], A["lnst"][:, :],
              waits=[p.now(sv), p.now(sp), p.now(ss)], inc=dC)
        dd5 = nc.dram_tensor("dbg_xsq", [128, 1024], F32,
                             kind="ExternalOutput")
        p.dma("sync", dd5[:, :], A["xsq"][:, :], waits=[p.now(ss)], inc=dC)
        dd6 = nc.dram_tensor("dbg_xbuf", [128, 2048], F32,
                             kind="ExternalOutput")
        p.dma("sync", dd6[:, :], A["xbuf1"][:, :], waits=[p.now(sv)], inc=dC)
        return finish()

    p.barrier()

    for i in range(2):
        p.op("vector", "memset", A[f"KTb{i}"][64:128, :], 0.0, inc=sv)
        p.op("vector", "memset", A[f"QTb{i}"][64:128, :], 0.0, inc=sv)
        for base in (64, 132):
            p.op("vector", "memset",
                 bass.AP(A[f"Vb{i}"], base, [[64 * 136 + 32, 128], [136, 64],
                                             [1, 4]]),
                 1.0, inc=sv)
    init_tok = p.now(sv)
    ones_toks = [p.dma("sync", A[f"KTb{i}"][96:97, :], d["ones_8k"][:, :],
                       waits=[init_tok], inc=dC) for i in range(2)]
    ones_tok = ones_toks[-1]

    with ExitStack() as pctx:
        psKC = [pctx.enter_context(nc.psum_tensor(f"psKC{i}", [128, 512], F32))
                for i in range(2)]
        psVD = [pctx.enter_context(nc.psum_tensor(f"psVD{i}", [128, 512], F32))
                for i in range(2)]
        psQC = [pctx.enter_context(nc.psum_tensor(f"psQC{i}", [128, 512], F32))
                for i in range(2)]
        kc_rd = [None, None]; vd_rd = [None, None]; qc_rd = [None, None]
        buf_free = [None, None]
        vbuf_free = [None, None]

        for h in range(NH):
            par = h % 2
            KTb, QTb = A[f"KTb{par}"], A[f"QTb{par}"]
            ev = None
            for n in range(16):
                pp = n % 2
                for kk in range(2):
                    t_mm = p.op("tensor", "matmul", psKC[pp][0:64, 0:512],
                                A["wkdec"][:, kk * D + h * 64:
                                           kk * D + h * 64 + 64],
                                A["kv_normT"][:, kk * T + n * 512:
                                              kk * T + n * 512 + 512],
                                start=(kk == 0), stop=(kk == 1),
                                waits=[wA2_tok, kc_rd[pp], buf_free[par],
                                       ones_tok],
                                inc=(st if kk else None))
                ev = p.op("vector", "tensor_copy",
                          KTb[0:64, n * 512:n * 512 + 512],
                          psKC[pp][0:64, 0:512], waits=[t_mm], inc=sv)
                kc_rd[pp] = ev
            t_rope = p.dma("sync", KTb[64:96, :],
                           kro_d[h * 32:h * 32 + 32, :],
                           waits=[buf_free[par]], inc=dRp[par])
            t_store = p.dma("scalar", kall_d[h], KTb[:, :],
                            waits=[ev, t_rope], inc=dSt[par])
            if h % 2 == 0:
                vpar = (h // 2) % 2
                pr = h // 2
                Vb = A[f"Vb{vpar}"]
                vev = None
                for ng in range(32):
                    pp = ng % 2
                    for sub in range(2):
                        n = ng * 2 + sub
                        for kk in range(2):
                            t_mm = p.op("tensor", "matmul",
                                        psVD[pp][:, sub * 128:sub * 128 + 128],
                                        A["kv_normT"][:, kk * T + n * 128:
                                                      kk * T + n * 128 + 128],
                                        A["wvdec"][:, kk * D + pr * 128:
                                                   kk * D + pr * 128 + 128],
                                        start=(kk == 0), stop=(kk == 1),
                                        waits=[vd_rd[pp], vbuf_free[vpar],
                                               init_tok],
                                        inc=(st if (kk and sub) else None))
                    vev = p.op("vector", "tensor_copy",
                               bass.AP(Vb, ng * 272,
                                       [[64 * 136 + 32, 128], [136, 2],
                                        [68, 2], [1, 64]]),
                               bass.AP(psVD[pp], 0,
                                       [[512, 128], [128, 2], [64, 2],
                                        [1, 64]]),
                               waits=[t_mm], inc=sv)
                    vd_rd[pp] = vev
                vbuf_free[vpar] = p.dma("scalar", vall_d[pr],
                                        Vb[:, 0:64 * 136],
                                        waits=[vev], inc=dSt[par])
            qev = None
            for n in range(4):
                pp = n % 2
                for kq in range(4):
                    t_mm = p.op("tensor", "matmul", psQC[pp][0:64, 0:512],
                                A["wqdec"][:, kq * D + h * 64:
                                           kq * D + h * 64 + 64],
                                A["q_cT"][:, kq * TPC + n * 512:
                                          kq * TPC + n * 512 + 512],
                                start=(kq == 0), stop=(kq == 3),
                                waits=[qc_rd[pp], buf_free[par]],
                                inc=(st if kq == 3 else None))
                qev = p.op("vector", "tensor_scalar",
                           QTb[0:64, n * 512:n * 512 + 512],
                           psQC[pp][0:64, 0:512], SCALE, None, ALU.mult,
                           waits=[t_mm], inc=sv)
                qc_rd[pp] = qev
            t_qrope = p.dma("gpsimd", QTb[64:96, :],
                            A["q_ropeT"][32 * (h % 4):32 * (h % 4) + 32,
                                         (h // 4) * TPC:(h // 4 + 1) * TPC],
                            waits=[buf_free[par], init_tok], inc=dQr[par])
            buf_free[par] = p.dma("scalar", qall_d[h], QTb[:, :],
                                  waits=[qev, t_qrope], inc=dSt[par])

    if dev == 2:
        return finish()

    p.barrier()


    with ExitStack() as pctx:
        psW = [pctx.enter_context(nc.psum_tensor(f"psW{i}", [128, 1024], F32))
               for i in range(2)]
        psP = [pctx.enter_context(nc.psum_tensor(f"psP{i}", [128, 512], F32))
               for i in range(2)]
        psV = pctx.enter_context(nc.psum_tensor("psV", [128, 512], F32))
        psTb = pctx.enter_context(nc.psum_tensor("psTb", [128, 1024], BF16))

        kt_load = [None, None]; v_load = [None, None]; qt_load = [None, None]
        kt_free = [None, None]; v_free = [None, None]; qt_free = [None, None]
        psW_rd = [None, None]; psP_rd = [None, None]
        psV_rd = [None, None]
        probs_free = [None, None]
        mask_free = [None, None]
        masktmp_free = None
        psTb_rd = None
        ones_init = None
        rlB_free = None
        mrow_busy = [None]

        vpad_init = [p.op("vector", "memset", A[f"vB{i}"][:, 8704:8736],
                          0.0, inc=sv) for i in range(2)]

        def load_head(h):
            par = h % 2
            kt_load[par] = p.dma("sync", A[f"ktB{par}"][:, :], kall_d[h],
                                 waits=[kt_free[par]], inc=dK[par])
            qt_load[par] = p.dma("sync", A[f"qtB{par}"][:, :], qall_d[h],
                                 waits=[qt_free[par]], inc=dQ[par])
            if h % 2 == 0:
                vp = (h // 2) % 2
                v_load[vp] = p.dma("sync", A[f"vB{vp}"][:, 0:8704],
                                   vall_d[h // 2],
                                   waits=[v_free[vp], vpad_init[vp]],
                                   inc=dV[vp])

        def pass1(h):
            nonlocal psTb_rd
            par = h % 2
            ktB, qtB = A[f"ktB{par}"], A[f"qtB{par}"]
            red_toks = []
            pp = 0
            for j in range(NJ):
                E = _E(j)
                n = E * 4
                ps = psP[pp]
                t_mm = p.op("tensor", "matmul", ps[:, 0:n],
                            qtB[:, j * 128:j * 128 + 128],
                            bass.AP(ktB, 0, [[T, 128], [32, n]]),
                            start=True, stop=True,
                            waits=[kt_load[par], qt_load[par], psP_rd[pp],
                                   psV_rd[1] if pp == 0 else None], inc=st)
                t_r = p.op("vector", "tensor_reduce",
                           A["mcols"][:, j:j + 1], ps[:, 0:n],
                           AX.X, ALU.max, waits=[t_mm], inc=sv)
                psP_rd[pp] = t_r
                pp ^= 1
                red_toks.append(t_r)
            t_n = p.op("gpsimd", "tensor_scalar", A["mneg"][:, :],
                       A["mcols"][:, :], -1.0, -MARGIN, ALU.mult, ALU.add,
                       waits=red_toks, inc=sp)
            t_t = p.op("tensor", "transpose", psTb[0:16, 0:128],
                       A["mneg"][:, :], A["c_eye_bf"][:, :],
                       waits=[t_n, psTb_rd], inc=st)
            t_e = p.op("vector", "tensor_copy", A["mrow"][:, :],
                       psTb[0:16, 0:128], waits=[t_t], inc=sv)
            psTb_rd = t_e
            t_d1 = p.dma("gpsimd",
                         mrow_d[0:1, :].rearrange("p (j q) -> p j q", j=16),
                         A["mrow"][:, :], waits=[t_e, mrow_busy[0]],
                         inc=dMr)
            t_d2 = p.dma("gpsimd", qtB[96:97, :], mrow_d[0:1, :],
                         waits=[t_d1], inc=dMr)
            mrow_busy[0] = t_d2
            return t_d2

        def pass2(h, row_tok):
            nonlocal psV_rd, masktmp_free, ones_init
            par = h % 2
            vp = (h // 2) % 2
            voff = (h % 2) * 68
            ktB, qtB = A[f"ktB{par}"], A[f"qtB{par}"]
            vB = A[f"vB{vp}"]
            if ones_init is None:
                ones_init = p.op("vector", "memset", A["ones64"][:, :], 1.0,
                                 inc=sv)
            bpar = 0
            pend = []
            rl_busy = [None]
            pend_fin = [None]

            def do_finalize(pend_fin):
                if pend_fin[0] is None:
                    return
                fpj, flast = pend_fin[0]
                pend_fin[0] = None
                psv = psV if fpj % 2 == 0 else psP[0]
                t_rl = p.op("vector", "reciprocal", A["rl64"][64:65, 0:256],
                            psv[64:65, 0:256],
                            waits=[flast, ones_init, rl_busy[0]], inc=sv)
                t_bc = p.op("tensor", "matmul", psP[1][0:64, 0:256],
                            A["ones64"][64:65, 0:64],
                            A["rl64"][64:65, 0:256],
                            start=True, stop=True,
                            waits=[t_rl, psP_rd[1]], inc=st)
                rl_busy[0] = t_bc
                t_rb2 = p.op("vector", "tensor_copy", A["rlB"][0:64, 0:256],
                             psP[1][0:64, 0:256], waits=[t_bc], inc=sv)
                psP_rd[1] = t_rb2
                t_o = p.op("vector", "tensor_tensor",
                           A["attn_o64"][0:64, h * TPC + fpj * 256:
                                         h * TPC + fpj * 256 + 256],
                           psv[0:64, 0:256], A["rlB"][0:64, 0:256],
                           ALU.mult, waits=[t_rb2], inc=sv)
                psV_rd[fpj % 2] = t_o
                if fpj % 2 == 1:
                    psP_rd[0] = t_o

            def flush_pv(pend):
                if not pend:
                    return None
                bk_, prb_, t_e_, E0_, vp_, voff_, pvb_, bp_, w_ = pend.pop(0)
                psv = psV if pvb_ == 0 else psP[0]
                lp = None
                for i, kt in enumerate(bk_):
                    lp = p.op(
                        "tensor", "matmul", psv[0:96, 0:w_],
                        A[f"vB{vp_}"][:, kt * 136 + voff_:
                                      kt * 136 + voff_ + 96],
                        prb_[:, i * w_:i * w_ + w_],
                        start=(kt == 0), stop=(kt == E0_ - 1),
                        waits=[t_e_, psV_rd[pvb_] if kt == 0 else None,
                               psP_rd[0] if (pvb_ == 1 and kt == 0) else None,
                               v_load[vp_]],
                        inc=(st if kt == E0_ - 1 else None))
                probs_free[bp_] = lp
                return lp

            for pj in range(NJ // 2):
                E0 = _E(2 * pj)
                mpar = pj % 2
                t_ml = p.dma("sync",
                             A[f"maskb{mpar}"][:, :].rearrange(
                                 "p (k q) -> p k q", k=8),
                             d["maskT"][pj].rearrange("k p q -> p k q"),
                             waits=[mask_free[mpar]], inc=dM[mpar])
                nb = E0 // 4
                last_pv = None
                for bi in range(nb):
                    bk = list(range(bi * 4, bi * 4 + 4))
                    lastb = (bi == nb - 1)
                    w = 128 if lastb else 256
                    ps = psW[bpar]
                    for i, kt in enumerate(bk):
                        tok = p.op("tensor", "matmul",
                                   ps[:, i * w:i * w + w],
                                   ktB[:, kt * 128:kt * 128 + 128],
                                   qtB[:, pj * 256:pj * 256 + w],
                                   start=True, stop=True,
                                   waits=[psW_rd[bpar], row_tok,
                                          kt_load[par]],
                                   inc=(st if i == 3 else None))
                    mm_tok = tok
                    if len(pend) >= 2:
                        fl = flush_pv(pend)
                        if fl is not None:
                            last_pv = fl
                    if bi == 1:
                        do_finalize(pend_fin)
                    prb = A[f"probsT{bpar}"]
                    if bi < nb - 2:
                        t_e = p.op("scalar", "activation", prb[:, :],
                                   ps[:, :], AF.Exp,
                                   waits=[mm_tok, probs_free[bpar]], inc=ss)
                    elif not lastb:
                        t_a = p.op("vector", "tensor_add",
                                   A["masktmp"][:, :], ps[:, :],
                                   A[f"maskb{mpar}"][:, 0:1024],
                                   waits=[mm_tok, t_ml, masktmp_free],
                                   inc=sv)
                        t_e = p.op("scalar", "activation", prb[:, :],
                                   A["masktmp"][:, :], AF.Exp,
                                   waits=[t_a, probs_free[bpar]], inc=ss)
                        masktmp_free = t_e
                    else:
                        t_a = p.op("vector", "tensor_add",
                                   bass.AP(A["masktmp"], 0,
                                           [[1024, 128], [128, 4], [1, 128]]),
                                   bass.AP(ps, 0,
                                           [[1024, 128], [128, 4], [1, 128]]),
                                   bass.AP(A[f"maskb{mpar}"], 1024,
                                           [[2048, 128], [256, 4], [1, 128]]),
                                   waits=[mm_tok, t_ml, masktmp_free],
                                   inc=sv)
                        t_e = p.op("scalar", "activation", prb[:, 0:512],
                                   A["masktmp"][:, 0:512], AF.Exp,
                                   waits=[t_a, probs_free[bpar]], inc=ss)
                        masktmp_free = t_e
                        mask_free[mpar] = t_e
                    psW_rd[bpar] = t_e
                    pend.append((bk, prb, t_e, E0, vp, voff, pj % 2, bpar, w))
                    bpar ^= 1
                while pend:
                    fl = flush_pv(pend)
                    if fl is not None:
                        last_pv = fl
                pend_fin[0] = (pj, last_pv)
                if pj == NJ // 2 - 1:
                    do_finalize(pend_fin)
            kt_free[par] = p.now(st)
            qt_free[par] = p.now(st)
            if h % 2 == 1:
                v_free[vp] = p.now(st)

        horder = list(range(NH))
        if dev == 31:
            horder = [2, 3, 0, 1] + list(range(4, NH))
        load_head(horder[0])
        load_head(horder[1])
        for hi in range(NH):
            h = horder[hi]
            rt = pass1(h)
            pass2(h, rt)
            if hi + 2 < NH:
                load_head(horder[hi + 2])

    if dev in (3, 31):
        dd = nc.dram_tensor("dbg_ao", [64, NH * TPC], BF16,
                            kind="ExternalOutput")
        p.dma("sync", dd[:, :], A["attn_o64"][0:64, :],
              waits=[p.now(sv), p.now(st), p.now(ss)], inc=dC)
        return finish()

    p.barrier()

    p.dma("gpsimd", A["wo"][:, :].rearrange("p (h n) -> p h n", h=16),
          d["w_o"].rearrange("(h p) n -> p h n", p=64), inc=dW)
    wo_tok = p.now(dW)
    with ExitStack() as pctx:
        psY = [pctx.enter_context(nc.psum_tensor(f"psY{i}", [128, 1024], F32))
               for i in range(2)]
        psT2 = [pctx.enter_context(nc.psum_tensor(f"psT2{i}", [128, 1024],
                                                  BF16)) for i in range(2)]
        psRt = pctx.enter_context(nc.psum_tensor("psRt", [128, 512], F32))
        psCw = pctx.enter_context(nc.psum_tensor("psCw", [128, 512], F32))

        xq_free = [None, None]; psY_rd = [None, None]
        psT2_rd = [None, None]; ys_free = [None, None]
        setfree2 = [None] * 4
        for t in range(NJ):
            par = t % 2
            t_x = p.dma("sync", A[f"xq{par}"][:, :],
                        d["x_q"][t * 128:(t + 1) * 128, :],
                        waits=[xq_free[par]], inc=dXQ[par])
            for half in range(2):
                for hh in range(NH):
                    t_mm = p.op("tensor", "matmul",
                                psY[par][:, half * 512:half * 512 + 512],
                                A["attn_o64"][0:64, hh * TPC + t * 128:
                                              hh * TPC + t * 128 + 128],
                                A["wo"][0:64, hh * 1024 + half * 512:
                                        hh * 1024 + half * 512 + 512],
                                start=(hh == 0), stop=(hh == NH - 1),
                                waits=[wo_tok, psY_rd[par]],
                                inc=(st if hh == NH - 1 else None))
            yb = A[f"yb{par}"]
            t_y = p.op("vector", "tensor_add", yb[:, :], psY[par][:, :],
                       A[f"xq{par}"][:, :],
                       waits=[t_mm, t_x, ys_free[par]], inc=sv)
            psY_rd[par] = t_y
            lb2 = (t % 4) * 8
            t_r0 = p.op("scalar", "activation", A["ylnb"][:, :], yb[:, :],
                        AF.Copy, accum_out=A["lnst2"][:, lb2 + 0:lb2 + 1],
                        waits=[t_y, setfree2[t % 4]], inc=ss)
            t_sq = p.op("scalar", "activation", A[f"xq{par}"][:, :], yb[:, :],
                        AF.Square, accum_out=A["lnst2"][:, lb2 + 1:lb2 + 2],
                        waits=[t_y, t_r0], inc=ss)
            xq_free[par] = t_sq
            t_r1 = t_sq
            t_ms = p.op("gpsimd", "tensor_scalar",
                        A["lnst2"][:, lb2 + 2:lb2 + 4],
                        A["lnst2"][:, lb2 + 0:lb2 + 2], 1.0 / D, None,
                        ALU.mult, waits=[t_r0, t_r1, setfree2[t % 4]],
                        inc=sp)
            p.op("gpsimd", "tensor_tensor", A["lnst2"][:, lb2 + 4:lb2 + 5],
                 A["lnst2"][:, lb2 + 2:lb2 + 3], A["lnst2"][:, lb2 + 2:lb2 + 3], ALU.mult)
            t_ve = p.op("gpsimd", "tensor_scalar", A["lnst2"][:, lb2 + 5:lb2 + 6],
                        A["lnst2"][:, lb2 + 3:lb2 + 4], A["lnst2"][:, lb2 + 4:lb2 + 5], EPS_LN,
                        ALU.subtract, ALU.add, inc=sp)
            t_s2 = p.op("scalar", "activation", A["lnst2"][:, lb2 + 6:lb2 + 7],
                        A["lnst2"][:, lb2 + 5:lb2 + 6], AF.Sqrt,
                        waits=[t_ve], inc=ss)
            t_rc2 = p.op("vector", "reciprocal", A["lnst2"][:, lb2 + 7:lb2 + 8],
                         A["lnst2"][:, lb2 + 6:lb2 + 7], waits=[t_s2], inc=sv)
            t_cp2 = p.op("gpsimd", "tensor_copy", A["lnst2"][:, lb2 + 5:lb2 + 6],
                         A["lnst2"][:, lb2 + 7:lb2 + 8], waits=[t_rc2], inc=sp)
            t_ln = p.op("vector", "tensor_scalar", A["ylnb2"][:, :],
                        A["ylnb"][:, :],
                        A["lnst2"][:, lb2 + 2:lb2 + 3], A["lnst2"][:, lb2 + 5:lb2 + 6],
                        ALU.subtract, ALU.mult,
                        waits=[t_cp2, t_ms, t_r0], inc=sv)
            setfree2[t % 4] = t_ln
            t_ys = p.dma("scalar", ys_d[t * 128:(t + 1) * 128, :], yb[:, :],
                         waits=[t_y, t_ln], inc=dYs[par])
            ys_free[par] = t_ys
            tp = t % 2
            for dk in range(8):
                t_t = p.op("tensor", "transpose",
                           psT2[tp][:, dk * 128:dk * 128 + 128],
                           A["ylnb2"][:, dk * 128:dk * 128 + 128],
                           A["c_eye_bf"][:, :],
                           waits=[t_ln, psT2_rd[tp]],
                           inc=(st if dk == 7 else None))
            t_e = p.op("vector", "tensor_copy",
                       bass.AP(A["y_lnT"], t * 128,
                               [[8 * TPC, 128], [TPC, 8], [1, 128]]),
                       psT2[tp][:, :].rearrange("p (k q) -> p k q", k=8),
                       waits=[t_t], inc=sv)
            psT2_rd[tp] = t_e
        ylnT_tok = p.now(sv)
        t_rb = p.dma("sync", A["lgt"][0:1, 0:8], d["rb"][:, :], inc=dC)
        t_rbb = p.op("tensor", "matmul", psRt[:, 128:136],
                     A["c_ones_row"][:, :], A["lgt"][0:1, 0:8],
                     start=True, stop=True, waits=[t_rb, consts_tok], inc=st)
        t_rbe = p.op("scalar", "copy", A["rbB"][:, :],
                     psRt[:, 128:136], waits=[t_rbb], inc=ss)
        t_wr = p.dma("gpsimd",
                     A["wrt"][:, :].rearrange("p (k e) -> p k e", k=8),
                     d["wr"].rearrange("(k p) e -> p k e", p=128), inc=dW)
        prev_l = None
        sig_toks = []
        for t in range(NJ):
            for dk in range(8):
                t_mm = p.op("tensor", "matmul", psRt[:, 0:8],
                            A["y_lnT"][:, dk * TPC + t * 128:
                                       dk * TPC + t * 128 + 128],
                            A["wrt"][:, dk * 8:dk * 8 + 8],
                            start=(dk == 0), stop=(dk == 7),
                            waits=[ylnT_tok, t_wr, prev_l, t_rbe],
                            inc=(st if dk == 7 else None))
            t_l = p.op("vector", "tensor_add", A["lgt"][:, 0:8],
                       psRt[:, 0:8], A["rbB"][:, :],
                       waits=[t_mm, t_rbe,
                              sig_toks[-1] if sig_toks else None], inc=sv)
            prev_l = t_l
            t_sg = p.op("scalar", "activation",
                        A["cwbuf"][:, t * 8:t * 8 + 8], A["lgt"][:, 0:8],
                        AF.Sigmoid, waits=[t_l], inc=ss)
            sig_toks.append(t_sg)

        def sl_e(e):
            return bass.AP(A["cwbuf"], e, [[128, 128], [8, 16]])

        sig_tok = p.now(ss)
        probs3 = bass.AP(A["cwbuf"], 0, [[128, 128], [8, 16], [1, 8]])
        t_m1 = p.op("vector", "tensor_reduce", A["r_m1"][:, :], probs3,
                    AX.X, ALU.max, waits=[sig_tok], inc=sv)
        t_m1c = p.op("scalar", "copy", A["r_m2"][:, :], A["r_m1"][:, :],
                     waits=[t_m1], inc=ss)
        t_ge = None
        for t in range(NJ):
            t_ge = p.op("vector", "tensor_scalar",
                        A["r_ge"][:, t * 1:t + 1] if False else
                        A["cw"][:, t * 8:t * 8 + 8],
                        A["cwbuf"][:, t * 8:t * 8 + 8],
                        A["r_m2"][:, t:t + 1], None, ALU.is_ge,
                        waits=[t_m1c], inc=sv)
        t_gec = p.op("scalar", "copy", A["cwT"][:, :], A["cw"][:, :],
                     waits=[t_ge], inc=ss)
        t_msk = p.op("vector", "scalar_tensor_tensor", A["cw"][:, :],
                     A["cwT"][:, :], -1e30, A["cwbuf"][:, :],
                     ALU.mult, ALU.add, waits=[t_gec], inc=sv)
        t_mskc = p.op("scalar", "copy", A["cwT"][:, :], A["cw"][:, :],
                      waits=[t_msk], inc=ss)
        masked3 = bass.AP(A["cwT"], 0, [[128, 128], [8, 16], [1, 8]])
        t_m2 = p.op("vector", "tensor_reduce", A["r_m2"][:, :], masked3,
                    AX.X, ALU.max, waits=[t_mskc], inc=sv)
        t_m2c = p.op("scalar", "copy", A["r_ge"][:, :], A["r_m2"][:, :],
                     waits=[t_m2], inc=ss)
        t_ge2 = None
        for t in range(NJ):
            t_ge2 = p.op("vector", "tensor_scalar",
                         A["cw"][:, t * 8:t * 8 + 8],
                         A["cwbuf"][:, t * 8:t * 8 + 8],
                         A["r_ge"][:, t:t + 1], None, ALU.is_ge,
                         waits=[t_m2c], inc=sv)
        t_gc2 = p.op("scalar", "copy", A["cwT"][:, :], A["cw"][:, :],
                     waits=[t_ge2], inc=ss)
        t_cw = p.op("vector", "tensor_mul", A["cw"][:, :], A["cwT"][:, :],
                    A["cwbuf"][:, :], waits=[t_gc2], inc=sv)
        cw_toks = [t_cw]
        t_ct = p.op("tensor", "transpose", psCw[:, 0:128], A["cw"][:, :],
                    A["c_eye_f"][:, :], waits=cw_toks + [consts_tok], inc=st)
        t_ce = p.op("vector", "tensor_copy", A["cwT"][:, :], psCw[:, 0:128],
                    waits=[t_ct], inc=sv)
        cwT_dma = p.dma("scalar", cwT_d[:, :], A["cwT"][:, :],
                        waits=[t_ce], inc=dYs[0])

    if dev == 4:
        dd = nc.dram_tensor("dbg_ylnT", [128, 8 * TPC], BF16,
                            kind="ExternalOutput")
        p.dma("sync", dd[:, :], A["y_lnT"][:, :],
              waits=[p.now(sv), p.now(ss), p.now(st), p.now(dYs[0]),
                     p.now(dYs[1])], inc=dC)
        dd2 = nc.dram_tensor("dbg_cw", [128, 128], F32, kind="ExternalOutput")
        p.dma("sync", dd2[:, :], A["cw"][:, :], waits=[p.now(sv)], inc=dC)
        return finish()

    p.barrier()

    with ExitStack() as pctx:
        psG = pctx.enter_context(nc.psum_tensor("psG", [128, 1024], F32))
        psU = pctx.enter_context(nc.psum_tensor("psU", [128, 1024], F32))
        psO = [pctx.enter_context(nc.psum_tensor(f"psO{i}", [128, 512], F32))
               for i in range(2)]

        ring_free = [None] * RING
        gtile = [0]

        def ring_load(src_ap, kt):
            slot = gtile[0] % RING
            gtile[0] += 1
            ncol = src_ap.shape[1] * src_ap.shape[2]
            ds = S[f"ring{slot}"][:, 0:ncol].rearrange(
                "p (k m) -> p k m", k=kt)
            tok = p.dma("gpsimd", ds, src_ap,
                        waits=[ring_free[slot]], inc=dR[slot])
            return slot, tok

        psG_rd = [None, None]; psU_rd = [None, None]
        psO_rd = [None, None]
        cw_tok = cw_toks[-1]
        yt_free = [None, None]
        tf_prev = {}
        act_guard = [None]

        for chunk in range(2):
            coff = chunk * 1024
            acc_toks = {}
            for e in range(NE + 1):
                routed = e > 0
                if routed:
                    wg_src, wu_src, wd_src = (d["wg"][e - 1], d["wu"][e - 1],
                                              d["wd"][e - 1])
                else:
                    wg_src, wu_src, wd_src = d["wg_s"], d["wu_s"], d["wd_s"]
                act_toks = []
                for q in range(4):
                    sg, tg = ring_load(
                        wg_src[:, q * 512:(q + 1) * 512].rearrange(
                            "(k p) m -> p k m", p=128), 8)
                    su, tu = ring_load(
                        wu_src[:, q * 512:(q + 1) * 512].rearrange(
                            "(k p) m -> p k m", p=128), 8)
                    t_g = t_u = None
                    for hi in range(4):
                        ht = q * 4 + hi
                        for sub in range(2):
                            i = (ht * 2 + sub) % 2
                            for dk in range(8):
                                t_g = p.op(
                                    "tensor", "matmul",
                                    psG[:, i * 512:i * 512 + 512],
                                    S[f"ring{sg}"][:, dk * 512 + hi * 128:
                                                   dk * 512 + hi * 128 + 128],
                                    A["y_lnT"][:, dk * TPC + coff + sub * 512:
                                               dk * TPC + coff + sub * 512
                                               + 512],
                                    start=(dk == 0), stop=(dk == 7),
                                    waits=[tg, psG_rd[i]],
                                    inc=(st if dk == 7 else None))
                            for dk in range(8):
                                t_u = p.op(
                                    "tensor", "matmul",
                                    psU[:, i * 512:i * 512 + 512],
                                    S[f"ring{su}"][:, dk * 512 + hi * 128:
                                                   dk * 512 + hi * 128 + 128],
                                    A["y_lnT"][:, dk * TPC + coff + sub * 512:
                                               dk * TPC + coff + sub * 512
                                               + 512],
                                    start=(dk == 0), stop=(dk == 7),
                                    waits=[tu, psU_rd[i]],
                                    inc=(st if dk == 7 else None))
                            t_si = p.op("scalar", "activation",
                                        A["sil"][:, i * 512:i * 512 + 512],
                                        psG[:, i * 512:i * 512 + 512],
                                        AF.Silu, waits=[t_g, psU_rd[i]],
                                        inc=ss)
                            psG_rd[i] = t_si
                            dst = A["actT"][:, ht * 1024 + sub * 512:
                                            ht * 1024 + sub * 512 + 512]
                            t_ac = p.op("vector", "scalar_tensor_tensor",
                                        dst,
                                        psU[:, i * 512:i * 512 + 512],
                                        1.0,
                                        A["sil"][:, i * 512:i * 512 + 512],
                                        ALU.mult, ALU.mult,
                                        waits=[t_u, t_si, act_guard[0]],
                                        inc=sv)
                            psU_rd[i] = t_ac
                            act_toks.append(t_ac)
                    ring_free[sg] = t_g
                    ring_free[su] = t_u
                wd_slots = []
                for q in range(4):
                    wd_slots.append(ring_load(
                        wd_src[q * 512:(q + 1) * 512, :].rearrange(
                            "(k p) m -> p k m", p=128), 4))
                act_tok = act_toks[-1]
                for tt in range(8):
                    for half in range(2):
                        i = (tt * 2 + half) % 2
                        t_o = None
                        for ht in range(16):
                            sd, td = wd_slots[ht // 4]
                            t_o = p.op(
                                "tensor", "matmul", psO[i][:, 0:512],
                                A["actT"][:, ht * 1024 + tt * 128:
                                          ht * 1024 + tt * 128 + 128],
                                S[f"ring{sd}"][:, (ht % 4) * 1024
                                               + half * 512:
                                               (ht % 4) * 1024 + half * 512
                                               + 512],
                                start=(ht == 0), stop=(ht == 15),
                                waits=[td, act_tok, psO_rd[i]],
                                inc=(st if ht == 15 else None))
                        sl = A["out_acc"][:, tt * 1024 + half * 512:
                                          tt * 1024 + half * 512 + 512]
                        if e == 0:
                            t_a = p.op("vector", "tensor_copy", sl,
                                       psO[i][:, 0:512],
                                       waits=[t_o, tf_prev.get(tt)], inc=sv)
                        else:
                            gcol = (chunk * 8 + tt) * 8 + (e - 1)
                            t_a = p.op("vector", "scalar_tensor_tensor", sl,
                                       psO[i][:, 0:512],
                                       A["cw"][:, gcol:gcol + 1], sl,
                                       ALU.mult, ALU.add,
                                       waits=[t_o, acc_toks[(tt, half)],
                                              cw_tok], inc=sv)
                        psO_rd[i] = t_a
                        acc_toks[(tt, half)] = t_a
                for sd, td in wd_slots:
                    ring_free[sd] = p.now(st)
            for tt in range(8):
                par = tt % 2
                gt = chunk * 8 + tt
                t_y = p.dma("sync", A[f"ytile{par}"][:, :],
                            ys_d[gt * 128:(gt + 1) * 128, :],
                            waits=[yt_free[par], p.now(dYs[0]),
                                   p.now(dYs[1])], inc=dYt[par])
                t_f = p.op("vector", "tensor_add", A[f"ytile{par}"][:, :],
                           A[f"ytile{par}"][:, :],
                           bass.AP(A["out_acc"], tt * 1024,
                                   [[8192, 128], [1, 1024]]),
                           waits=[t_y, acc_toks[(tt, 0)],
                                  acc_toks[(tt, 1)]], inc=sv)
                t_od = p.dma("sync", out_d[gt], A[f"ytile{par}"][:, :],
                             waits=[t_f], inc=dOut[par])
                yt_free[par] = t_od
                tf_prev[tt] = t_f
            act_guard[0] = p.now(st)

    return finish()


def kernel(**inputs):
    in_maps = _host_prep(inputs)
    nc = _build()
    res = run_bass_kernel_spmd(nc, in_maps, core_ids=list(range(8)))
    y = np.zeros((B, T, D), np.float32)
    for core in range(8):
        b, r = core // 4, core % 4
        o = np.asarray(res.results[core]["out"])
        for j in range(NJ):
            q = _Q(r, j)
            y[b, q * 128:(q + 1) * 128] = o[j]
    return y


# version 14
# speedup vs baseline: 1.0900x; 1.0127x over previous
import numpy as np
import ml_dtypes
import concourse.bass as bass
import concourse.mybir as mybir
from concourse.bass_utils import run_bass_kernel_spmd
from contextlib import ExitStack

F32 = mybir.dt.float32
BF16 = mybir.dt.bfloat16
AF = mybir.ActivationFunctionType
ALU = mybir.AluOpType
AX = mybir.AxisListType
BF = ml_dtypes.bfloat16

B, T, D = 2, 8192, 1024
NH, HD, RD = 16, 64, 32
KVR, QR = 256, 512
FFI, NE = 2048, 8
NJ = 16
TPC = NJ * 128
SCALE = float(96.0 ** -0.5)
MARGIN = 6.0
EPS_LN, EPS_RMS = 1e-5, 1e-6
NKC = 32
NQC = 8
RING = 7

ENGINES = ("sync", "gpsimd", "vector", "scalar", "tensor")


def _Q(r, j):
    return 63 - 4 * j - r


def _E(j):
    return 64 - 4 * j


def _rope_tables():
    inv = 1.0 / 10000.0 ** (np.arange(0, RD, 2, dtype=np.float32) / RD)
    t = np.arange(T, dtype=np.float32)
    fr = np.outer(t, inv)
    emb = np.concatenate([fr, fr], -1)
    cos = np.cos(emb).astype(np.float32)
    sin = np.sin(emb).astype(np.float32)
    dd = np.arange(NH * RD) % RD
    sign = np.where(dd < RD // 2, -1.0, 1.0).astype(np.float32)
    cosT = np.ascontiguousarray(cos[:, dd].T)
    ssinT = np.ascontiguousarray((sin[:, dd] * sign[None, :]).T)
    return cosT, ssinT


def _swap_perm(ncols):
    dd = np.arange(ncols)
    j = dd % RD
    return (dd // RD) * RD + (j + RD // 2) % RD


def _host_prep(inp):
    cosT, ssinT = _rope_tables()
    f32 = np.float32
    asc = np.ascontiguousarray
    W_kr = asc(inp["L0_attn_W_krope"], f32)
    W_qr = asc(inp["L0_attn_W_qrope"], f32)
    shared = dict(
        w_kv=asc(inp["L0_attn_W_kv"], f32),
        w_kdec=asc(inp["L0_attn_W_kdec"], f32),
        w_vdec=asc(inp["L0_attn_W_vdec"], f32),
        w_q=asc(inp["L0_attn_W_q"], f32),
        w_qdec=asc(inp["L0_attn_W_qdec"], f32),
        w_kr=W_kr, w_kr_sw=asc(W_kr[:, _swap_perm(NH * RD)]),
        w_qr=W_qr, w_qr_sw=asc(W_qr[:, _swap_perm(NH * RD)]),
        w_o=asc(inp["L0_attn_W_o"], f32),
        g_kv=asc(inp["L0_attn_g_kv"], f32).reshape(2, 128).T.copy(),
        kcos=cosT, kssin=ssinT,
        wg_s=asc(inp["L0_ffn_Wg_s"].T, f32),
        wu_s=asc(inp["L0_ffn_Wu_s"].T, f32),
        wd_s=asc(inp["L0_ffn_Wd_s"].T, f32),
        wg=asc(np.transpose(inp["L0_ffn_Wg"], (0, 2, 1)), f32),
        wu=asc(np.transpose(inp["L0_ffn_Wu"], (0, 2, 1)), f32),
        wd=asc(np.transpose(inp["L0_ffn_Wd"], (0, 2, 1)), f32),
        wr=asc(inp["L0_ffn_Wr"].T, f32),
        rb=asc(inp["L0_ffn_rb"], f32).reshape(1, NE),
        eye_bf=np.eye(128, dtype=BF),
        eye_f32=np.eye(128, dtype=f32),
        ones_row_f=np.ones((1, 128), f32),
        ones_col_f=np.ones((128, 1), f32),
        ones_8k=np.ones((1, T), BF),
    )
    x = np.asarray(inp["L0_attn_x"], f32)
    in_maps = []
    for core in range(8):
        b, r = core // 4, core % 4
        qt = [_Q(r, j) for j in range(NJ)]
        x_q = asc(np.concatenate([x[b, q * 128:(q + 1) * 128] for q in qt], 0))
        qpos = np.concatenate([np.arange(q * 128, (q + 1) * 128) for q in qt])
        mask = np.zeros((NJ // 2, 8, 128, 256), np.float32)
        for pj in range(NJ // 2):
            E0 = _E(2 * pj)
            for pp in range(8):
                k0 = (E0 - 8 + pp) * 128
                ki = k0 + np.arange(128)[:, None]
                for jj in range(2):
                    q0 = qt[2 * pj + jj] * 128
                    qi = q0 + np.arange(128)[None, :]
                    mask[pj, pp, :, jj * 128:(jj + 1) * 128] = np.where(
                        ki <= qi, 0.0, -30000.0)
        m = dict(shared)
        m.update(
            x_b=x[b], x_q=x_q,
            qcos=asc(cosT[:, qpos] * SCALE),
            qssin=asc(ssinT[:, qpos] * SCALE),
            maskT=mask.astype(BF),
        )
        in_maps.append(m)
    return in_maps


class P:
    def __init__(self, nc, ctx):
        self.nc, self.ctx = nc, ctx
        self.ev = {e: [] for e in ENGINES}
        self.cnt = {}
        self.sems = []

    def sem(self, name):
        s = self.ctx.enter_context(self.nc.semaphore(name))
        self.cnt[s] = 0
        self.sems.append(s)
        return s

    def add(self, eng, fn, waits=(), inc=None, n=1):
        w = [t for t in waits if t is not None]
        tok = None
        if inc is not None:
            self.cnt[inc] += n
            tok = (inc, self.cnt[inc])
        self.ev[eng].append((fn, w, inc, n))
        return tok

    def op(self, eng, meth, *args, waits=(), inc=None, n=1, **kw):
        def fn(e, m=meth, a=args, k=kw):
            return getattr(e, m)(*a, **k)
        return self.add(eng, fn, waits, inc, n)

    def dma(self, eng, out, in_, waits=(), inc=None):
        def fn(e, o=out, i=in_):
            return e.dma_start(out=o, in_=i)
        return self.add(eng, fn, waits, inc, 16)

    def now(self, sem):
        return (sem, self.cnt[sem])

    def barrier(self):
        toks = [self.now(s) for s in self.sems if s is not self._bar]
        for e in ENGINES:
            self.op(e, "nop", waits=toks, inc=self._bar, n=1)
        full = self.now(self._bar)
        for e in ENGINES:
            self.op(e, "nop", waits=[full])

    def emit(self):
        with self.nc.Block() as block:
            for eng in ENGINES:
                def body(e, name=eng):
                    seen = {}
                    for fn, waits, inc, n in self.ev[name]:
                        for (s, v) in waits:
                            if seen.get(s, -1) < v:
                                e.wait_ge(s, v)
                                seen[s] = v
                        inst = fn(e)
                        if inc is not None:
                            inst.then_inc(inc, n)
                getattr(block, eng)(body)


def _build(dev=99):
    nc = bass.Bass("TRN2", target_bir_lowering=False)
    d = {}

    def din(name, shape, dt=F32):
        d[name] = nc.dram_tensor(name, list(shape), dt, kind="ExternalInput")
        return d[name]

    def dint(name, shape, dt):
        kind = "ExternalOutput" if dev < 99 else "Internal"
        d[name] = nc.dram_tensor(name, list(shape), dt, kind=kind)
        return d[name]

    din("x_b", (T, D)); din("x_q", (TPC, D))
    din("w_kv", (D, KVR)); din("w_kdec", (KVR, D)); din("w_vdec", (KVR, D))
    din("w_q", (D, QR)); din("w_qdec", (QR, D))
    din("w_kr", (D, NH * RD)); din("w_kr_sw", (D, NH * RD))
    din("w_qr", (QR, NH * RD)); din("w_qr_sw", (QR, NH * RD))
    din("w_o", (D, D)); din("g_kv", (128, 2))
    din("kcos", (NH * RD, T)); din("kssin", (NH * RD, T))
    din("qcos", (NH * RD, TPC)); din("qssin", (NH * RD, TPC))
    din("maskT", (NJ // 2, 8, 128, 256), BF16)
    din("wg_s", (D, FFI)); din("wu_s", (D, FFI)); din("wd_s", (FFI, D))
    din("wg", (NE, D, FFI)); din("wu", (NE, D, FFI)); din("wd", (NE, FFI, D))
    din("wr", (D, NE)); din("rb", (1, NE))
    din("eye_bf", (128, 128), BF16); din("eye_f32", (128, 128))
    din("ones_row_f", (1, 128)); din("ones_col_f", (128, 1))
    din("ones_8k", (1, T), BF16)

    out_d = nc.dram_tensor("out", [NJ, 128, D], F32, kind="ExternalOutput")

    kro_d = dint("kro_s", (NH * RD, T), BF16)
    kall_d = dint("k_all", (NH, 128, T), BF16)
    vall_d = dint("v_all", (NH // 2, 128, 64 * 136), BF16)
    qall_d = dint("qt_all", (NH, 128, TPC), BF16)
    ys_d = dint("y_s", (TPC, D), F32)
    cwT_d = dint("cwT_s", (128, 128), F32)
    mrow_d = dint("mrow_s", (1, 2048), BF16)

    S = {}

    SBASE = 16512

    def sb(name, shape, dt, off):
        S[name] = nc.alloc_sbuf_tensor_at(name, list(shape), dt,
                                          offset=SBASE + off)
        return S[name]

    sb("c_eye_bf", (128, 128), BF16, 0)
    sb("c_eye_f", (128, 128), F32, 256)
    sb("c_ones_row", (1, 128), F32, 768)
    sb("c_ones_col", (128, 1), F32, 1280)
    sb("c_g", (128, 2), F32, 1312)
    sb("wrt", (128, 64), BF16, 1344)
    sb("cwbuf", (128, 128), F32, 2048)
    sb("cw", (128, 128), F32, 2560)
    sb("cwT", (128, 128), F32, 3072)
    sb("r_m1", (128, 16), F32, 3584)
    sb("r_m2", (128, 16), F32, 3648)
    sb("r_ge", (128, 16), F32, 3712)
    sb("rbB", (128, 8), F32, 3776)
    sb("kv_normT", (128, 2 * T), BF16, 4096)
    sb("q_cT", (128, 4 * TPC), BF16, 36864)
    sb("q_ropeT", (128, 4 * TPC), BF16, 53248)
    sb("attn_o64", (128, NH * TPC), BF16, 4096)
    sb("out_acc", (128, 8 * 1024), F32, 4096)
    sb("cwB", (128, 1024), F32, 36864)
    sb("ytile0", (128, 1024), F32, 40960)
    sb("ytile1", (128, 1024), F32, 45056)
    sb("sil", (128, 1024), BF16, 49152)
    sb("an", (128, 1024), F32, 51200)
    sb("cwrow", (1, 1024), F32, 55296)
    M0 = 69632
    o = M0
    for nm, cols, dt, npart in [
        ("wkv", 8 * KVR, BF16, 128), ("wkr", 8 * 512, BF16, 128),
        ("wkr_sw", 8 * 512, BF16, 128), ("wq", 8 * 512, BF16, 128),
        ("wqr", 4 * 512, BF16, 128), ("wqr_sw", 4 * 512, BF16, 128),
        ("xbuf0", 2048, F32, 128), ("xbuf1", 2048, F32, 128),
        ("xbuf2", 2048, F32, 128),
        ("xsq", 1024, F32, 128), ("xln", 2048, BF16, 128),
        ("xlnT0", 2048, BF16, 128), ("xlnT1", 2048, BF16, 128),
        ("tabc0", 1024, F32, 128), ("tabs0", 1024, F32, 128),
        ("tabc1", 1024, F32, 128), ("tabs1", 1024, F32, 128),
        ("kro0", 4096, BF16, 128), ("kro1", 4096, BF16, 128),
        ("ropet1", 1024, F32, 128), ("sqv", 512, F32, 128),
        ("rkv", 512, F32, 1), ("rkvB", 512, F32, 128), ("lnst", 32, F32, 128),
    ]:
        sz = cols * (2 if dt == BF16 else 4)
        sb(nm, (npart, cols), dt, o)
        o += (sz + 31) & ~31
    assert o <= 196352, ("A", o)
    sb("wkdec", (128, 2 * D), BF16, 196352)
    sb("wvdec", (128, 2 * D), BF16, 200448)
    sb("wqdec", (128, 4 * D), BF16, 204544)
    assert 204544 + 4 * D * 2 <= 212800
    o = M0
    for nm, cols in [
        ("KTb0", T), ("KTb1", T),
        ("Vb0", 64 * 136 + 32), ("Vb1", 64 * 136 + 32),
        ("QTb0", TPC), ("QTb1", TPC),
    ]:
        sb(nm, (128, cols), BF16, o)
        o += (cols * 2 + 31) & ~31
    assert o <= 212800, ("A2", o)
    o = M0
    for nm, cols, dt in [
        ("ktB0", T, BF16), ("ktB1", T, BF16),
        ("vB0", 64 * 136 + 32, BF16), ("vB1", 64 * 136 + 32, BF16),
        ("qtB0", TPC, BF16), ("qtB1", TPC, BF16),
        ("probsT0", 1024, BF16), ("probsT1", 1024, BF16),
        ("maskb0", 2048, BF16), ("maskb1", 2048, BF16),
        ("ones64", 64, F32), ("rl64", 256, F32), ("rlB", 256, F32),
        ("p1tmp", 8, F32), ("mcols", 16, F32), ("mneg", 16, BF16),
    ]:
        sz = cols * (2 if dt == BF16 else 4)
        sb(nm, (128, cols), dt, o)
        o += (sz + 31) & ~31
    sb("mrow", (16, 128), BF16, o); o += 256
    assert o <= 163584, ("B", o)
    sb("masktmp", (128, 1024), F32, 163584)
    o = M0
    for nm, cols, dt in [
        ("xq0", 1024, F32), ("xq1", 1024, F32),
        ("yb0", 1024, F32), ("yb1", 1024, F32),
        ("ylnb", 1024, F32), ("ylnb2", 1024, BF16),
        ("lnst2", 32, F32), ("lgt", 8, F32),
    ]:
        sz = cols * (2 if dt == BF16 else 4)
        sb(nm, (128, cols), dt, o)
        o += (sz + 31) & ~31
    assert o <= 163584, ("C", o)
    sb("y_lnT", (128, 8 * TPC), BF16, 163584)
    sb("wo", (64, 16 * 1024), BF16, 106496)
    sb("actT", (128, 16 * 1024), BF16, M0)
    for i in range(RING):
        sb(f"ring{i}", (128, 4096), BF16, M0 + 32768 + i * 8192)
    assert M0 + 32768 + RING * 8192 <= 163584

    ctx = ExitStack()
    p = P(nc, ctx)
    sv = p.sem("sv"); ss = p.sem("ss"); st = p.sem("st")
    sp = p.sem("sp")
    dC = p.sem("dC")
    dW = p.sem("dW")
    dX = [p.sem(f"dX{i}") for i in range(3)]
    dT = [p.sem(f"dT{i}") for i in range(2)]
    dKr = [p.sem(f"dKr{i}") for i in range(2)]
    dRp = [p.sem(f"dRp{i}") for i in range(2)]
    dQr = [p.sem(f"dQr{i}") for i in range(2)]
    dSt = [p.sem(f"dSt{i}") for i in range(2)]
    dK = [p.sem(f"dK{i}") for i in range(2)]
    dQ = [p.sem(f"dQ{i}") for i in range(2)]
    dV = [p.sem(f"dV{i}") for i in range(2)]
    dM = [p.sem(f"dM{i}") for i in range(2)]
    dMr = p.sem("dMr")
    dXQ = [p.sem(f"dXQ{i}") for i in range(2)]
    dYs = [p.sem(f"dYs{i}") for i in range(2)]
    dR = [p.sem(f"dR{i}") for i in range(RING)]
    dCw = p.sem("dCw")
    dYt = [p.sem(f"dYt{i}") for i in range(2)]
    dOut = [p.sem(f"dOut{i}") for i in range(2)]
    p._bar = p.sem("bar")
    A = S

    def finish():
        for e in ENGINES:
            p.op(e, "nop", waits=[p.now(s) for s in p.sems
                                  if s is not p._bar])
        p.emit()
        ctx.close()
        return nc

    p.dma("sync", A["c_eye_bf"][:, :], d["eye_bf"][:, :], inc=dC)
    p.dma("sync", A["c_eye_f"][:, :], d["eye_f32"][:, :], inc=dC)
    p.dma("sync", A["c_ones_row"][:, :], d["ones_row_f"][:, :], inc=dC)
    p.dma("sync", A["c_ones_col"][:, :], d["ones_col_f"][:, :], inc=dC)
    p.dma("sync", A["c_g"][:, :], d["g_kv"][:, :], inc=dC)
    consts_tok = p.now(dC)

    def wload(dst, src, inc=dW):
        kt = src.shape[0] // 128
        sr = src.rearrange("(k p) m -> p k m", p=128)
        ds = dst[:, :].rearrange("p (k m) -> p k m", k=kt)
        return p.dma("gpsimd", ds, sr, inc=inc)

    wload(A["wkv"], d["w_kv"])
    wload(A["wkr"], d["w_kr"])
    wload(A["wkr_sw"], d["w_kr_sw"])
    wload(A["wq"], d["w_q"])
    wload(A["wqr"], d["w_qr"])
    wload(A["wqr_sw"], d["w_qr_sw"])
    wA_tok = p.now(dW)
    wload(A["wkdec"], d["w_kdec"])
    wload(A["wvdec"], d["w_vdec"])
    wload(A["wqdec"], d["w_qdec"])
    wA2_tok = p.now(dW)

    with ExitStack() as pctx:
        psT = [pctx.enter_context(nc.psum_tensor(f"psT{i}", [128, 1024], BF16))
               for i in range(2)]
        psK = [pctx.enter_context(nc.psum_tensor(f"psK{i}", [128, 512], F32))
               for i in range(2)]
        psR = [pctx.enter_context(nc.psum_tensor(f"psR{i}", [128, 512], F32))
               for i in range(4)]

        xfree = [None, None, None]
        xTfree = [None, None]
        tabfree = [None, None]
        krofree = [None, None]
        ropet1_free = [None, None]
        psT_rd = [None, None]
        psK_rd = [None, None]
        psR_rd = [None, None, None, None]
        xln_free = None
        xsq_free = None
        setfree = [None] * 4
        pend_rms = [None]

        def sweep(nchunks, x_src, cos_src, sin_src, qside):
            nonlocal xln_free, xsq_free
            for c in range(nchunks):
                par = c % 2
                xpar = c % 3
                xb = A[f"xbuf{xpar}"]
                xs = x_src[c * 256:(c + 1) * 256].rearrange(
                    "(t p) d -> p t d", p=128)
                tx = p.dma("sync",
                           xb[:, :].rearrange("p (t d) -> p t d", t=2),
                           xs, waits=[xfree[xpar]], inc=dX[xpar])
                tc = p.dma("scalar",
                           A[f"tabc{par}"][:, :].rearrange(
                               "p (k t) -> p k t", k=4),
                           cos_src[:, c * 256:(c + 1) * 256].rearrange(
                               "(k p) t -> p k t", p=128),
                           waits=[tabfree[par]], inc=dT[par])
                tsn = p.dma("scalar",
                            A[f"tabs{par}"][:, :].rearrange(
                                "p (k t) -> p k t", k=4),
                            sin_src[:, c * 256:(c + 1) * 256].rearrange(
                                "(k p) t -> p k t", p=128),
                            waits=[tabfree[par]], inc=dT[par])
                ln_toks = []
                for t in range(2):
                    lb = ((c * 2 + t) % 4) * 8
                    xt = xb[:, t * 1024:(t + 1) * 1024]
                    t_r0 = p.op("vector", "reduce_sum", A["lnst"][:, lb + 0:lb + 1],
                                xt, AX.X, waits=[tx], inc=sv)
                    tq = p.op("scalar", "activation", A["xsq"][:, :], xt,
                              AF.Square, waits=[tx, xsq_free], inc=ss)
                    xsq_free = p.op("vector", "reduce_sum", A["lnst"][:, lb + 1:lb + 2],
                                    A["xsq"][:, :], AX.X, waits=[tq], inc=sv)
                    t_ms = p.op("gpsimd", "tensor_scalar", A["lnst"][:, lb + 2:lb + 4],
                                A["lnst"][:, lb + 0:lb + 2], 1.0 / D, None, ALU.mult,
                                waits=[xsq_free, setfree[(c * 2 + t) % 4]],
                                inc=sp)
                    p.op("gpsimd", "tensor_tensor", A["lnst"][:, lb + 4:lb + 5],
                         A["lnst"][:, lb + 2:lb + 3], A["lnst"][:, lb + 2:lb + 3], ALU.mult)
                    t_ve = p.op("gpsimd", "tensor_scalar", A["lnst"][:, lb + 5:lb + 6],
                                A["lnst"][:, lb + 3:lb + 4], A["lnst"][:, lb + 4:lb + 5], EPS_LN,
                                ALU.subtract, ALU.add, inc=sp)
                    t_rs = p.op("scalar", "activation", A["lnst"][:, lb + 6:lb + 7],
                                A["lnst"][:, lb + 5:lb + 6], AF.Sqrt,
                                waits=[t_ve], inc=ss)
                    t_rc = p.op("vector", "reciprocal", A["lnst"][:, lb + 7:lb + 8],
                                A["lnst"][:, lb + 6:lb + 7], waits=[t_rs], inc=sv)
                    t_cp = p.op("gpsimd", "tensor_copy", A["lnst"][:, lb + 5:lb + 6],
                                A["lnst"][:, lb + 7:lb + 8], waits=[t_rc], inc=sp)
                    tok = p.op("vector", "tensor_scalar",
                               A["xln"][:, t * 1024:(t + 1) * 1024], xt,
                               A["lnst"][:, lb + 2:lb + 3], A["lnst"][:, lb + 5:lb + 6],
                               ALU.subtract, ALU.mult,
                               waits=[xln_free, t_cp, t_ms], inc=sv)
                    setfree[(c * 2 + t) % 4] = tok
                    ln_toks.append(tok)
                xfree[xpar] = ln_toks[-1]
                if pend_rms[0] is not None:
                    cp, kvt = pend_rms[0]
                    rko = (cp % 2) * 256
                    sq_toks = [
                        p.op("scalar", "activation",
                             A["sqv"][:, m * 256:m * 256 + 256],
                             psK[m][:, 0:256], AF.Square,
                             waits=[kvt[m]], inc=ss)
                        for m in range(2)]
                    for m in range(2):
                        t_om = p.op("tensor", "matmul", psK[0][0:1, 256:512],
                                    A["c_ones_col"][:, :],
                                    A["sqv"][:, m * 256:m * 256 + 256],
                                    start=(m == 0), stop=(m == 1),
                                    waits=[sq_toks[m], consts_tok], inc=st)
                    t_pre = p.op("vector", "tensor_scalar",
                                 A["rkv"][0:1, rko:rko + 256],
                                 psK[0][0:1, 256:512], 1.0 / KVR, EPS_RMS,
                                 ALU.mult, ALU.add, waits=[t_om], inc=sv)
                    t_sq = p.op("scalar", "activation",
                                A["rkv"][0:1, rko:rko + 256],
                                A["rkv"][0:1, rko:rko + 256], AF.Sqrt,
                                waits=[t_pre], inc=ss)
                    t_rc = p.op("vector", "reciprocal",
                                A["rkv"][0:1, rko:rko + 256],
                                A["rkv"][0:1, rko:rko + 256], waits=[t_sq],
                                inc=sv)
                    pend_rms[0] = (cp, rko, t_rc)
                xT = A[f"xlnT{par}"]
                last_tp = None
                for t in range(2):
                    for dkk in range(8):
                        tt = p.op("tensor", "transpose",
                                  psT[t][:, dkk * 128:dkk * 128 + 128],
                                  A["xln"][:, t * 1024 + dkk * 128:
                                           t * 1024 + dkk * 128 + 128],
                                  A["c_eye_bf"][:, :],
                                  waits=[ln_toks[t], consts_tok, psT_rd[t],
                                         xTfree[par]],
                                  inc=(st if dkk == 7 else None))
                    ev = p.op("vector", "tensor_copy",
                              bass.AP(A[f"xlnT{par}"], t * 128,
                                      [[2048, 128], [256, 8], [1, 128]]),
                              psT[t][:, :].rearrange(
                                  "p (k q) -> p k q", k=8),
                              waits=[tt], inc=sv)
                    psT_rd[t] = ev
                    last_tp = tt
                xln_free = last_tp
                xT_toks = [psT_rd[0], psT_rd[1]]
                if pend_rms[0] is not None and len(pend_rms[0]) == 3:
                    cp, rko, t_rc = pend_rms[0]
                    t_bc = p.op("tensor", "matmul", psK[1][:, 256:512],
                                A["c_ones_row"][:, :],
                                A["rkv"][0:1, rko:rko + 256],
                                start=True, stop=True, waits=[t_rc], inc=st)
                    t_ev = p.op("scalar", "copy", A["rkvB"][:, rko:rko + 256],
                                psK[1][:, 256:512], waits=[t_bc], inc=ss)
                    for m in range(2):
                        tok = p.op("vector", "scalar_tensor_tensor",
                                   A["kv_normT"][:, m * T + cp * 256:
                                                 m * T + cp * 256 + 256],
                                   psK[m][:, 0:256], A["c_g"][:, m:m + 1],
                                   A["rkvB"][:, rko:rko + 256],
                                   ALU.mult, ALU.mult, waits=[t_ev], inc=sv)
                        psK_rd[m] = tok
                    pend_rms[0] = None
                kv_toks = []
                for m in range(2):
                    for dk in range(8):
                        tok = p.op("tensor", "matmul", psK[m][:, 0:256],
                                   A["wkv" if not qside else "wq"][
                                       :, dk * (256 if not qside else 512)
                                       + m * 128:
                                       dk * (256 if not qside else 512)
                                       + m * 128 + 128],
                                   xT[:, dk * 256:dk * 256 + 256],
                                   start=(dk == 0), stop=(dk == 7),
                                   waits=xT_toks + [wA_tok, psK_rd[m]],
                                   inc=(st if dk == 7 else None))
                    kv_toks.append(tok)
                if not qside:
                    pend_rms[0] = (c, kv_toks)
                    qc_tok = None
                else:
                    for m in range(2):
                        tok = p.op("vector", "tensor_copy",
                                   A["q_cT"][:, m * TPC + c * 256:
                                             m * TPC + c * 256 + 256],
                                   psK[m][:, 0:256],
                                   waits=[kv_toks[m]], inc=sv)
                        psK_rd[m] = tok
                    for m in range(2, 4):
                        mm = m - 2
                        for dk in range(8):
                            tok = p.op("tensor", "matmul", psK[mm][:, 0:256],
                                       A["wq"][:, dk * 512 + m * 128:
                                               dk * 512 + m * 128 + 128],
                                       xT[:, dk * 256:dk * 256 + 256],
                                       start=(dk == 0), stop=(dk == 7),
                                       waits=[psK_rd[mm]],
                                       inc=(st if dk == 7 else None))
                        tok = p.op("vector", "tensor_copy",
                                   A["q_cT"][:, m * TPC + c * 256:
                                             m * TPC + c * 256 + 256],
                                   psK[mm][:, 0:256], waits=[tok], inc=sv)
                        psK_rd[mm] = tok
                    qc_tok = tok
                par4, slot = (c // 4) % 2, c % 4
                kro = A[f"kro{par4}"]
                rope_evs = []
                for m in range(4):
                    pr = m % 2
                    ro = pr * 512
                    bs_, bw_ = psR[pr * 2], psR[pr * 2 + 1]
                    if not qside:
                        wst, wsw, nk = A["wkr"], A["wkr_sw"], 8

                        def rr(dk):
                            return xT[:, dk * 256:dk * 256 + 256]
                        rw = xT_toks
                    else:
                        wst, wsw, nk = A["wqr"], A["wqr_sw"], 4

                        def rr(dk):
                            return A["q_cT"][:, dk * TPC + c * 256:
                                             dk * TPC + c * 256 + 256]
                        rw = [qc_tok]
                    for dk in range(nk):
                        t_s = p.op("tensor", "matmul", bs_[:, 0:256],
                                   wst[:, dk * 512 + m * 128:
                                       dk * 512 + m * 128 + 128], rr(dk),
                                   start=(dk == 0), stop=(dk == nk - 1),
                                   waits=rw + [psR_rd[pr * 2]],
                                   inc=(st if dk == nk - 1 else None))
                    for dk in range(nk):
                        t_w = p.op("tensor", "matmul", bw_[:, 0:256],
                                   wsw[:, dk * 512 + m * 128:
                                       dk * 512 + m * 128 + 128], rr(dk),
                                   start=(dk == 0), stop=(dk == nk - 1),
                                   waits=[psR_rd[pr * 2 + 1]],
                                   inc=(st if dk == nk - 1 else None))
                    ct = A[f"tabc{par}"][:, m * 256:m * 256 + 256]
                    st_ = A[f"tabs{par}"][:, m * 256:m * 256 + 256]
                    t1 = p.op("vector", "tensor_tensor",
                              A["ropet1"][:, ro:ro + 256], bs_[:, 0:256], ct,
                              ALU.mult,
                              waits=[t_s, tc, ropet1_free[pr]], inc=sv)
                    psR_rd[pr * 2] = t1
                    t2 = p.op("vector", "tensor_tensor",
                              A["ropet1"][:, ro + 256:ro + 512],
                              bw_[:, 0:256], st_, ALU.mult,
                              waits=[t_w, tsn, ropet1_free[pr]], inc=sv)
                    psR_rd[pr * 2 + 1] = t2
                    dst = (kro[:, m * 1024 + slot * 256:
                               m * 1024 + slot * 256 + 256] if not qside else
                           A["q_ropeT"][:, m * TPC + c * 256:
                                        m * TPC + c * 256 + 256])
                    t3 = p.op("gpsimd", "tensor_add", dst,
                              A["ropet1"][:, ro:ro + 256],
                              A["ropet1"][:, ro + 256:ro + 512],
                              waits=[t1, t2,
                                     krofree[par4] if (not qside and slot == 0
                                                       and m == 0) else None],
                              inc=sp)
                    ropet1_free[pr] = t3
                    rope_evs.append(t3)
                tabfree[par] = rope_evs[-1]
                xTfree[par] = rope_evs[-1]
                if not qside and slot == 3:
                    kd = kro_d[:, (c - 3) * 256:(c + 1) * 256].rearrange(
                        "(m p) t -> p m t", p=128)
                    krofree[par4] = p.dma(
                        "scalar" if (c // 4) % 2 == 0 else "gpsimd", kd,
                        kro[:, :].rearrange("p (m t) -> p m t", m=4),
                        waits=[rope_evs[-1]], inc=dKr[par4])

        sweep(NKC, d["x_b"], d["kcos"], d["kssin"], qside=False)
        sweep(NQC, d["x_q"], d["qcos"], d["qssin"], qside=True)

    if dev == 1:
        dd = nc.dram_tensor("dbg_kvnorm", [128, 2 * T], BF16,
                            kind="ExternalOutput")
        p.dma("sync", dd[:, :], A["kv_normT"][:, :],
              waits=[p.now(sv), p.now(st), p.now(ss), p.now(dKr[0]),
                     p.now(dKr[1])], inc=dC)
        dd2 = nc.dram_tensor("dbg_qc", [128, 4 * TPC], BF16,
                             kind="ExternalOutput")
        p.dma("sync", dd2[:, :], A["q_cT"][:, :], waits=[p.now(sv)], inc=dC)
        dd3 = nc.dram_tensor("dbg_qrope", [128, 4 * TPC], BF16,
                             kind="ExternalOutput")
        p.dma("sync", dd3[:, :], A["q_ropeT"][:, :], waits=[p.now(sv)],
              inc=dC)
        dd4 = nc.dram_tensor("dbg_lnst", [128, 32], F32, kind="ExternalOutput")
        p.dma("sync", dd4[:, :], A["lnst"][:, :],
              waits=[p.now(sv), p.now(sp), p.now(ss)], inc=dC)
        dd5 = nc.dram_tensor("dbg_xsq", [128, 1024], F32,
                             kind="ExternalOutput")
        p.dma("sync", dd5[:, :], A["xsq"][:, :], waits=[p.now(ss)], inc=dC)
        dd6 = nc.dram_tensor("dbg_xbuf", [128, 2048], F32,
                             kind="ExternalOutput")
        p.dma("sync", dd6[:, :], A["xbuf1"][:, :], waits=[p.now(sv)], inc=dC)
        return finish()

    p.barrier()

    for i in range(2):
        p.op("vector", "memset", A[f"KTb{i}"][64:128, :], 0.0, inc=sv)
        p.op("vector", "memset", A[f"QTb{i}"][64:128, :], 0.0, inc=sv)
        for base in (64, 132):
            p.op("vector", "memset",
                 bass.AP(A[f"Vb{i}"], base, [[64 * 136 + 32, 128], [136, 64],
                                             [1, 4]]),
                 1.0, inc=sv)
    init_tok = p.now(sv)
    ones_toks = [p.dma("sync", A[f"KTb{i}"][96:97, :], d["ones_8k"][:, :],
                       waits=[init_tok], inc=dC) for i in range(2)]
    ones_tok = ones_toks[-1]

    with ExitStack() as pctx:
        psKC = [pctx.enter_context(nc.psum_tensor(f"psKC{i}", [128, 512], F32))
                for i in range(2)]
        psVD = [pctx.enter_context(nc.psum_tensor(f"psVD{i}", [128, 512], F32))
                for i in range(2)]
        psQC = [pctx.enter_context(nc.psum_tensor(f"psQC{i}", [128, 512], F32))
                for i in range(2)]
        kc_rd = [None, None]; vd_rd = [None, None]; qc_rd = [None, None]
        buf_free = [None, None]
        vbuf_free = [None, None]

        for h in range(NH):
            par = h % 2
            KTb, QTb = A[f"KTb{par}"], A[f"QTb{par}"]
            ev = None
            for n in range(16):
                pp = n % 2
                for kk in range(2):
                    t_mm = p.op("tensor", "matmul", psKC[pp][0:64, 0:512],
                                A["wkdec"][:, kk * D + h * 64:
                                           kk * D + h * 64 + 64],
                                A["kv_normT"][:, kk * T + n * 512:
                                              kk * T + n * 512 + 512],
                                start=(kk == 0), stop=(kk == 1),
                                waits=[wA2_tok, kc_rd[pp], buf_free[par],
                                       ones_tok],
                                inc=(st if kk else None))
                ev = p.op("vector", "tensor_copy",
                          KTb[0:64, n * 512:n * 512 + 512],
                          psKC[pp][0:64, 0:512], waits=[t_mm], inc=sv)
                kc_rd[pp] = ev
            t_rope = p.dma("sync", KTb[64:96, :],
                           kro_d[h * 32:h * 32 + 32, :],
                           waits=[buf_free[par]], inc=dRp[par])
            t_store = p.dma("scalar", kall_d[h], KTb[:, :],
                            waits=[ev, t_rope], inc=dSt[par])
            if h % 2 == 0:
                vpar = (h // 2) % 2
                pr = h // 2
                Vb = A[f"Vb{vpar}"]
                vev = None
                for ng in range(32):
                    pp = ng % 2
                    for sub in range(2):
                        n = ng * 2 + sub
                        for kk in range(2):
                            t_mm = p.op("tensor", "matmul",
                                        psVD[pp][:, sub * 128:sub * 128 + 128],
                                        A["kv_normT"][:, kk * T + n * 128:
                                                      kk * T + n * 128 + 128],
                                        A["wvdec"][:, kk * D + pr * 128:
                                                   kk * D + pr * 128 + 128],
                                        start=(kk == 0), stop=(kk == 1),
                                        waits=[vd_rd[pp], vbuf_free[vpar],
                                               init_tok],
                                        inc=(st if (kk and sub) else None))
                    vev = p.op("vector", "tensor_copy",
                               bass.AP(Vb, ng * 272,
                                       [[64 * 136 + 32, 128], [136, 2],
                                        [68, 2], [1, 64]]),
                               bass.AP(psVD[pp], 0,
                                       [[512, 128], [128, 2], [64, 2],
                                        [1, 64]]),
                               waits=[t_mm], inc=sv)
                    vd_rd[pp] = vev
                vbuf_free[vpar] = p.dma("scalar", vall_d[pr],
                                        Vb[:, 0:64 * 136],
                                        waits=[vev], inc=dSt[par])
            qev = None
            for n in range(4):
                pp = n % 2
                for kq in range(4):
                    t_mm = p.op("tensor", "matmul", psQC[pp][0:64, 0:512],
                                A["wqdec"][:, kq * D + h * 64:
                                           kq * D + h * 64 + 64],
                                A["q_cT"][:, kq * TPC + n * 512:
                                          kq * TPC + n * 512 + 512],
                                start=(kq == 0), stop=(kq == 3),
                                waits=[qc_rd[pp], buf_free[par]],
                                inc=(st if kq == 3 else None))
                qev = p.op("vector", "tensor_scalar",
                           QTb[0:64, n * 512:n * 512 + 512],
                           psQC[pp][0:64, 0:512], SCALE, None, ALU.mult,
                           waits=[t_mm], inc=sv)
                qc_rd[pp] = qev
            t_qrope = p.dma("gpsimd", QTb[64:96, :],
                            A["q_ropeT"][32 * (h % 4):32 * (h % 4) + 32,
                                         (h // 4) * TPC:(h // 4 + 1) * TPC],
                            waits=[buf_free[par], init_tok], inc=dQr[par])
            buf_free[par] = p.dma("scalar", qall_d[h], QTb[:, :],
                                  waits=[qev, t_qrope], inc=dSt[par])

    if dev == 2:
        return finish()

    p.barrier()


    with ExitStack() as pctx:
        psW = [pctx.enter_context(nc.psum_tensor(f"psW{i}", [128, 1024], F32))
               for i in range(2)]
        psP = [pctx.enter_context(nc.psum_tensor(f"psP{i}", [128, 512], F32))
               for i in range(2)]
        psV = pctx.enter_context(nc.psum_tensor("psV", [128, 512], F32))
        psTb = pctx.enter_context(nc.psum_tensor("psTb", [128, 1024], BF16))

        kt_load = [None, None]; v_load = [None, None]; qt_load = [None, None]
        kt_free = [None, None]; v_free = [None, None]; qt_free = [None, None]
        psW_rd = [None, None]; psP_rd = [None, None]
        psV_rd = [None, None]
        probs_free = [None, None]
        mask_free = [None, None]
        masktmp_free = None
        psTb_rd = None
        ones_init = None
        rlB_free = None
        mrow_busy = [None]

        vpad_init = [p.op("vector", "memset", A[f"vB{i}"][:, 8704:8736],
                          0.0, inc=sv) for i in range(2)]

        def load_head(h):
            par = h % 2
            kt_load[par] = p.dma("sync", A[f"ktB{par}"][:, :], kall_d[h],
                                 waits=[kt_free[par]], inc=dK[par])
            qt_load[par] = p.dma("sync", A[f"qtB{par}"][:, :], qall_d[h],
                                 waits=[qt_free[par]], inc=dQ[par])
            if h % 2 == 0:
                vp = (h // 2) % 2
                v_load[vp] = p.dma("sync", A[f"vB{vp}"][:, 0:8704],
                                   vall_d[h // 2],
                                   waits=[v_free[vp], vpad_init[vp]],
                                   inc=dV[vp])

        def pass1(h):
            nonlocal psTb_rd
            par = h % 2
            ktB, qtB = A[f"ktB{par}"], A[f"qtB{par}"]
            red_toks = []
            pp = 0
            for j in range(NJ):
                E = _E(j)
                n = E * 4
                ps = psP[pp]
                t_mm = p.op("tensor", "matmul", ps[:, 0:n],
                            qtB[:, j * 128:j * 128 + 128],
                            bass.AP(ktB, 0, [[T, 128], [32, n]]),
                            start=True, stop=True,
                            waits=[kt_load[par], qt_load[par], psP_rd[pp],
                                   psV_rd[1] if pp == 0 else None], inc=st)
                t_r = p.op("vector", "tensor_reduce",
                           A["mcols"][:, j:j + 1], ps[:, 0:n],
                           AX.X, ALU.max, waits=[t_mm], inc=sv)
                psP_rd[pp] = t_r
                pp ^= 1
                red_toks.append(t_r)
            t_n = p.op("gpsimd", "tensor_scalar", A["mneg"][:, :],
                       A["mcols"][:, :], -1.0, -MARGIN, ALU.mult, ALU.add,
                       waits=red_toks, inc=sp)
            t_t = p.op("tensor", "transpose", psTb[0:16, 0:128],
                       A["mneg"][:, :], A["c_eye_bf"][:, :],
                       waits=[t_n, psTb_rd], inc=st)
            t_e = p.op("vector", "tensor_copy", A["mrow"][:, :],
                       psTb[0:16, 0:128], waits=[t_t], inc=sv)
            psTb_rd = t_e
            t_d1 = p.dma("gpsimd",
                         mrow_d[0:1, :].rearrange("p (j q) -> p j q", j=16),
                         A["mrow"][:, :], waits=[t_e, mrow_busy[0]],
                         inc=dMr)
            t_d2 = p.dma("gpsimd", qtB[96:97, :], mrow_d[0:1, :],
                         waits=[t_d1], inc=dMr)
            mrow_busy[0] = t_d2
            return t_d2

        def pass2(h, row_tok):
            nonlocal psV_rd, masktmp_free, ones_init
            par = h % 2
            vp = (h // 2) % 2
            voff = (h % 2) * 68
            ktB, qtB = A[f"ktB{par}"], A[f"qtB{par}"]
            vB = A[f"vB{vp}"]
            if ones_init is None:
                ones_init = p.op("vector", "memset", A["ones64"][:, :], 1.0,
                                 inc=sv)
            bpar = 0
            pend = []
            rl_busy = [None]
            pend_fin = [None]

            def do_finalize(pend_fin):
                if pend_fin[0] is None:
                    return
                fpj, flast = pend_fin[0]
                pend_fin[0] = None
                psv = psV if fpj % 2 == 0 else psP[0]
                t_rl = p.op("vector", "reciprocal", A["rl64"][64:65, 0:256],
                            psv[64:65, 0:256],
                            waits=[flast, ones_init, rl_busy[0]], inc=sv)
                t_bc = p.op("tensor", "matmul", psP[1][0:64, 0:256],
                            A["ones64"][64:65, 0:64],
                            A["rl64"][64:65, 0:256],
                            start=True, stop=True,
                            waits=[t_rl, psP_rd[1]], inc=st)
                rl_busy[0] = t_bc
                t_rb2 = p.op("vector", "tensor_copy", A["rlB"][0:64, 0:256],
                             psP[1][0:64, 0:256], waits=[t_bc], inc=sv)
                psP_rd[1] = t_rb2
                t_o = p.op("vector", "tensor_tensor",
                           A["attn_o64"][0:64, h * TPC + fpj * 256:
                                         h * TPC + fpj * 256 + 256],
                           psv[0:64, 0:256], A["rlB"][0:64, 0:256],
                           ALU.mult, waits=[t_rb2], inc=sv)
                psV_rd[fpj % 2] = t_o
                if fpj % 2 == 1:
                    psP_rd[0] = t_o

            def flush_pv(pend):
                if not pend:
                    return None
                bk_, prb_, t_e_, E0_, vp_, voff_, pvb_, bp_, w_ = pend.pop(0)
                psv = psV if pvb_ == 0 else psP[0]
                lp = None
                for i, kt in enumerate(bk_):
                    lp = p.op(
                        "tensor", "matmul", psv[0:96, 0:w_],
                        A[f"vB{vp_}"][:, kt * 136 + voff_:
                                      kt * 136 + voff_ + 96],
                        prb_[:, i * w_:i * w_ + w_],
                        start=(kt == 0), stop=(kt == E0_ - 1),
                        waits=[t_e_, psV_rd[pvb_] if kt == 0 else None,
                               psP_rd[0] if (pvb_ == 1 and kt == 0) else None,
                               v_load[vp_]],
                        inc=(st if kt == E0_ - 1 else None))
                probs_free[bp_] = lp
                return lp

            for pj in range(NJ // 2):
                E0 = _E(2 * pj)
                mpar = pj % 2
                t_ml = p.dma("sync",
                             A[f"maskb{mpar}"][:, :].rearrange(
                                 "p (k q) -> p k q", k=8),
                             d["maskT"][pj].rearrange("k p q -> p k q"),
                             waits=[mask_free[mpar]], inc=dM[mpar])
                nb = E0 // 4
                last_pv = None
                for bi in range(nb):
                    bk = list(range(bi * 4, bi * 4 + 4))
                    lastb = (bi == nb - 1)
                    w = 128 if lastb else 256
                    ps = psW[bpar]
                    for i, kt in enumerate(bk):
                        tok = p.op("tensor", "matmul",
                                   ps[:, i * w:i * w + w],
                                   ktB[:, kt * 128:kt * 128 + 128],
                                   qtB[:, pj * 256:pj * 256 + w],
                                   start=True, stop=True,
                                   waits=[psW_rd[bpar], row_tok,
                                          kt_load[par]],
                                   inc=(st if i == 3 else None))
                    mm_tok = tok
                    if len(pend) >= 2:
                        fl = flush_pv(pend)
                        if fl is not None:
                            last_pv = fl
                    if bi == 1:
                        do_finalize(pend_fin)
                    prb = A[f"probsT{bpar}"]
                    if bi < nb - 2:
                        t_e = p.op("scalar", "activation", prb[:, :],
                                   ps[:, :], AF.Exp,
                                   waits=[mm_tok, probs_free[bpar]], inc=ss)
                    elif not lastb:
                        t_a = p.op("vector", "tensor_add",
                                   A["masktmp"][:, :], ps[:, :],
                                   A[f"maskb{mpar}"][:, 0:1024],
                                   waits=[mm_tok, t_ml, masktmp_free],
                                   inc=sv)
                        t_e = p.op("scalar", "activation", prb[:, :],
                                   A["masktmp"][:, :], AF.Exp,
                                   waits=[t_a, probs_free[bpar]], inc=ss)
                        masktmp_free = t_e
                    else:
                        t_a = p.op("vector", "tensor_add",
                                   bass.AP(A["masktmp"], 0,
                                           [[1024, 128], [128, 4], [1, 128]]),
                                   bass.AP(ps, 0,
                                           [[1024, 128], [128, 4], [1, 128]]),
                                   bass.AP(A[f"maskb{mpar}"], 1024,
                                           [[2048, 128], [256, 4], [1, 128]]),
                                   waits=[mm_tok, t_ml, masktmp_free],
                                   inc=sv)
                        t_e = p.op("scalar", "activation", prb[:, 0:512],
                                   A["masktmp"][:, 0:512], AF.Exp,
                                   waits=[t_a, probs_free[bpar]], inc=ss)
                        masktmp_free = t_e
                        mask_free[mpar] = t_e
                    psW_rd[bpar] = t_e
                    pend.append((bk, prb, t_e, E0, vp, voff, pj % 2, bpar, w))
                    bpar ^= 1
                while pend:
                    fl = flush_pv(pend)
                    if fl is not None:
                        last_pv = fl
                pend_fin[0] = (pj, last_pv)
                if pj == NJ // 2 - 1:
                    do_finalize(pend_fin)
            kt_free[par] = p.now(st)
            qt_free[par] = p.now(st)
            if h % 2 == 1:
                v_free[vp] = p.now(st)

        horder = list(range(NH))
        if dev == 31:
            horder = [2, 3, 0, 1] + list(range(4, NH))
        load_head(horder[0])
        load_head(horder[1])
        for hi in range(NH):
            h = horder[hi]
            rt = pass1(h)
            pass2(h, rt)
            if hi + 2 < NH:
                load_head(horder[hi + 2])

    if dev in (3, 31):
        dd = nc.dram_tensor("dbg_ao", [64, NH * TPC], BF16,
                            kind="ExternalOutput")
        p.dma("sync", dd[:, :], A["attn_o64"][0:64, :],
              waits=[p.now(sv), p.now(st), p.now(ss)], inc=dC)
        return finish()

    p.barrier()

    p.dma("gpsimd", A["wo"][:, :].rearrange("p (h n) -> p h n", h=16),
          d["w_o"].rearrange("(h p) n -> p h n", p=64), inc=dW)
    wo_tok = p.now(dW)
    with ExitStack() as pctx:
        psY = [pctx.enter_context(nc.psum_tensor(f"psY{i}", [128, 1024], F32))
               for i in range(2)]
        psT2 = [pctx.enter_context(nc.psum_tensor(f"psT2{i}", [128, 1024],
                                                  BF16)) for i in range(2)]
        psRt = pctx.enter_context(nc.psum_tensor("psRt", [128, 512], F32))
        psCw = pctx.enter_context(nc.psum_tensor("psCw", [128, 512], F32))

        xq_free = [None, None]; psY_rd = [None, None]
        psT2_rd = [None, None]; ys_free = [None, None]
        setfree2 = [None] * 4
        for t in range(NJ):
            par = t % 2
            t_x = p.dma("sync", A[f"xq{par}"][:, :],
                        d["x_q"][t * 128:(t + 1) * 128, :],
                        waits=[xq_free[par]], inc=dXQ[par])
            for half in range(2):
                for hh in range(NH):
                    t_mm = p.op("tensor", "matmul",
                                psY[par][:, half * 512:half * 512 + 512],
                                A["attn_o64"][0:64, hh * TPC + t * 128:
                                              hh * TPC + t * 128 + 128],
                                A["wo"][0:64, hh * 1024 + half * 512:
                                        hh * 1024 + half * 512 + 512],
                                start=(hh == 0), stop=(hh == NH - 1),
                                waits=[wo_tok, psY_rd[par]],
                                inc=(st if hh == NH - 1 else None))
            yb = A[f"yb{par}"]
            t_y = p.op("vector", "tensor_add", yb[:, :], psY[par][:, :],
                       A[f"xq{par}"][:, :],
                       waits=[t_mm, t_x, ys_free[par]], inc=sv)
            psY_rd[par] = t_y
            lb2 = (t % 4) * 8
            t_r0 = p.op("scalar", "activation", A["ylnb"][:, :], yb[:, :],
                        AF.Copy, accum_out=A["lnst2"][:, lb2 + 0:lb2 + 1],
                        waits=[t_y, setfree2[t % 4]], inc=ss)
            t_sq = p.op("scalar", "activation", A[f"xq{par}"][:, :], yb[:, :],
                        AF.Square, accum_out=A["lnst2"][:, lb2 + 1:lb2 + 2],
                        waits=[t_y, t_r0], inc=ss)
            xq_free[par] = t_sq
            t_r1 = t_sq
            t_ms = p.op("gpsimd", "tensor_scalar",
                        A["lnst2"][:, lb2 + 2:lb2 + 4],
                        A["lnst2"][:, lb2 + 0:lb2 + 2], 1.0 / D, None,
                        ALU.mult, waits=[t_r0, t_r1, setfree2[t % 4]],
                        inc=sp)
            p.op("gpsimd", "tensor_tensor", A["lnst2"][:, lb2 + 4:lb2 + 5],
                 A["lnst2"][:, lb2 + 2:lb2 + 3], A["lnst2"][:, lb2 + 2:lb2 + 3], ALU.mult)
            t_ve = p.op("gpsimd", "tensor_scalar", A["lnst2"][:, lb2 + 5:lb2 + 6],
                        A["lnst2"][:, lb2 + 3:lb2 + 4], A["lnst2"][:, lb2 + 4:lb2 + 5], EPS_LN,
                        ALU.subtract, ALU.add, inc=sp)
            t_s2 = p.op("scalar", "activation", A["lnst2"][:, lb2 + 6:lb2 + 7],
                        A["lnst2"][:, lb2 + 5:lb2 + 6], AF.Sqrt,
                        waits=[t_ve], inc=ss)
            t_rc2 = p.op("vector", "reciprocal", A["lnst2"][:, lb2 + 7:lb2 + 8],
                         A["lnst2"][:, lb2 + 6:lb2 + 7], waits=[t_s2], inc=sv)
            t_cp2 = p.op("gpsimd", "tensor_copy", A["lnst2"][:, lb2 + 5:lb2 + 6],
                         A["lnst2"][:, lb2 + 7:lb2 + 8], waits=[t_rc2], inc=sp)
            t_ln = p.op("vector", "tensor_scalar", A["ylnb2"][:, :],
                        A["ylnb"][:, :],
                        A["lnst2"][:, lb2 + 2:lb2 + 3], A["lnst2"][:, lb2 + 5:lb2 + 6],
                        ALU.subtract, ALU.mult,
                        waits=[t_cp2, t_ms, t_r0], inc=sv)
            setfree2[t % 4] = t_ln
            t_ys = p.dma("scalar", ys_d[t * 128:(t + 1) * 128, :], yb[:, :],
                         waits=[t_y, t_ln], inc=dYs[par])
            ys_free[par] = t_ys
            tp = t % 2
            for dk in range(8):
                t_t = p.op("tensor", "transpose",
                           psT2[tp][:, dk * 128:dk * 128 + 128],
                           A["ylnb2"][:, dk * 128:dk * 128 + 128],
                           A["c_eye_bf"][:, :],
                           waits=[t_ln, psT2_rd[tp]],
                           inc=(st if dk == 7 else None))
            t_e = p.op("vector", "tensor_copy",
                       bass.AP(A["y_lnT"], t * 128,
                               [[8 * TPC, 128], [TPC, 8], [1, 128]]),
                       psT2[tp][:, :].rearrange("p (k q) -> p k q", k=8),
                       waits=[t_t], inc=sv)
            psT2_rd[tp] = t_e
        ylnT_tok = p.now(sv)
        t_rb = p.dma("sync", A["lgt"][0:1, 0:8], d["rb"][:, :], inc=dC)
        t_rbb = p.op("tensor", "matmul", psRt[:, 128:136],
                     A["c_ones_row"][:, :], A["lgt"][0:1, 0:8],
                     start=True, stop=True, waits=[t_rb, consts_tok], inc=st)
        t_rbe = p.op("scalar", "copy", A["rbB"][:, :],
                     psRt[:, 128:136], waits=[t_rbb], inc=ss)
        t_wr = p.dma("gpsimd",
                     A["wrt"][:, :].rearrange("p (k e) -> p k e", k=8),
                     d["wr"].rearrange("(k p) e -> p k e", p=128), inc=dW)
        prev_l = None
        sig_toks = []
        for t in range(NJ):
            for dk in range(8):
                t_mm = p.op("tensor", "matmul", psRt[:, 0:8],
                            A["y_lnT"][:, dk * TPC + t * 128:
                                       dk * TPC + t * 128 + 128],
                            A["wrt"][:, dk * 8:dk * 8 + 8],
                            start=(dk == 0), stop=(dk == 7),
                            waits=[ylnT_tok, t_wr, prev_l, t_rbe],
                            inc=(st if dk == 7 else None))
            t_l = p.op("vector", "tensor_add", A["lgt"][:, 0:8],
                       psRt[:, 0:8], A["rbB"][:, :],
                       waits=[t_mm, t_rbe,
                              sig_toks[-1] if sig_toks else None], inc=sv)
            prev_l = t_l
            t_sg = p.op("scalar", "activation",
                        A["cwbuf"][:, t * 8:t * 8 + 8], A["lgt"][:, 0:8],
                        AF.Sigmoid, waits=[t_l], inc=ss)
            sig_toks.append(t_sg)

        def sl_e(e):
            return bass.AP(A["cwbuf"], e, [[128, 128], [8, 16]])

        sig_tok = p.now(ss)
        probs3 = bass.AP(A["cwbuf"], 0, [[128, 128], [8, 16], [1, 8]])
        t_m1 = p.op("vector", "tensor_reduce", A["r_m1"][:, :], probs3,
                    AX.X, ALU.max, waits=[sig_tok], inc=sv)
        t_m1c = p.op("scalar", "copy", A["r_m2"][:, :], A["r_m1"][:, :],
                     waits=[t_m1], inc=ss)
        t_ge = None
        for t in range(NJ):
            t_ge = p.op("vector", "tensor_scalar",
                        A["r_ge"][:, t * 1:t + 1] if False else
                        A["cw"][:, t * 8:t * 8 + 8],
                        A["cwbuf"][:, t * 8:t * 8 + 8],
                        A["r_m2"][:, t:t + 1], None, ALU.is_ge,
                        waits=[t_m1c], inc=sv)
        t_gec = p.op("scalar", "copy", A["cwT"][:, :], A["cw"][:, :],
                     waits=[t_ge], inc=ss)
        t_msk = p.op("vector", "scalar_tensor_tensor", A["cw"][:, :],
                     A["cwT"][:, :], -1e30, A["cwbuf"][:, :],
                     ALU.mult, ALU.add, waits=[t_gec], inc=sv)
        t_mskc = p.op("scalar", "copy", A["cwT"][:, :], A["cw"][:, :],
                      waits=[t_msk], inc=ss)
        masked3 = bass.AP(A["cwT"], 0, [[128, 128], [8, 16], [1, 8]])
        t_m2 = p.op("vector", "tensor_reduce", A["r_m2"][:, :], masked3,
                    AX.X, ALU.max, waits=[t_mskc], inc=sv)
        t_m2c = p.op("scalar", "copy", A["r_ge"][:, :], A["r_m2"][:, :],
                     waits=[t_m2], inc=ss)
        t_ge2 = None
        for t in range(NJ):
            t_ge2 = p.op("vector", "tensor_scalar",
                         A["cw"][:, t * 8:t * 8 + 8],
                         A["cwbuf"][:, t * 8:t * 8 + 8],
                         A["r_ge"][:, t:t + 1], None, ALU.is_ge,
                         waits=[t_m2c], inc=sv)
        t_gc2 = p.op("scalar", "copy", A["cwT"][:, :], A["cw"][:, :],
                     waits=[t_ge2], inc=ss)
        t_cw = p.op("vector", "tensor_mul", A["cw"][:, :], A["cwT"][:, :],
                    A["cwbuf"][:, :], waits=[t_gc2], inc=sv)
        cw_toks = [t_cw]
        t_ct = p.op("tensor", "transpose", psCw[:, 0:128], A["cw"][:, :],
                    A["c_eye_f"][:, :], waits=cw_toks + [consts_tok], inc=st)
        t_ce = p.op("vector", "tensor_copy", A["cwT"][:, :], psCw[:, 0:128],
                    waits=[t_ct], inc=sv)
        cwT_dma = p.dma("scalar", cwT_d[:, :], A["cwT"][:, :],
                        waits=[t_ce], inc=dYs[0])

    if dev == 4:
        dd = nc.dram_tensor("dbg_ylnT", [128, 8 * TPC], BF16,
                            kind="ExternalOutput")
        p.dma("sync", dd[:, :], A["y_lnT"][:, :],
              waits=[p.now(sv), p.now(ss), p.now(st), p.now(dYs[0]),
                     p.now(dYs[1])], inc=dC)
        dd2 = nc.dram_tensor("dbg_cw", [128, 128], F32, kind="ExternalOutput")
        p.dma("sync", dd2[:, :], A["cw"][:, :], waits=[p.now(sv)], inc=dC)
        return finish()

    p.barrier()

    with ExitStack() as pctx:
        psG = pctx.enter_context(nc.psum_tensor("psG", [128, 1024], F32))
        psU = pctx.enter_context(nc.psum_tensor("psU", [128, 1024], F32))
        psO = [pctx.enter_context(nc.psum_tensor(f"psO{i}", [128, 512], F32))
               for i in range(2)]

        ring_free = [None] * RING
        gtile = [0]

        def ring_load(src_ap, kt):
            slot = gtile[0] % RING
            gtile[0] += 1
            ncol = src_ap.shape[1] * src_ap.shape[2]
            ds = S[f"ring{slot}"][:, 0:ncol].rearrange(
                "p (k m) -> p k m", k=kt)
            tok = p.dma("gpsimd", ds, src_ap,
                        waits=[ring_free[slot]], inc=dR[slot])
            return slot, tok

        psG_rd = [None, None]; psU_rd = [None, None]
        psO_rd = [None, None]
        cw_tok = cw_toks[-1]
        yt_free = [None, None]
        tf_prev = {}
        act_guard = [None]

        for chunk in range(2):
            coff = chunk * 1024
            acc_toks = {}
            for e in range(NE + 1):
                routed = e > 0
                if routed:
                    wg_src, wu_src, wd_src = (d["wg"][e - 1], d["wu"][e - 1],
                                              d["wd"][e - 1])
                else:
                    wg_src, wu_src, wd_src = d["wg_s"], d["wu_s"], d["wd_s"]
                act_toks = []
                for q in range(4):
                    sg, tg = ring_load(
                        wg_src[:, q * 512:(q + 1) * 512].rearrange(
                            "(k p) m -> p k m", p=128), 8)
                    su, tu = ring_load(
                        wu_src[:, q * 512:(q + 1) * 512].rearrange(
                            "(k p) m -> p k m", p=128), 8)
                    t_g = t_u = None
                    for hi in range(4):
                        ht = q * 4 + hi
                        for sub in range(2):
                            i = (ht * 2 + sub) % 2
                            for dk in range(8):
                                t_g = p.op(
                                    "tensor", "matmul",
                                    psG[:, i * 512:i * 512 + 512],
                                    S[f"ring{sg}"][:, dk * 512 + hi * 128:
                                                   dk * 512 + hi * 128 + 128],
                                    A["y_lnT"][:, dk * TPC + coff + sub * 512:
                                               dk * TPC + coff + sub * 512
                                               + 512],
                                    start=(dk == 0), stop=(dk == 7),
                                    waits=[tg, psG_rd[i]],
                                    inc=(st if dk == 7 else None))
                            for dk in range(8):
                                t_u = p.op(
                                    "tensor", "matmul",
                                    psU[:, i * 512:i * 512 + 512],
                                    S[f"ring{su}"][:, dk * 512 + hi * 128:
                                                   dk * 512 + hi * 128 + 128],
                                    A["y_lnT"][:, dk * TPC + coff + sub * 512:
                                               dk * TPC + coff + sub * 512
                                               + 512],
                                    start=(dk == 0), stop=(dk == 7),
                                    waits=[tu, psU_rd[i]],
                                    inc=(st if dk == 7 else None))
                            t_si = p.op("scalar", "activation",
                                        A["sil"][:, i * 512:i * 512 + 512],
                                        psG[:, i * 512:i * 512 + 512],
                                        AF.Silu, waits=[t_g, psU_rd[i]],
                                        inc=ss)
                            psG_rd[i] = t_si
                            dst = A["actT"][:, ht * 1024 + sub * 512:
                                            ht * 1024 + sub * 512 + 512]
                            t_ac = p.op("vector", "scalar_tensor_tensor",
                                        dst,
                                        psU[:, i * 512:i * 512 + 512],
                                        1.0,
                                        A["sil"][:, i * 512:i * 512 + 512],
                                        ALU.mult, ALU.mult,
                                        waits=[t_u, t_si, act_guard[0]],
                                        inc=sv)
                            psU_rd[i] = t_ac
                            act_toks.append(t_ac)
                    ring_free[sg] = t_g
                    ring_free[su] = t_u
                wd_slots = []
                for q in range(4):
                    wd_slots.append(ring_load(
                        wd_src[q * 512:(q + 1) * 512, :].rearrange(
                            "(k p) m -> p k m", p=128), 4))
                act_tok = act_toks[-1]
                for tt in range(8):
                    for half in range(2):
                        i = (tt * 2 + half) % 2
                        t_o = None
                        for ht in range(16):
                            sd, td = wd_slots[ht // 4]
                            t_o = p.op(
                                "tensor", "matmul", psO[i][:, 0:512],
                                A["actT"][:, ht * 1024 + tt * 128:
                                          ht * 1024 + tt * 128 + 128],
                                S[f"ring{sd}"][:, (ht % 4) * 1024
                                               + half * 512:
                                               (ht % 4) * 1024 + half * 512
                                               + 512],
                                start=(ht == 0), stop=(ht == 15),
                                waits=[td, act_tok, psO_rd[i]],
                                inc=(st if ht == 15 else None))
                        sl = A["out_acc"][:, tt * 1024 + half * 512:
                                          tt * 1024 + half * 512 + 512]
                        if e == 0:
                            t_a = p.op("vector", "tensor_copy", sl,
                                       psO[i][:, 0:512],
                                       waits=[t_o, tf_prev.get(tt)], inc=sv)
                        else:
                            gcol = (chunk * 8 + tt) * 8 + (e - 1)
                            t_a = p.op("vector", "scalar_tensor_tensor", sl,
                                       psO[i][:, 0:512],
                                       A["cw"][:, gcol:gcol + 1], sl,
                                       ALU.mult, ALU.add,
                                       waits=[t_o, acc_toks[(tt, half)],
                                              cw_tok], inc=sv)
                        psO_rd[i] = t_a
                        acc_toks[(tt, half)] = t_a
                for sd, td in wd_slots:
                    ring_free[sd] = p.now(st)
            for tt in range(8):
                par = tt % 2
                gt = chunk * 8 + tt
                t_y = p.dma("sync", A[f"ytile{par}"][:, :],
                            ys_d[gt * 128:(gt + 1) * 128, :],
                            waits=[yt_free[par], p.now(dYs[0]),
                                   p.now(dYs[1])], inc=dYt[par])
                t_f = p.op("vector", "tensor_add", A[f"ytile{par}"][:, :],
                           A[f"ytile{par}"][:, :],
                           bass.AP(A["out_acc"], tt * 1024,
                                   [[8192, 128], [1, 1024]]),
                           waits=[t_y, acc_toks[(tt, 0)],
                                  acc_toks[(tt, 1)]], inc=sv)
                t_od = p.dma("sync", out_d[gt], A[f"ytile{par}"][:, :],
                             waits=[t_f], inc=dOut[par])
                yt_free[par] = t_od
                tf_prev[tt] = t_f
            act_guard[0] = p.now(st)

    return finish()


def kernel(**inputs):
    in_maps = _host_prep(inputs)
    nc = _build()
    res = run_bass_kernel_spmd(nc, in_maps, core_ids=list(range(8)))
    y = np.zeros((B, T, D), np.float32)
    for core in range(8):
        b, r = core // 4, core % 4
        o = np.asarray(res.results[core]["out"])
        for j in range(NJ):
            q = _Q(r, j)
            y[b, q * 128:(q + 1) * 128] = o[j]
    return y


# version 15
# speedup vs baseline: 1.0943x; 1.0039x over previous
import numpy as np
import ml_dtypes
import concourse.bass as bass
import concourse.mybir as mybir
from concourse.bass_utils import run_bass_kernel_spmd
from contextlib import ExitStack

F32 = mybir.dt.float32
BF16 = mybir.dt.bfloat16
AF = mybir.ActivationFunctionType
ALU = mybir.AluOpType
AX = mybir.AxisListType
BF = ml_dtypes.bfloat16

B, T, D = 2, 8192, 1024
NH, HD, RD = 16, 64, 32
KVR, QR = 256, 512
FFI, NE = 2048, 8
NJ = 16
TPC = NJ * 128
SCALE = float(96.0 ** -0.5)
MARGIN = 6.0
EPS_LN, EPS_RMS = 1e-5, 1e-6
NKC = 32
NQC = 8
RING = 7

ENGINES = ("sync", "gpsimd", "vector", "scalar", "tensor")


def _Q(r, j):
    return 63 - 4 * j - r


def _E(j):
    return 64 - 4 * j


def _rope_tables():
    inv = 1.0 / 10000.0 ** (np.arange(0, RD, 2, dtype=np.float32) / RD)
    t = np.arange(T, dtype=np.float32)
    fr = np.outer(t, inv)
    emb = np.concatenate([fr, fr], -1)
    cos = np.cos(emb).astype(np.float32)
    sin = np.sin(emb).astype(np.float32)
    dd = np.arange(NH * RD) % RD
    sign = np.where(dd < RD // 2, -1.0, 1.0).astype(np.float32)
    cosT = np.ascontiguousarray(cos[:, dd].T)
    ssinT = np.ascontiguousarray((sin[:, dd] * sign[None, :]).T)
    return cosT, ssinT


def _swap_perm(ncols):
    dd = np.arange(ncols)
    j = dd % RD
    return (dd // RD) * RD + (j + RD // 2) % RD


def _host_prep(inp):
    cosT, ssinT = _rope_tables()
    f32 = np.float32
    asc = np.ascontiguousarray
    W_kr = asc(inp["L0_attn_W_krope"], f32)
    W_qr = asc(inp["L0_attn_W_qrope"], f32)
    shared = dict(
        w_kv=asc(inp["L0_attn_W_kv"], f32),
        w_kdec=asc(inp["L0_attn_W_kdec"], f32),
        w_vdec=asc(inp["L0_attn_W_vdec"], f32),
        w_q=asc(inp["L0_attn_W_q"], f32),
        w_qdec=asc(inp["L0_attn_W_qdec"], f32),
        w_kr=W_kr, w_kr_sw=asc(W_kr[:, _swap_perm(NH * RD)]),
        w_qr=W_qr, w_qr_sw=asc(W_qr[:, _swap_perm(NH * RD)]),
        w_o=asc(inp["L0_attn_W_o"], f32),
        g_kv=asc(inp["L0_attn_g_kv"], f32).reshape(2, 128).T.copy(),
        kcos=cosT, kssin=ssinT,
        wg_s=asc(inp["L0_ffn_Wg_s"].T, f32),
        wu_s=asc(inp["L0_ffn_Wu_s"].T, f32),
        wd_s=asc(inp["L0_ffn_Wd_s"].T, f32),
        wg=asc(np.transpose(inp["L0_ffn_Wg"], (0, 2, 1)), f32),
        wu=asc(np.transpose(inp["L0_ffn_Wu"], (0, 2, 1)), f32),
        wd=asc(np.transpose(inp["L0_ffn_Wd"], (0, 2, 1)), f32),
        wr=asc(inp["L0_ffn_Wr"].T, f32),
        rb=asc(inp["L0_ffn_rb"], f32).reshape(1, NE),
        eye_bf=np.eye(128, dtype=BF),
        eye_f32=np.eye(128, dtype=f32),
        ones_row_f=np.ones((1, 128), f32),
        ones_col_f=np.ones((128, 1), f32),
        ones_8k=np.ones((1, T), BF),
    )
    x = np.asarray(inp["L0_attn_x"], f32)
    in_maps = []
    for core in range(8):
        b, r = core // 4, core % 4
        qt = [_Q(r, j) for j in range(NJ)]
        x_q = asc(np.concatenate([x[b, q * 128:(q + 1) * 128] for q in qt], 0))
        qpos = np.concatenate([np.arange(q * 128, (q + 1) * 128) for q in qt])
        mask = np.zeros((NJ // 2, 8, 128, 256), np.float32)
        for pj in range(NJ // 2):
            E0 = _E(2 * pj)
            for pp in range(8):
                k0 = (E0 - 8 + pp) * 128
                ki = k0 + np.arange(128)[:, None]
                for jj in range(2):
                    q0 = qt[2 * pj + jj] * 128
                    qi = q0 + np.arange(128)[None, :]
                    mask[pj, pp, :, jj * 128:(jj + 1) * 128] = np.where(
                        ki <= qi, 0.0, -30000.0)
        m = dict(shared)
        m.update(
            x_b=x[b], x_q=x_q,
            qcos=asc(cosT[:, qpos] * SCALE),
            qssin=asc(ssinT[:, qpos] * SCALE),
            maskT=mask.astype(BF),
        )
        in_maps.append(m)
    return in_maps


class P:
    def __init__(self, nc, ctx):
        self.nc, self.ctx = nc, ctx
        self.ev = {e: [] for e in ENGINES}
        self.cnt = {}
        self.sems = []

    def sem(self, name):
        s = self.ctx.enter_context(self.nc.semaphore(name))
        self.cnt[s] = 0
        self.sems.append(s)
        return s

    def add(self, eng, fn, waits=(), inc=None, n=1):
        w = [t for t in waits if t is not None]
        tok = None
        if inc is not None:
            self.cnt[inc] += n
            tok = (inc, self.cnt[inc])
        self.ev[eng].append((fn, w, inc, n))
        return tok

    def op(self, eng, meth, *args, waits=(), inc=None, n=1, **kw):
        def fn(e, m=meth, a=args, k=kw):
            return getattr(e, m)(*a, **k)
        return self.add(eng, fn, waits, inc, n)

    def dma(self, eng, out, in_, waits=(), inc=None):
        def fn(e, o=out, i=in_):
            return e.dma_start(out=o, in_=i)
        return self.add(eng, fn, waits, inc, 16)

    def now(self, sem):
        return (sem, self.cnt[sem])

    def barrier(self):
        toks = [self.now(s) for s in self.sems if s is not self._bar]
        for e in ENGINES:
            self.op(e, "nop", waits=toks, inc=self._bar, n=1)
        full = self.now(self._bar)
        for e in ENGINES:
            self.op(e, "nop", waits=[full])

    def emit(self):
        with self.nc.Block() as block:
            for eng in ENGINES:
                def body(e, name=eng):
                    seen = {}
                    for fn, waits, inc, n in self.ev[name]:
                        for (s, v) in waits:
                            if seen.get(s, -1) < v:
                                e.wait_ge(s, v)
                                seen[s] = v
                        inst = fn(e)
                        if inc is not None:
                            inst.then_inc(inc, n)
                getattr(block, eng)(body)


def _build(dev=99):
    nc = bass.Bass("TRN2", target_bir_lowering=False)
    d = {}

    def din(name, shape, dt=F32):
        d[name] = nc.dram_tensor(name, list(shape), dt, kind="ExternalInput")
        return d[name]

    def dint(name, shape, dt):
        kind = "ExternalOutput" if dev < 99 else "Internal"
        d[name] = nc.dram_tensor(name, list(shape), dt, kind=kind)
        return d[name]

    din("x_b", (T, D)); din("x_q", (TPC, D))
    din("w_kv", (D, KVR)); din("w_kdec", (KVR, D)); din("w_vdec", (KVR, D))
    din("w_q", (D, QR)); din("w_qdec", (QR, D))
    din("w_kr", (D, NH * RD)); din("w_kr_sw", (D, NH * RD))
    din("w_qr", (QR, NH * RD)); din("w_qr_sw", (QR, NH * RD))
    din("w_o", (D, D)); din("g_kv", (128, 2))
    din("kcos", (NH * RD, T)); din("kssin", (NH * RD, T))
    din("qcos", (NH * RD, TPC)); din("qssin", (NH * RD, TPC))
    din("maskT", (NJ // 2, 8, 128, 256), BF16)
    din("wg_s", (D, FFI)); din("wu_s", (D, FFI)); din("wd_s", (FFI, D))
    din("wg", (NE, D, FFI)); din("wu", (NE, D, FFI)); din("wd", (NE, FFI, D))
    din("wr", (D, NE)); din("rb", (1, NE))
    din("eye_bf", (128, 128), BF16); din("eye_f32", (128, 128))
    din("ones_row_f", (1, 128)); din("ones_col_f", (128, 1))
    din("ones_8k", (1, T), BF16)

    out_d = nc.dram_tensor("out", [NJ, 128, D], F32, kind="ExternalOutput")

    kro_d = dint("kro_s", (NH * RD, T), BF16)
    kall_d = dint("k_all", (NH, 128, T), BF16)
    vall_d = dint("v_all", (NH // 2, 128, 64 * 136), BF16)
    qall_d = dint("qt_all", (NH, 128, TPC), BF16)
    ys_d = dint("y_s", (TPC, D), F32)
    cwT_d = dint("cwT_s", (128, 128), F32)
    mrow_d = dint("mrow_s", (1, 2048), BF16)

    S = {}

    SBASE = 16512

    def sb(name, shape, dt, off):
        S[name] = nc.alloc_sbuf_tensor_at(name, list(shape), dt,
                                          offset=SBASE + off)
        return S[name]

    sb("c_eye_bf", (128, 128), BF16, 0)
    sb("c_eye_f", (128, 128), F32, 256)
    sb("c_ones_row", (1, 128), F32, 768)
    sb("c_ones_col", (128, 1), F32, 1280)
    sb("c_g", (128, 2), F32, 1312)
    sb("wrt", (128, 64), BF16, 1344)
    sb("cwbuf", (128, 128), F32, 2048)
    sb("cw", (128, 128), F32, 2560)
    sb("cwT", (128, 128), F32, 3072)
    sb("r_m1", (128, 16), F32, 3584)
    sb("r_m2", (128, 16), F32, 3648)
    sb("r_ge", (128, 16), F32, 3712)
    sb("rbB", (128, 8), F32, 3776)
    sb("kv_normT", (128, 2 * T), BF16, 4096)
    sb("q_cT", (128, 4 * TPC), BF16, 36864)
    sb("q_ropeT", (128, 4 * TPC), BF16, 53248)
    sb("attn_o64", (128, NH * TPC), BF16, 4096)
    sb("out_acc", (128, 8 * 1024), F32, 4096)
    sb("cwB", (128, 1024), F32, 36864)
    sb("ytile0", (128, 1024), F32, 40960)
    sb("ytile1", (128, 1024), F32, 45056)
    sb("sil", (128, 1024), BF16, 49152)
    sb("an", (128, 1024), F32, 51200)
    sb("cwrow", (1, 1024), F32, 55296)
    M0 = 69632
    o = M0
    for nm, cols, dt, npart in [
        ("wkv", 8 * KVR, BF16, 128), ("wkr", 8 * 512, BF16, 128),
        ("wkr_sw", 8 * 512, BF16, 128), ("wq", 8 * 512, BF16, 128),
        ("wqr", 4 * 512, BF16, 128), ("wqr_sw", 4 * 512, BF16, 128),
        ("xbuf0", 2048, F32, 128), ("xbuf1", 2048, F32, 128),
        ("xbuf2", 2048, F32, 128),
        ("xsq", 1024, F32, 128), ("xln", 2048, BF16, 128),
        ("xlnT0", 2048, BF16, 128), ("xlnT1", 2048, BF16, 128),
        ("tabc0", 1024, F32, 128), ("tabs0", 1024, F32, 128),
        ("tabc1", 1024, F32, 128), ("tabs1", 1024, F32, 128),
        ("kro0", 4096, BF16, 128), ("kro1", 4096, BF16, 128),
        ("ropet1", 1024, F32, 128), ("sqv", 512, F32, 128),
        ("rkv", 512, F32, 1), ("rkvB", 512, F32, 128), ("lnst", 32, F32, 128),
    ]:
        sz = cols * (2 if dt == BF16 else 4)
        sb(nm, (npart, cols), dt, o)
        o += (sz + 31) & ~31
    assert o <= 196352, ("A", o)
    sb("wkdec", (128, 2 * D), BF16, 196352)
    sb("wvdec", (128, 2 * D), BF16, 200448)
    sb("wqdec", (128, 4 * D), BF16, 204544)
    assert 204544 + 4 * D * 2 <= 212800
    o = M0
    for nm, cols in [
        ("KTb0", T), ("KTb1", T),
        ("Vb0", 64 * 136 + 32), ("Vb1", 64 * 136 + 32),
        ("QTb0", TPC), ("QTb1", TPC),
    ]:
        sb(nm, (128, cols), BF16, o)
        o += (cols * 2 + 31) & ~31
    assert o <= 212800, ("A2", o)
    o = M0
    for nm, cols, dt in [
        ("ktB0", T, BF16), ("ktB1", T, BF16),
        ("vB0", 64 * 136 + 32, BF16), ("vB1", 64 * 136 + 32, BF16),
        ("qtB0", TPC, BF16), ("qtB1", TPC, BF16),
        ("probsT0", 1024, BF16), ("probsT1", 1024, BF16),
        ("maskb0", 2048, BF16), ("maskb1", 2048, BF16),
        ("ones64", 64, F32), ("rl64", 256, F32), ("rlB", 256, F32),
        ("p1tmp", 8, F32), ("mcols", 16, F32), ("mneg", 16, BF16),
    ]:
        sz = cols * (2 if dt == BF16 else 4)
        sb(nm, (128, cols), dt, o)
        o += (sz + 31) & ~31
    sb("mrow", (16, 128), BF16, o); o += 256
    assert o <= 163584, ("B", o)
    sb("masktmp", (128, 1024), F32, 163584)
    o = M0
    for nm, cols, dt in [
        ("xq0", 1024, F32), ("xq1", 1024, F32),
        ("yb0", 1024, F32), ("yb1", 1024, F32),
        ("ylnb", 1024, F32), ("ylnb2", 1024, BF16),
        ("lnst2", 32, F32), ("lgt", 8, F32),
    ]:
        sz = cols * (2 if dt == BF16 else 4)
        sb(nm, (128, cols), dt, o)
        o += (sz + 31) & ~31
    assert o <= 163584, ("C", o)
    sb("y_lnT", (128, 8 * TPC), BF16, 163584)
    sb("wo", (64, 16 * 1024), BF16, 106496)
    sb("actT", (128, 16 * 1024), BF16, M0)
    for i in range(RING):
        sb(f"ring{i}", (128, 4096), BF16, M0 + 32768 + i * 8192)
    assert M0 + 32768 + RING * 8192 <= 163584

    ctx = ExitStack()
    p = P(nc, ctx)
    sv = p.sem("sv"); ss = p.sem("ss"); st = p.sem("st")
    sp = p.sem("sp")
    dC = p.sem("dC")
    dW = p.sem("dW")
    dX = [p.sem(f"dX{i}") for i in range(3)]
    dT = [p.sem(f"dT{i}") for i in range(2)]
    dKr = [p.sem(f"dKr{i}") for i in range(2)]
    dRp = [p.sem(f"dRp{i}") for i in range(2)]
    dQr = [p.sem(f"dQr{i}") for i in range(2)]
    dSt = [p.sem(f"dSt{i}") for i in range(2)]
    dK = [p.sem(f"dK{i}") for i in range(2)]
    dQ = [p.sem(f"dQ{i}") for i in range(2)]
    dV = [p.sem(f"dV{i}") for i in range(2)]
    dM = [p.sem(f"dM{i}") for i in range(2)]
    dMr = p.sem("dMr")
    dXQ = [p.sem(f"dXQ{i}") for i in range(2)]
    dYs = [p.sem(f"dYs{i}") for i in range(2)]
    dR = [p.sem(f"dR{i}") for i in range(RING)]
    dCw = p.sem("dCw")
    dYt = [p.sem(f"dYt{i}") for i in range(2)]
    dOut = [p.sem(f"dOut{i}") for i in range(2)]
    p._bar = p.sem("bar")
    A = S

    def finish():
        for e in ENGINES:
            p.op(e, "nop", waits=[p.now(s) for s in p.sems
                                  if s is not p._bar])
        p.emit()
        ctx.close()
        return nc

    p.dma("sync", A["c_eye_bf"][:, :], d["eye_bf"][:, :], inc=dC)
    p.dma("sync", A["c_eye_f"][:, :], d["eye_f32"][:, :], inc=dC)
    p.dma("sync", A["c_ones_row"][:, :], d["ones_row_f"][:, :], inc=dC)
    p.dma("sync", A["c_ones_col"][:, :], d["ones_col_f"][:, :], inc=dC)
    p.dma("sync", A["c_g"][:, :], d["g_kv"][:, :], inc=dC)
    consts_tok = p.now(dC)

    def wload(dst, src, inc=dW):
        kt = src.shape[0] // 128
        sr = src.rearrange("(k p) m -> p k m", p=128)
        ds = dst[:, :].rearrange("p (k m) -> p k m", k=kt)
        return p.dma("gpsimd", ds, sr, inc=inc)

    wload(A["wkv"], d["w_kv"])
    wload(A["wkr"], d["w_kr"])
    wload(A["wkr_sw"], d["w_kr_sw"])
    wload(A["wq"], d["w_q"])
    wload(A["wqr"], d["w_qr"])
    wload(A["wqr_sw"], d["w_qr_sw"])
    wA_tok = p.now(dW)
    wload(A["wkdec"], d["w_kdec"])
    wload(A["wvdec"], d["w_vdec"])
    wload(A["wqdec"], d["w_qdec"])
    wA2_tok = p.now(dW)

    with ExitStack() as pctx:
        psT = [pctx.enter_context(nc.psum_tensor(f"psT{i}", [128, 1024], BF16))
               for i in range(2)]
        psK = [pctx.enter_context(nc.psum_tensor(f"psK{i}", [128, 512], F32))
               for i in range(2)]
        psR = [pctx.enter_context(nc.psum_tensor(f"psR{i}", [128, 512], F32))
               for i in range(4)]

        xfree = [None, None, None]
        xTfree = [None, None]
        tabfree = [None, None]
        krofree = [None, None]
        ropet1_free = [None, None]
        psT_rd = [None, None]
        psK_rd = [None, None]
        psR_rd = [None, None, None, None]
        xln_free = None
        xsq_free = None
        setfree = [None] * 4
        pend_rms = [None]

        def sweep(nchunks, x_src, cos_src, sin_src, qside):
            nonlocal xln_free, xsq_free
            for c in range(nchunks):
                if pend_rms[0] is not None and pend_rms[0][0] == 1:
                    _, cp, kvt = pend_rms[0]
                    rkp = (cp % 2) * 256
                    sq_toks = [
                        p.op("scalar", "activation",
                             A["sqv"][:, m * 256:m * 256 + 256],
                             psK[m][:, 0:256], AF.Square,
                             waits=[kvt[m]], inc=ss)
                        for m in range(2)]
                    for m in range(2):
                        t_om = p.op("tensor", "matmul", psK[0][0:1, 256:512],
                                    A["c_ones_col"][:, :],
                                    A["sqv"][:, m * 256:m * 256 + 256],
                                    start=(m == 0), stop=(m == 1),
                                    waits=[sq_toks[m], consts_tok], inc=st)
                    pend_rms[0] = (2, cp, rkp, t_om)
                par = c % 2
                xpar = c % 3
                xb = A[f"xbuf{xpar}"]
                xs = x_src[c * 256:(c + 1) * 256].rearrange(
                    "(t p) d -> p t d", p=128)
                tx = p.dma("sync",
                           xb[:, :].rearrange("p (t d) -> p t d", t=2),
                           xs, waits=[xfree[xpar]], inc=dX[xpar])
                tc = p.dma("scalar",
                           A[f"tabc{par}"][:, :].rearrange(
                               "p (k t) -> p k t", k=4),
                           cos_src[:, c * 256:(c + 1) * 256].rearrange(
                               "(k p) t -> p k t", p=128),
                           waits=[tabfree[par]], inc=dT[par])
                tsn = p.dma("scalar",
                            A[f"tabs{par}"][:, :].rearrange(
                                "p (k t) -> p k t", k=4),
                            sin_src[:, c * 256:(c + 1) * 256].rearrange(
                                "(k p) t -> p k t", p=128),
                            waits=[tabfree[par]], inc=dT[par])
                ln_toks = []
                for t in range(2):
                    lb = ((c * 2 + t) % 4) * 8
                    xt = xb[:, t * 1024:(t + 1) * 1024]
                    t_r0 = p.op("vector", "reduce_sum", A["lnst"][:, lb + 0:lb + 1],
                                xt, AX.X, waits=[tx], inc=sv)
                    tq = p.op("scalar", "activation", A["xsq"][:, :], xt,
                              AF.Square, waits=[tx, xsq_free], inc=ss)
                    xsq_free = p.op("vector", "reduce_sum", A["lnst"][:, lb + 1:lb + 2],
                                    A["xsq"][:, :], AX.X, waits=[tq], inc=sv)
                    t_ms = p.op("gpsimd", "tensor_scalar", A["lnst"][:, lb + 2:lb + 4],
                                A["lnst"][:, lb + 0:lb + 2], 1.0 / D, None, ALU.mult,
                                waits=[xsq_free, setfree[(c * 2 + t) % 4]],
                                inc=sp)
                    p.op("gpsimd", "tensor_tensor", A["lnst"][:, lb + 4:lb + 5],
                         A["lnst"][:, lb + 2:lb + 3], A["lnst"][:, lb + 2:lb + 3], ALU.mult)
                    t_ve = p.op("gpsimd", "tensor_scalar", A["lnst"][:, lb + 5:lb + 6],
                                A["lnst"][:, lb + 3:lb + 4], A["lnst"][:, lb + 4:lb + 5], EPS_LN,
                                ALU.subtract, ALU.add, inc=sp)
                    t_rs = p.op("scalar", "activation", A["lnst"][:, lb + 6:lb + 7],
                                A["lnst"][:, lb + 5:lb + 6], AF.Sqrt,
                                waits=[t_ve], inc=ss)
                    t_rc = p.op("vector", "reciprocal", A["lnst"][:, lb + 7:lb + 8],
                                A["lnst"][:, lb + 6:lb + 7], waits=[t_rs], inc=sv)
                    t_cp = p.op("gpsimd", "tensor_copy", A["lnst"][:, lb + 5:lb + 6],
                                A["lnst"][:, lb + 7:lb + 8], waits=[t_rc], inc=sp)
                    tok = p.op("vector", "tensor_scalar",
                               A["xln"][:, t * 1024:(t + 1) * 1024], xt,
                               A["lnst"][:, lb + 2:lb + 3], A["lnst"][:, lb + 5:lb + 6],
                               ALU.subtract, ALU.mult,
                               waits=[xln_free, t_cp, t_ms], inc=sv)
                    setfree[(c * 2 + t) % 4] = tok
                    ln_toks.append(tok)
                xfree[xpar] = ln_toks[-1]
                if pend_rms[0] is not None and pend_rms[0][0] == 2:
                    _, cp, rkp, t_om = pend_rms[0]
                    t_pre = p.op("vector", "tensor_scalar",
                                 A["rkv"][0:1, rkp:rkp + 256],
                                 psK[0][0:1, 256:512], 1.0 / KVR, EPS_RMS,
                                 ALU.mult, ALU.add, waits=[t_om], inc=sv)
                    t_sq = p.op("scalar", "activation",
                                A["rkv"][0:1, rkp:rkp + 256],
                                A["rkv"][0:1, rkp:rkp + 256], AF.Sqrt,
                                waits=[t_pre], inc=ss)
                    t_rc = p.op("vector", "reciprocal",
                                A["rkv"][0:1, rkp:rkp + 256],
                                A["rkv"][0:1, rkp:rkp + 256], waits=[t_sq],
                                inc=sv)
                    pend_rms[0] = (3, cp, rkp, t_rc)
                xT = A[f"xlnT{par}"]
                last_tp = None
                for t in range(2):
                    for dkk in range(8):
                        tt = p.op("tensor", "transpose",
                                  psT[t][:, dkk * 128:dkk * 128 + 128],
                                  A["xln"][:, t * 1024 + dkk * 128:
                                           t * 1024 + dkk * 128 + 128],
                                  A["c_eye_bf"][:, :],
                                  waits=[ln_toks[t], consts_tok, psT_rd[t],
                                         xTfree[par]],
                                  inc=(st if dkk == 7 else None))
                    ev = p.op("vector", "tensor_copy",
                              bass.AP(A[f"xlnT{par}"], t * 128,
                                      [[2048, 128], [256, 8], [1, 128]]),
                              psT[t][:, :].rearrange(
                                  "p (k q) -> p k q", k=8),
                              waits=[tt], inc=sv)
                    psT_rd[t] = ev
                    last_tp = tt
                xln_free = last_tp
                xT_toks = [psT_rd[0], psT_rd[1]]
                if pend_rms[0] is not None and pend_rms[0][0] == 3:
                    _, cp, rkp, t_rc = pend_rms[0]
                    t_bc = p.op("tensor", "matmul", psK[1][:, 256:512],
                                A["c_ones_row"][:, :],
                                A["rkv"][0:1, rkp:rkp + 256],
                                start=True, stop=True, waits=[t_rc], inc=st)
                    t_ev = p.op("scalar", "copy", A["rkvB"][:, rkp:rkp + 256],
                                psK[1][:, 256:512], waits=[t_bc], inc=ss)
                    for m in range(2):
                        tok = p.op("vector", "scalar_tensor_tensor",
                                   A["kv_normT"][:, m * T + cp * 256:
                                                 m * T + cp * 256 + 256],
                                   psK[m][:, 0:256], A["c_g"][:, m:m + 1],
                                   A["rkvB"][:, rkp:rkp + 256],
                                   ALU.mult, ALU.mult, waits=[t_ev], inc=sv)
                        psK_rd[m] = tok
                    pend_rms[0] = None
                kv_toks = []
                for m in range(2):
                    for dk in range(8):
                        tok = p.op("tensor", "matmul", psK[m][:, 0:256],
                                   A["wkv" if not qside else "wq"][
                                       :, dk * (256 if not qside else 512)
                                       + m * 128:
                                       dk * (256 if not qside else 512)
                                       + m * 128 + 128],
                                   xT[:, dk * 256:dk * 256 + 256],
                                   start=(dk == 0), stop=(dk == 7),
                                   waits=xT_toks + [wA_tok, psK_rd[m]],
                                   inc=(st if dk == 7 else None))
                    kv_toks.append(tok)
                if not qside:
                    pend_rms[0] = (1, c, kv_toks)
                    qc_tok = None
                else:
                    for m in range(2):
                        tok = p.op("vector", "tensor_copy",
                                   A["q_cT"][:, m * TPC + c * 256:
                                             m * TPC + c * 256 + 256],
                                   psK[m][:, 0:256],
                                   waits=[kv_toks[m]], inc=sv)
                        psK_rd[m] = tok
                    for m in range(2, 4):
                        mm = m - 2
                        for dk in range(8):
                            tok = p.op("tensor", "matmul", psK[mm][:, 0:256],
                                       A["wq"][:, dk * 512 + m * 128:
                                               dk * 512 + m * 128 + 128],
                                       xT[:, dk * 256:dk * 256 + 256],
                                       start=(dk == 0), stop=(dk == 7),
                                       waits=[psK_rd[mm]],
                                       inc=(st if dk == 7 else None))
                        tok = p.op("vector", "tensor_copy",
                                   A["q_cT"][:, m * TPC + c * 256:
                                             m * TPC + c * 256 + 256],
                                   psK[mm][:, 0:256], waits=[tok], inc=sv)
                        psK_rd[mm] = tok
                    qc_tok = tok
                par4, slot = (c // 4) % 2, c % 4
                kro = A[f"kro{par4}"]
                rope_evs = []
                for m in range(4):
                    pr = m % 2
                    ro = pr * 512
                    bs_, bw_ = psR[pr * 2], psR[pr * 2 + 1]
                    if not qside:
                        wst, wsw, nk = A["wkr"], A["wkr_sw"], 8

                        def rr(dk):
                            return xT[:, dk * 256:dk * 256 + 256]
                        rw = xT_toks
                    else:
                        wst, wsw, nk = A["wqr"], A["wqr_sw"], 4

                        def rr(dk):
                            return A["q_cT"][:, dk * TPC + c * 256:
                                             dk * TPC + c * 256 + 256]
                        rw = [qc_tok]
                    for dk in range(nk):
                        t_s = p.op("tensor", "matmul", bs_[:, 0:256],
                                   wst[:, dk * 512 + m * 128:
                                       dk * 512 + m * 128 + 128], rr(dk),
                                   start=(dk == 0), stop=(dk == nk - 1),
                                   waits=rw + [psR_rd[pr * 2]],
                                   inc=(st if dk == nk - 1 else None))
                    for dk in range(nk):
                        t_w = p.op("tensor", "matmul", bw_[:, 0:256],
                                   wsw[:, dk * 512 + m * 128:
                                       dk * 512 + m * 128 + 128], rr(dk),
                                   start=(dk == 0), stop=(dk == nk - 1),
                                   waits=[psR_rd[pr * 2 + 1]],
                                   inc=(st if dk == nk - 1 else None))
                    ct = A[f"tabc{par}"][:, m * 256:m * 256 + 256]
                    st_ = A[f"tabs{par}"][:, m * 256:m * 256 + 256]
                    t1 = p.op("vector", "tensor_tensor",
                              A["ropet1"][:, ro:ro + 256], bs_[:, 0:256], ct,
                              ALU.mult,
                              waits=[t_s, tc, ropet1_free[pr]], inc=sv)
                    psR_rd[pr * 2] = t1
                    t2 = p.op("vector", "tensor_tensor",
                              A["ropet1"][:, ro + 256:ro + 512],
                              bw_[:, 0:256], st_, ALU.mult,
                              waits=[t_w, tsn, ropet1_free[pr]], inc=sv)
                    psR_rd[pr * 2 + 1] = t2
                    dst = (kro[:, m * 1024 + slot * 256:
                               m * 1024 + slot * 256 + 256] if not qside else
                           A["q_ropeT"][:, m * TPC + c * 256:
                                        m * TPC + c * 256 + 256])
                    t3 = p.op("gpsimd", "tensor_add", dst,
                              A["ropet1"][:, ro:ro + 256],
                              A["ropet1"][:, ro + 256:ro + 512],
                              waits=[t1, t2,
                                     krofree[par4] if (not qside and slot == 0
                                                       and m == 0) else None],
                              inc=sp)
                    ropet1_free[pr] = t3
                    rope_evs.append(t3)
                tabfree[par] = rope_evs[-1]
                xTfree[par] = rope_evs[-1]
                if not qside and slot == 3:
                    kd = kro_d[:, (c - 3) * 256:(c + 1) * 256].rearrange(
                        "(m p) t -> p m t", p=128)
                    krofree[par4] = p.dma(
                        "scalar" if (c // 4) % 2 == 0 else "gpsimd", kd,
                        kro[:, :].rearrange("p (m t) -> p m t", m=4),
                        waits=[rope_evs[-1]], inc=dKr[par4])

        sweep(NKC, d["x_b"], d["kcos"], d["kssin"], qside=False)
        sweep(NQC, d["x_q"], d["qcos"], d["qssin"], qside=True)

    if dev == 1:
        dd = nc.dram_tensor("dbg_kvnorm", [128, 2 * T], BF16,
                            kind="ExternalOutput")
        p.dma("sync", dd[:, :], A["kv_normT"][:, :],
              waits=[p.now(sv), p.now(st), p.now(ss), p.now(dKr[0]),
                     p.now(dKr[1])], inc=dC)
        dd2 = nc.dram_tensor("dbg_qc", [128, 4 * TPC], BF16,
                             kind="ExternalOutput")
        p.dma("sync", dd2[:, :], A["q_cT"][:, :], waits=[p.now(sv)], inc=dC)
        dd3 = nc.dram_tensor("dbg_qrope", [128, 4 * TPC], BF16,
                             kind="ExternalOutput")
        p.dma("sync", dd3[:, :], A["q_ropeT"][:, :], waits=[p.now(sv)],
              inc=dC)
        dd4 = nc.dram_tensor("dbg_lnst", [128, 32], F32, kind="ExternalOutput")
        p.dma("sync", dd4[:, :], A["lnst"][:, :],
              waits=[p.now(sv), p.now(sp), p.now(ss)], inc=dC)
        dd5 = nc.dram_tensor("dbg_xsq", [128, 1024], F32,
                             kind="ExternalOutput")
        p.dma("sync", dd5[:, :], A["xsq"][:, :], waits=[p.now(ss)], inc=dC)
        dd6 = nc.dram_tensor("dbg_xbuf", [128, 2048], F32,
                             kind="ExternalOutput")
        p.dma("sync", dd6[:, :], A["xbuf1"][:, :], waits=[p.now(sv)], inc=dC)
        return finish()

    p.barrier()

    for i in range(2):
        p.op("vector", "memset", A[f"KTb{i}"][64:128, :], 0.0, inc=sv)
        p.op("vector", "memset", A[f"QTb{i}"][64:128, :], 0.0, inc=sv)
        for base in (64, 132):
            p.op("vector", "memset",
                 bass.AP(A[f"Vb{i}"], base, [[64 * 136 + 32, 128], [136, 64],
                                             [1, 4]]),
                 1.0, inc=sv)
    init_tok = p.now(sv)
    ones_toks = [p.dma("sync", A[f"KTb{i}"][96:97, :], d["ones_8k"][:, :],
                       waits=[init_tok], inc=dC) for i in range(2)]
    ones_tok = ones_toks[-1]

    with ExitStack() as pctx:
        psKC = [pctx.enter_context(nc.psum_tensor(f"psKC{i}", [128, 512], F32))
                for i in range(2)]
        psVD = [pctx.enter_context(nc.psum_tensor(f"psVD{i}", [128, 512], F32))
                for i in range(2)]
        psQC = [pctx.enter_context(nc.psum_tensor(f"psQC{i}", [128, 512], F32))
                for i in range(2)]
        kc_rd = [None, None]; vd_rd = [None, None]; qc_rd = [None, None]
        buf_free = [None, None]
        vbuf_free = [None, None]

        for h in range(NH):
            par = h % 2
            KTb, QTb = A[f"KTb{par}"], A[f"QTb{par}"]
            ev = None
            for n in range(16):
                pp = n % 2
                for kk in range(2):
                    t_mm = p.op("tensor", "matmul", psKC[pp][0:64, 0:512],
                                A["wkdec"][:, kk * D + h * 64:
                                           kk * D + h * 64 + 64],
                                A["kv_normT"][:, kk * T + n * 512:
                                              kk * T + n * 512 + 512],
                                start=(kk == 0), stop=(kk == 1),
                                waits=[wA2_tok, kc_rd[pp], buf_free[par],
                                       ones_tok],
                                inc=(st if kk else None))
                ev = p.op("vector", "tensor_copy",
                          KTb[0:64, n * 512:n * 512 + 512],
                          psKC[pp][0:64, 0:512], waits=[t_mm], inc=sv)
                kc_rd[pp] = ev
            t_rope = p.dma("sync", KTb[64:96, :],
                           kro_d[h * 32:h * 32 + 32, :],
                           waits=[buf_free[par]], inc=dRp[par])
            t_store = p.dma("scalar", kall_d[h], KTb[:, :],
                            waits=[ev, t_rope], inc=dSt[par])
            if h % 2 == 0:
                vpar = (h // 2) % 2
                pr = h // 2
                Vb = A[f"Vb{vpar}"]
                vev = None
                for ng in range(32):
                    pp = ng % 2
                    for sub in range(2):
                        n = ng * 2 + sub
                        for kk in range(2):
                            t_mm = p.op("tensor", "matmul",
                                        psVD[pp][:, sub * 128:sub * 128 + 128],
                                        A["kv_normT"][:, kk * T + n * 128:
                                                      kk * T + n * 128 + 128],
                                        A["wvdec"][:, kk * D + pr * 128:
                                                   kk * D + pr * 128 + 128],
                                        start=(kk == 0), stop=(kk == 1),
                                        waits=[vd_rd[pp], vbuf_free[vpar],
                                               init_tok],
                                        inc=(st if (kk and sub) else None))
                    vev = p.op("vector", "tensor_copy",
                               bass.AP(Vb, ng * 272,
                                       [[64 * 136 + 32, 128], [136, 2],
                                        [68, 2], [1, 64]]),
                               bass.AP(psVD[pp], 0,
                                       [[512, 128], [128, 2], [64, 2],
                                        [1, 64]]),
                               waits=[t_mm], inc=sv)
                    vd_rd[pp] = vev
                vbuf_free[vpar] = p.dma("scalar", vall_d[pr],
                                        Vb[:, 0:64 * 136],
                                        waits=[vev], inc=dSt[par])
            qev = None
            for n in range(4):
                pp = n % 2
                for kq in range(4):
                    t_mm = p.op("tensor", "matmul", psQC[pp][0:64, 0:512],
                                A["wqdec"][:, kq * D + h * 64:
                                           kq * D + h * 64 + 64],
                                A["q_cT"][:, kq * TPC + n * 512:
                                          kq * TPC + n * 512 + 512],
                                start=(kq == 0), stop=(kq == 3),
                                waits=[qc_rd[pp], buf_free[par]],
                                inc=(st if kq == 3 else None))
                qev = p.op("vector", "tensor_scalar",
                           QTb[0:64, n * 512:n * 512 + 512],
                           psQC[pp][0:64, 0:512], SCALE, None, ALU.mult,
                           waits=[t_mm], inc=sv)
                qc_rd[pp] = qev
            t_qrope = p.dma("gpsimd", QTb[64:96, :],
                            A["q_ropeT"][32 * (h % 4):32 * (h % 4) + 32,
                                         (h // 4) * TPC:(h // 4 + 1) * TPC],
                            waits=[buf_free[par], init_tok], inc=dQr[par])
            buf_free[par] = p.dma("scalar", qall_d[h], QTb[:, :],
                                  waits=[qev, t_qrope], inc=dSt[par])

    if dev == 2:
        return finish()

    p.barrier()


    with ExitStack() as pctx:
        psW = [pctx.enter_context(nc.psum_tensor(f"psW{i}", [128, 1024], F32))
               for i in range(2)]
        psP = [pctx.enter_context(nc.psum_tensor(f"psP{i}", [128, 512], F32))
               for i in range(2)]
        psV = pctx.enter_context(nc.psum_tensor("psV", [128, 512], F32))
        psTb = pctx.enter_context(nc.psum_tensor("psTb", [128, 1024], BF16))

        kt_load = [None, None]; v_load = [None, None]; qt_load = [None, None]
        kt_free = [None, None]; v_free = [None, None]; qt_free = [None, None]
        psW_rd = [None, None]; psP_rd = [None, None]
        psV_rd = [None, None]
        probs_free = [None, None]
        mask_free = [None, None]
        masktmp_free = None
        psTb_rd = None
        ones_init = None
        rlB_free = None
        mrow_busy = [None]

        vpad_init = [p.op("vector", "memset", A[f"vB{i}"][:, 8704:8736],
                          0.0, inc=sv) for i in range(2)]

        def load_head(h):
            par = h % 2
            kt_load[par] = p.dma("sync", A[f"ktB{par}"][:, :], kall_d[h],
                                 waits=[kt_free[par]], inc=dK[par])
            qt_load[par] = p.dma("sync", A[f"qtB{par}"][:, :], qall_d[h],
                                 waits=[qt_free[par]], inc=dQ[par])
            if h % 2 == 0:
                vp = (h // 2) % 2
                v_load[vp] = p.dma("sync", A[f"vB{vp}"][:, 0:8704],
                                   vall_d[h // 2],
                                   waits=[v_free[vp], vpad_init[vp]],
                                   inc=dV[vp])

        def pass1(h):
            nonlocal psTb_rd
            par = h % 2
            ktB, qtB = A[f"ktB{par}"], A[f"qtB{par}"]
            red_toks = []
            pp = 0
            for j in range(NJ):
                E = _E(j)
                n = E * 4
                ps = psP[pp]
                t_mm = p.op("tensor", "matmul", ps[:, 0:n],
                            qtB[:, j * 128:j * 128 + 128],
                            bass.AP(ktB, 0, [[T, 128], [32, n]]),
                            start=True, stop=True,
                            waits=[kt_load[par], qt_load[par], psP_rd[pp],
                                   psV_rd[1] if pp == 0 else None], inc=st)
                t_r = p.op("vector", "tensor_reduce",
                           A["mcols"][:, j:j + 1], ps[:, 0:n],
                           AX.X, ALU.max, waits=[t_mm], inc=sv)
                psP_rd[pp] = t_r
                pp ^= 1
                red_toks.append(t_r)
            t_n = p.op("gpsimd", "tensor_scalar", A["mneg"][:, :],
                       A["mcols"][:, :], -1.0, -MARGIN, ALU.mult, ALU.add,
                       waits=red_toks, inc=sp)
            t_t = p.op("tensor", "transpose", psTb[0:16, 0:128],
                       A["mneg"][:, :], A["c_eye_bf"][:, :],
                       waits=[t_n, psTb_rd], inc=st)
            t_e = p.op("vector", "tensor_copy", A["mrow"][:, :],
                       psTb[0:16, 0:128], waits=[t_t], inc=sv)
            psTb_rd = t_e
            t_d1 = p.dma("gpsimd",
                         mrow_d[0:1, :].rearrange("p (j q) -> p j q", j=16),
                         A["mrow"][:, :], waits=[t_e, mrow_busy[0]],
                         inc=dMr)
            t_d2 = p.dma("gpsimd", qtB[96:97, :], mrow_d[0:1, :],
                         waits=[t_d1], inc=dMr)
            mrow_busy[0] = t_d2
            return t_d2

        def pass2(h, row_tok):
            nonlocal psV_rd, masktmp_free, ones_init
            par = h % 2
            vp = (h // 2) % 2
            voff = (h % 2) * 68
            ktB, qtB = A[f"ktB{par}"], A[f"qtB{par}"]
            vB = A[f"vB{vp}"]
            if ones_init is None:
                ones_init = p.op("vector", "memset", A["ones64"][:, :], 1.0,
                                 inc=sv)
            bpar = 0
            pend = []
            rl_busy = [None]
            pend_fin = [None]

            def do_finalize(pend_fin):
                if pend_fin[0] is None:
                    return
                fpj, flast = pend_fin[0]
                pend_fin[0] = None
                psv = psV if fpj % 2 == 0 else psP[0]
                t_rl = p.op("vector", "reciprocal", A["rl64"][64:65, 0:256],
                            psv[64:65, 0:256],
                            waits=[flast, ones_init, rl_busy[0]], inc=sv)
                t_bc = p.op("tensor", "matmul", psP[1][0:64, 0:256],
                            A["ones64"][64:65, 0:64],
                            A["rl64"][64:65, 0:256],
                            start=True, stop=True,
                            waits=[t_rl, psP_rd[1]], inc=st)
                rl_busy[0] = t_bc
                t_rb2 = p.op("vector", "tensor_copy", A["rlB"][0:64, 0:256],
                             psP[1][0:64, 0:256], waits=[t_bc], inc=sv)
                psP_rd[1] = t_rb2
                t_o = p.op("vector", "tensor_tensor",
                           A["attn_o64"][0:64, h * TPC + fpj * 256:
                                         h * TPC + fpj * 256 + 256],
                           psv[0:64, 0:256], A["rlB"][0:64, 0:256],
                           ALU.mult, waits=[t_rb2], inc=sv)
                psV_rd[fpj % 2] = t_o
                if fpj % 2 == 1:
                    psP_rd[0] = t_o

            def flush_pv(pend):
                if not pend:
                    return None
                bk_, prb_, t_e_, E0_, vp_, voff_, pvb_, bp_, w_ = pend.pop(0)
                psv = psV if pvb_ == 0 else psP[0]
                lp = None
                for i, kt in enumerate(bk_):
                    lp = p.op(
                        "tensor", "matmul", psv[0:96, 0:w_],
                        A[f"vB{vp_}"][:, kt * 136 + voff_:
                                      kt * 136 + voff_ + 96],
                        prb_[:, i * w_:i * w_ + w_],
                        start=(kt == 0), stop=(kt == E0_ - 1),
                        waits=[t_e_, psV_rd[pvb_] if kt == 0 else None,
                               psP_rd[0] if (pvb_ == 1 and kt == 0) else None,
                               v_load[vp_]],
                        inc=(st if kt == E0_ - 1 else None))
                probs_free[bp_] = lp
                return lp

            for pj in range(NJ // 2):
                E0 = _E(2 * pj)
                mpar = pj % 2
                t_ml = p.dma("sync",
                             A[f"maskb{mpar}"][:, :].rearrange(
                                 "p (k q) -> p k q", k=8),
                             d["maskT"][pj].rearrange("k p q -> p k q"),
                             waits=[mask_free[mpar]], inc=dM[mpar])
                nb = E0 // 4
                last_pv = None
                for bi in range(nb):
                    bk = list(range(bi * 4, bi * 4 + 4))
                    lastb = (bi == nb - 1)
                    w = 128 if lastb else 256
                    ps = psW[bpar]
                    for i, kt in enumerate(bk):
                        tok = p.op("tensor", "matmul",
                                   ps[:, i * w:i * w + w],
                                   ktB[:, kt * 128:kt * 128 + 128],
                                   qtB[:, pj * 256:pj * 256 + w],
                                   start=True, stop=True,
                                   waits=[psW_rd[bpar], row_tok,
                                          kt_load[par]],
                                   inc=(st if i == 3 else None))
                    mm_tok = tok
                    if len(pend) >= 2:
                        fl = flush_pv(pend)
                        if fl is not None:
                            last_pv = fl
                    if bi == 1:
                        do_finalize(pend_fin)
                    prb = A[f"probsT{bpar}"]
                    if bi < nb - 2:
                        t_e = p.op("scalar", "activation", prb[:, :],
                                   ps[:, :], AF.Exp,
                                   waits=[mm_tok, probs_free[bpar]], inc=ss)
                    elif not lastb:
                        t_a = p.op("vector", "tensor_add",
                                   A["masktmp"][:, :], ps[:, :],
                                   A[f"maskb{mpar}"][:, 0:1024],
                                   waits=[mm_tok, t_ml, masktmp_free],
                                   inc=sv)
                        t_e = p.op("scalar", "activation", prb[:, :],
                                   A["masktmp"][:, :], AF.Exp,
                                   waits=[t_a, probs_free[bpar]], inc=ss)
                        masktmp_free = t_e
                    else:
                        t_a = p.op("vector", "tensor_add",
                                   bass.AP(A["masktmp"], 0,
                                           [[1024, 128], [128, 4], [1, 128]]),
                                   bass.AP(ps, 0,
                                           [[1024, 128], [128, 4], [1, 128]]),
                                   bass.AP(A[f"maskb{mpar}"], 1024,
                                           [[2048, 128], [256, 4], [1, 128]]),
                                   waits=[mm_tok, t_ml, masktmp_free],
                                   inc=sv)
                        t_e = p.op("scalar", "activation", prb[:, 0:512],
                                   A["masktmp"][:, 0:512], AF.Exp,
                                   waits=[t_a, probs_free[bpar]], inc=ss)
                        masktmp_free = t_e
                        mask_free[mpar] = t_e
                    psW_rd[bpar] = t_e
                    pend.append((bk, prb, t_e, E0, vp, voff, pj % 2, bpar, w))
                    bpar ^= 1
                while pend:
                    fl = flush_pv(pend)
                    if fl is not None:
                        last_pv = fl
                pend_fin[0] = (pj, last_pv)
                if pj == NJ // 2 - 1:
                    do_finalize(pend_fin)
            kt_free[par] = p.now(st)
            qt_free[par] = p.now(st)
            if h % 2 == 1:
                v_free[vp] = p.now(st)

        horder = list(range(NH))
        if dev == 31:
            horder = [2, 3, 0, 1] + list(range(4, NH))
        load_head(horder[0])
        load_head(horder[1])
        for hi in range(NH):
            h = horder[hi]
            rt = pass1(h)
            pass2(h, rt)
            if hi + 2 < NH:
                load_head(horder[hi + 2])

    if dev in (3, 31):
        dd = nc.dram_tensor("dbg_ao", [64, NH * TPC], BF16,
                            kind="ExternalOutput")
        p.dma("sync", dd[:, :], A["attn_o64"][0:64, :],
              waits=[p.now(sv), p.now(st), p.now(ss)], inc=dC)
        return finish()

    p.barrier()

    p.dma("gpsimd", A["wo"][:, :].rearrange("p (h n) -> p h n", h=16),
          d["w_o"].rearrange("(h p) n -> p h n", p=64), inc=dW)
    wo_tok = p.now(dW)
    with ExitStack() as pctx:
        psY = [pctx.enter_context(nc.psum_tensor(f"psY{i}", [128, 1024], F32))
               for i in range(2)]
        psT2 = [pctx.enter_context(nc.psum_tensor(f"psT2{i}", [128, 1024],
                                                  BF16)) for i in range(2)]
        psRt = pctx.enter_context(nc.psum_tensor("psRt", [128, 512], F32))
        psCw = pctx.enter_context(nc.psum_tensor("psCw", [128, 512], F32))

        xq_free = [None, None]; psY_rd = [None, None]
        psT2_rd = [None, None]; ys_free = [None, None]
        setfree2 = [None] * 4
        for t in range(NJ):
            par = t % 2
            t_x = p.dma("sync", A[f"xq{par}"][:, :],
                        d["x_q"][t * 128:(t + 1) * 128, :],
                        waits=[xq_free[par]], inc=dXQ[par])
            for half in range(2):
                for hh in range(NH):
                    t_mm = p.op("tensor", "matmul",
                                psY[par][:, half * 512:half * 512 + 512],
                                A["attn_o64"][0:64, hh * TPC + t * 128:
                                              hh * TPC + t * 128 + 128],
                                A["wo"][0:64, hh * 1024 + half * 512:
                                        hh * 1024 + half * 512 + 512],
                                start=(hh == 0), stop=(hh == NH - 1),
                                waits=[wo_tok, psY_rd[par]],
                                inc=(st if hh == NH - 1 else None))
            yb = A[f"yb{par}"]
            t_y = p.op("vector", "tensor_add", yb[:, :], psY[par][:, :],
                       A[f"xq{par}"][:, :],
                       waits=[t_mm, t_x, ys_free[par]], inc=sv)
            psY_rd[par] = t_y
            lb2 = (t % 4) * 8
            t_r0 = p.op("scalar", "activation", A["ylnb"][:, :], yb[:, :],
                        AF.Copy, accum_out=A["lnst2"][:, lb2 + 0:lb2 + 1],
                        waits=[t_y, setfree2[t % 4]], inc=ss)
            t_sq = p.op("scalar", "activation", A[f"xq{par}"][:, :], yb[:, :],
                        AF.Square, accum_out=A["lnst2"][:, lb2 + 1:lb2 + 2],
                        waits=[t_y, t_r0], inc=ss)
            xq_free[par] = t_sq
            t_r1 = t_sq
            t_ms = p.op("gpsimd", "tensor_scalar",
                        A["lnst2"][:, lb2 + 2:lb2 + 4],
                        A["lnst2"][:, lb2 + 0:lb2 + 2], 1.0 / D, None,
                        ALU.mult, waits=[t_r0, t_r1, setfree2[t % 4]],
                        inc=sp)
            p.op("gpsimd", "tensor_tensor", A["lnst2"][:, lb2 + 4:lb2 + 5],
                 A["lnst2"][:, lb2 + 2:lb2 + 3], A["lnst2"][:, lb2 + 2:lb2 + 3], ALU.mult)
            t_ve = p.op("gpsimd", "tensor_scalar", A["lnst2"][:, lb2 + 5:lb2 + 6],
                        A["lnst2"][:, lb2 + 3:lb2 + 4], A["lnst2"][:, lb2 + 4:lb2 + 5], EPS_LN,
                        ALU.subtract, ALU.add, inc=sp)
            t_s2 = p.op("scalar", "activation", A["lnst2"][:, lb2 + 6:lb2 + 7],
                        A["lnst2"][:, lb2 + 5:lb2 + 6], AF.Sqrt,
                        waits=[t_ve], inc=ss)
            t_rc2 = p.op("vector", "reciprocal", A["lnst2"][:, lb2 + 7:lb2 + 8],
                         A["lnst2"][:, lb2 + 6:lb2 + 7], waits=[t_s2], inc=sv)
            t_cp2 = p.op("gpsimd", "tensor_copy", A["lnst2"][:, lb2 + 5:lb2 + 6],
                         A["lnst2"][:, lb2 + 7:lb2 + 8], waits=[t_rc2], inc=sp)
            t_ln = p.op("vector", "tensor_scalar", A["ylnb2"][:, :],
                        A["ylnb"][:, :],
                        A["lnst2"][:, lb2 + 2:lb2 + 3], A["lnst2"][:, lb2 + 5:lb2 + 6],
                        ALU.subtract, ALU.mult,
                        waits=[t_cp2, t_ms, t_r0], inc=sv)
            setfree2[t % 4] = t_ln
            t_ys = p.dma("scalar", ys_d[t * 128:(t + 1) * 128, :], yb[:, :],
                         waits=[t_y, t_ln], inc=dYs[par])
            ys_free[par] = t_ys
            tp = t % 2
            for dk in range(8):
                t_t = p.op("tensor", "transpose",
                           psT2[tp][:, dk * 128:dk * 128 + 128],
                           A["ylnb2"][:, dk * 128:dk * 128 + 128],
                           A["c_eye_bf"][:, :],
                           waits=[t_ln, psT2_rd[tp]],
                           inc=(st if dk == 7 else None))
            t_e = p.op("vector", "tensor_copy",
                       bass.AP(A["y_lnT"], t * 128,
                               [[8 * TPC, 128], [TPC, 8], [1, 128]]),
                       psT2[tp][:, :].rearrange("p (k q) -> p k q", k=8),
                       waits=[t_t], inc=sv)
            psT2_rd[tp] = t_e
        ylnT_tok = p.now(sv)
        t_rb = p.dma("sync", A["lgt"][0:1, 0:8], d["rb"][:, :], inc=dC)
        t_rbb = p.op("tensor", "matmul", psRt[:, 128:136],
                     A["c_ones_row"][:, :], A["lgt"][0:1, 0:8],
                     start=True, stop=True, waits=[t_rb, consts_tok], inc=st)
        t_rbe = p.op("scalar", "copy", A["rbB"][:, :],
                     psRt[:, 128:136], waits=[t_rbb], inc=ss)
        t_wr = p.dma("gpsimd",
                     A["wrt"][:, :].rearrange("p (k e) -> p k e", k=8),
                     d["wr"].rearrange("(k p) e -> p k e", p=128), inc=dW)
        prev_l = None
        sig_toks = []
        for t in range(NJ):
            for dk in range(8):
                t_mm = p.op("tensor", "matmul", psRt[:, 0:8],
                            A["y_lnT"][:, dk * TPC + t * 128:
                                       dk * TPC + t * 128 + 128],
                            A["wrt"][:, dk * 8:dk * 8 + 8],
                            start=(dk == 0), stop=(dk == 7),
                            waits=[ylnT_tok, t_wr, prev_l, t_rbe],
                            inc=(st if dk == 7 else None))
            t_l = p.op("vector", "tensor_add", A["lgt"][:, 0:8],
                       psRt[:, 0:8], A["rbB"][:, :],
                       waits=[t_mm, t_rbe,
                              sig_toks[-1] if sig_toks else None], inc=sv)
            prev_l = t_l
            t_sg = p.op("scalar", "activation",
                        A["cwbuf"][:, t * 8:t * 8 + 8], A["lgt"][:, 0:8],
                        AF.Sigmoid, waits=[t_l], inc=ss)
            sig_toks.append(t_sg)

        def sl_e(e):
            return bass.AP(A["cwbuf"], e, [[128, 128], [8, 16]])

        sig_tok = p.now(ss)
        probs3 = bass.AP(A["cwbuf"], 0, [[128, 128], [8, 16], [1, 8]])
        t_m1 = p.op("vector", "tensor_reduce", A["r_m1"][:, :], probs3,
                    AX.X, ALU.max, waits=[sig_tok], inc=sv)
        t_m1c = p.op("scalar", "copy", A["r_m2"][:, :], A["r_m1"][:, :],
                     waits=[t_m1], inc=ss)
        t_ge = None
        for t in range(NJ):
            t_ge = p.op("vector", "tensor_scalar",
                        A["r_ge"][:, t * 1:t + 1] if False else
                        A["cw"][:, t * 8:t * 8 + 8],
                        A["cwbuf"][:, t * 8:t * 8 + 8],
                        A["r_m2"][:, t:t + 1], None, ALU.is_ge,
                        waits=[t_m1c], inc=sv)
        t_gec = p.op("scalar", "copy", A["cwT"][:, :], A["cw"][:, :],
                     waits=[t_ge], inc=ss)
        t_msk = p.op("vector", "scalar_tensor_tensor", A["cw"][:, :],
                     A["cwT"][:, :], -1e30, A["cwbuf"][:, :],
                     ALU.mult, ALU.add, waits=[t_gec], inc=sv)
        t_mskc = p.op("scalar", "copy", A["cwT"][:, :], A["cw"][:, :],
                      waits=[t_msk], inc=ss)
        masked3 = bass.AP(A["cwT"], 0, [[128, 128], [8, 16], [1, 8]])
        t_m2 = p.op("vector", "tensor_reduce", A["r_m2"][:, :], masked3,
                    AX.X, ALU.max, waits=[t_mskc], inc=sv)
        t_m2c = p.op("scalar", "copy", A["r_ge"][:, :], A["r_m2"][:, :],
                     waits=[t_m2], inc=ss)
        t_ge2 = None
        for t in range(NJ):
            t_ge2 = p.op("vector", "tensor_scalar",
                         A["cw"][:, t * 8:t * 8 + 8],
                         A["cwbuf"][:, t * 8:t * 8 + 8],
                         A["r_ge"][:, t:t + 1], None, ALU.is_ge,
                         waits=[t_m2c], inc=sv)
        t_gc2 = p.op("scalar", "copy", A["cwT"][:, :], A["cw"][:, :],
                     waits=[t_ge2], inc=ss)
        t_cw = p.op("vector", "tensor_mul", A["cw"][:, :], A["cwT"][:, :],
                    A["cwbuf"][:, :], waits=[t_gc2], inc=sv)
        cw_toks = [t_cw]
        t_ct = p.op("tensor", "transpose", psCw[:, 0:128], A["cw"][:, :],
                    A["c_eye_f"][:, :], waits=cw_toks + [consts_tok], inc=st)
        t_ce = p.op("vector", "tensor_copy", A["cwT"][:, :], psCw[:, 0:128],
                    waits=[t_ct], inc=sv)
        cwT_dma = p.dma("scalar", cwT_d[:, :], A["cwT"][:, :],
                        waits=[t_ce], inc=dYs[0])

    if dev == 4:
        dd = nc.dram_tensor("dbg_ylnT", [128, 8 * TPC], BF16,
                            kind="ExternalOutput")
        p.dma("sync", dd[:, :], A["y_lnT"][:, :],
              waits=[p.now(sv), p.now(ss), p.now(st), p.now(dYs[0]),
                     p.now(dYs[1])], inc=dC)
        dd2 = nc.dram_tensor("dbg_cw", [128, 128], F32, kind="ExternalOutput")
        p.dma("sync", dd2[:, :], A["cw"][:, :], waits=[p.now(sv)], inc=dC)
        return finish()

    p.barrier()

    with ExitStack() as pctx:
        psG = pctx.enter_context(nc.psum_tensor("psG", [128, 1024], F32))
        psU = pctx.enter_context(nc.psum_tensor("psU", [128, 1024], F32))
        psO = [pctx.enter_context(nc.psum_tensor(f"psO{i}", [128, 512], F32))
               for i in range(2)]

        ring_free = [None] * RING
        gtile = [0]

        def ring_load(src_ap, kt):
            slot = gtile[0] % RING
            gtile[0] += 1
            ncol = src_ap.shape[1] * src_ap.shape[2]
            ds = S[f"ring{slot}"][:, 0:ncol].rearrange(
                "p (k m) -> p k m", k=kt)
            tok = p.dma("gpsimd", ds, src_ap,
                        waits=[ring_free[slot]], inc=dR[slot])
            return slot, tok

        psG_rd = [None, None]; psU_rd = [None, None]
        psO_rd = [None, None]
        cw_tok = cw_toks[-1]
        yt_free = [None, None]
        tf_prev = {}
        act_guard = [None]

        for chunk in range(2):
            coff = chunk * 1024
            acc_toks = {}
            for e in range(NE + 1):
                routed = e > 0
                if routed:
                    wg_src, wu_src, wd_src = (d["wg"][e - 1], d["wu"][e - 1],
                                              d["wd"][e - 1])
                else:
                    wg_src, wu_src, wd_src = d["wg_s"], d["wu_s"], d["wd_s"]
                act_toks = []
                for q in range(4):
                    sg, tg = ring_load(
                        wg_src[:, q * 512:(q + 1) * 512].rearrange(
                            "(k p) m -> p k m", p=128), 8)
                    su, tu = ring_load(
                        wu_src[:, q * 512:(q + 1) * 512].rearrange(
                            "(k p) m -> p k m", p=128), 8)
                    t_g = t_u = None
                    for hi in range(4):
                        ht = q * 4 + hi
                        for sub in range(2):
                            i = (ht * 2 + sub) % 2
                            for dk in range(8):
                                t_g = p.op(
                                    "tensor", "matmul",
                                    psG[:, i * 512:i * 512 + 512],
                                    S[f"ring{sg}"][:, dk * 512 + hi * 128:
                                                   dk * 512 + hi * 128 + 128],
                                    A["y_lnT"][:, dk * TPC + coff + sub * 512:
                                               dk * TPC + coff + sub * 512
                                               + 512],
                                    start=(dk == 0), stop=(dk == 7),
                                    waits=[tg, psG_rd[i]],
                                    inc=(st if dk == 7 else None))
                            for dk in range(8):
                                t_u = p.op(
                                    "tensor", "matmul",
                                    psU[:, i * 512:i * 512 + 512],
                                    S[f"ring{su}"][:, dk * 512 + hi * 128:
                                                   dk * 512 + hi * 128 + 128],
                                    A["y_lnT"][:, dk * TPC + coff + sub * 512:
                                               dk * TPC + coff + sub * 512
                                               + 512],
                                    start=(dk == 0), stop=(dk == 7),
                                    waits=[tu, psU_rd[i]],
                                    inc=(st if dk == 7 else None))
                            t_si = p.op("scalar", "activation",
                                        A["sil"][:, i * 512:i * 512 + 512],
                                        psG[:, i * 512:i * 512 + 512],
                                        AF.Silu, waits=[t_g, psU_rd[i]],
                                        inc=ss)
                            psG_rd[i] = t_si
                            dst = A["actT"][:, ht * 1024 + sub * 512:
                                            ht * 1024 + sub * 512 + 512]
                            t_ac = p.op("vector", "scalar_tensor_tensor",
                                        dst,
                                        psU[:, i * 512:i * 512 + 512],
                                        1.0,
                                        A["sil"][:, i * 512:i * 512 + 512],
                                        ALU.mult, ALU.mult,
                                        waits=[t_u, t_si, act_guard[0]],
                                        inc=sv)
                            psU_rd[i] = t_ac
                            act_toks.append(t_ac)
                    ring_free[sg] = t_g
                    ring_free[su] = t_u
                wd_slots = []
                for q in range(4):
                    wd_slots.append(ring_load(
                        wd_src[q * 512:(q + 1) * 512, :].rearrange(
                            "(k p) m -> p k m", p=128), 4))
                act_tok = act_toks[-1]
                for tt in range(8):
                    for half in range(2):
                        i = (tt * 2 + half) % 2
                        t_o = None
                        for ht in range(16):
                            sd, td = wd_slots[ht // 4]
                            t_o = p.op(
                                "tensor", "matmul", psO[i][:, 0:512],
                                A["actT"][:, ht * 1024 + tt * 128:
                                          ht * 1024 + tt * 128 + 128],
                                S[f"ring{sd}"][:, (ht % 4) * 1024
                                               + half * 512:
                                               (ht % 4) * 1024 + half * 512
                                               + 512],
                                start=(ht == 0), stop=(ht == 15),
                                waits=[td, act_tok, psO_rd[i]],
                                inc=(st if ht == 15 else None))
                        sl = A["out_acc"][:, tt * 1024 + half * 512:
                                          tt * 1024 + half * 512 + 512]
                        if e == 0:
                            t_a = p.op("vector", "tensor_copy", sl,
                                       psO[i][:, 0:512],
                                       waits=[t_o, tf_prev.get(tt)], inc=sv)
                        else:
                            gcol = (chunk * 8 + tt) * 8 + (e - 1)
                            t_a = p.op("vector", "scalar_tensor_tensor", sl,
                                       psO[i][:, 0:512],
                                       A["cw"][:, gcol:gcol + 1], sl,
                                       ALU.mult, ALU.add,
                                       waits=[t_o, acc_toks[(tt, half)],
                                              cw_tok], inc=sv)
                        psO_rd[i] = t_a
                        acc_toks[(tt, half)] = t_a
                for sd, td in wd_slots:
                    ring_free[sd] = p.now(st)
            for tt in range(8):
                par = tt % 2
                gt = chunk * 8 + tt
                t_y = p.dma("sync", A[f"ytile{par}"][:, :],
                            ys_d[gt * 128:(gt + 1) * 128, :],
                            waits=[yt_free[par], p.now(dYs[0]),
                                   p.now(dYs[1])], inc=dYt[par])
                t_f = p.op("vector", "tensor_add", A[f"ytile{par}"][:, :],
                           A[f"ytile{par}"][:, :],
                           bass.AP(A["out_acc"], tt * 1024,
                                   [[8192, 128], [1, 1024]]),
                           waits=[t_y, acc_toks[(tt, 0)],
                                  acc_toks[(tt, 1)]], inc=sv)
                t_od = p.dma("sync", out_d[gt], A[f"ytile{par}"][:, :],
                             waits=[t_f], inc=dOut[par])
                yt_free[par] = t_od
                tf_prev[tt] = t_f
            act_guard[0] = p.now(st)

    return finish()


def kernel(**inputs):
    in_maps = _host_prep(inputs)
    nc = _build()
    res = run_bass_kernel_spmd(nc, in_maps, core_ids=list(range(8)))
    y = np.zeros((B, T, D), np.float32)
    for core in range(8):
        b, r = core // 4, core % 4
        o = np.asarray(res.results[core]["out"])
        for j in range(NJ):
            q = _Q(r, j)
            y[b, q * 128:(q + 1) * 128] = o[j]
    return y
